# version 04
# baseline (speedup 1.0000x reference)
import concourse.bass as bass
import concourse.mybir as mybir

ENGINES = ("tensor", "vector", "scalar", "gpsimd", "sync")
DMA_RING = 10


class Region:
    __slots__ = ("buf", "p0", "p1", "f0", "f1", "tname")

    def __init__(self, buf, p0, p1, f0, f1, tname=None):
        self.buf = buf
        self.p0, self.p1 = p0, p1
        self.f0, self.f1 = f0, f1
        self.tname = tname

    def overlaps(self, o):
        return (self.buf == o.buf and self.p0 < o.p1 and o.p0 < self.p1
                and self.f0 < o.f1 and o.f0 < self.f1)


class Buf:

    def __init__(self, name, handle, shape, addr=None, esize=4, space=None):
        self.name = name
        self.h = handle
        self.shape = shape
        self.addr = addr
        self.esize = esize
        self.space = space or name

    def __getitem__(self, key):
        return self.ap()[key]

    def ap(self):
        return self.h.ap() if hasattr(self.h, "ap") else self.h

    def r(self, p0=0, p1=None, f0=0, f1=None):
        if p1 is None:
            p1 = self.shape[0]
        if f1 is None:
            f1 = self.shape[-1] if len(self.shape) == 2 else 1 << 40
        if self.addr is not None:
            return Region(self.space, p0, p1,
                          self.addr + f0 * self.esize, self.addr + f1 * self.esize,
                          tname=self.name)
        return Region(self.space, p0, p1, f0, f1, tname=self.name)

    def all(self):
        return self.r()


class Op:
    __slots__ = ("idx", "engine", "emit", "reads", "writes", "is_dma",
                 "queue", "tag", "inc_needed", "dma_sem", "dma_val")

    def __init__(self, idx, engine, emit, reads, writes, is_dma, queue, tag):
        self.idx = idx
        self.engine = engine
        self.emit = emit
        self.reads = reads
        self.writes = writes
        self.is_dma = is_dma
        self.queue = queue
        self.tag = tag
        self.inc_needed = False
        self.dma_sem = None
        self.dma_val = None


class Prog:
    def __init__(self, nc: bass.Bass):
        self.nc = nc
        self.ops = []
        self.bufs = {}
        self._anon = 0

    def sbuf(self, name, shape, dtype, at=None):
        es = mybir.dt.size(dtype)
        if at is not None:
            h = self.nc.alloc_sbuf_tensor_at(name, shape, dtype, offset=at)
            addr = at
        else:
            h = self.nc.alloc_sbuf_tensor(name, shape, dtype)
            addr = None
        b = Buf(name, h, shape, addr=addr, esize=es,
                space=("SB" if at is not None else name))
        self.bufs[name] = b
        return b

    def psum(self, name, shape, dtype=mybir.dt.float32):
        h = self.nc.alloc_psum_tensor(name, shape, dtype)
        b = Buf(name, h, shape, esize=mybir.dt.size(dtype))
        self.bufs[name] = b
        return b

    def dram(self, name, shape, dtype, kind="Internal"):
        h = self.nc.dram_tensor(name, shape, dtype, kind=kind)
        b = Buf(name, h, list(shape))
        self.bufs[name] = b
        return b

    def op(self, engine, emit, reads=(), writes=(), tag=""):
        o = Op(len(self.ops), engine, emit, list(reads), list(writes),
               False, None, tag)
        self.ops.append(o)
        return o

    def dma(self, queue, emit, reads=(), writes=(), tag=""):
        o = Op(len(self.ops), queue, emit, list(reads), list(writes),
               True, queue, tag)
        self.ops.append(o)
        return o

    def _find_deps(self):
        BLK = 4096
        hist = {}

        def keys(rg):
            if rg.buf == "SB":
                b0 = max(0, int(rg.f0)) // BLK
                b1 = max(0, int(min(rg.f1, 1 << 40)) - 1) // BLK
                return [("SB", b) for b in range(b0, b1 + 1)]
            return [(rg.buf,)]

        deps = [set() for _ in self.ops]
        for o in self.ops:
            dd = deps[o.idx]
            for rg, want_reads in [(r_, False) for r_ in o.reads] +                                   [(w_, True) for w_ in o.writes]:
                for k in keys(rg):
                    lst = hist.get(k)
                    if not lst:
                        continue
                    for (pi, kind, prg) in reversed(lst):
                        if not rg.overlaps(prg):
                            continue
                        if kind == "w":
                            dd.add(pi)
                            if (prg.p0 <= rg.p0 and prg.p1 >= rg.p1 and
                                    prg.f0 <= rg.f0 and prg.f1 >= rg.f1):
                                break
                        elif want_reads:
                            dd.add(pi)
            for rg in o.reads:
                for k in keys(rg):
                    hist.setdefault(k, []).append((o.idx, "r", rg))
            for rg in o.writes:
                for k in keys(rg):
                    hist.setdefault(k, []).append((o.idx, "w", rg))
        for i, ds in enumerate(deps):
            by_eng = {}
            for d in ds:
                pp = self.ops[d]
                if pp.is_dma:
                    continue
                if pp.engine not in by_eng or by_eng[pp.engine] < d:
                    by_eng[pp.engine] = d
            pruned = set(d for d in ds if self.ops[d].is_dma)
            pruned.update(by_eng.values())
            deps[i] = pruned
        return deps

    def resolve_and_emit(self, final_waits=True):
        nc = self.nc
        deps = self._find_deps()

        for i, ds in enumerate(deps):
            for d in ds:
                if not self.ops[d].is_dma:
                    self.ops[d].inc_needed = True
        qcount = {}
        for o in self.ops:
            if o.is_dma:
                k = qcount.get(o.queue, 0)
                qcount[o.queue] = k + 1
                slot = k % DMA_RING
                use = k // DMA_RING
                o.dma_sem = (o.queue, slot)
                o.dma_val = 16 * (use + 1)

        inc_prefix = {}
        ecount = {e: 0 for e in ENGINES}
        for o in self.ops:
            if not o.is_dma and o.inc_needed:
                ecount[o.engine] += 1
                inc_prefix[o.idx] = ecount[o.engine]

        waits = [[] for _ in self.ops]
        for i, ds in enumerate(deps):
            for d in sorted(ds):
                p = self.ops[d]
                if p.is_dma:
                    waits[i].append(("dma", p.dma_sem, p.dma_val, d))
                else:
                    waits[i].append(("eng", p.engine, inc_prefix[d], d))

        know = {e: {} for e in ENGINES}
        op_know = {}
        emit_waits = [[] for _ in self.ops]
        for o in self.ops:
            st = know[o.engine]
            for (kind, key, val, d) in waits[o.idx]:
                k = (kind, key)
                if val > st.get(k, 0):
                    emit_waits[o.idx].append((kind, key, val))
                    st[k] = val
                    pk = op_know.get(d)
                    if pk:
                        for kk, vv in pk.items():
                            if st.get(kk, 0) < vv:
                                st[kk] = vv
            k = dict(st)
            if not o.is_dma and o.inc_needed:
                k[("eng", o.engine)] = inc_prefix[o.idx]
            elif o.is_dma:
                k[("dma", o.dma_sem)] = o.dma_val
            op_know[o.idx] = k

        sems = {}
        for e in ENGINES:
            sems[("eng", e)] = nc.alloc_semaphore(f"S_{e}")
        for q in set(o.queue for o in self.ops if o.is_dma):
            for slot in range(DMA_RING):
                sems[("dma", (q, slot))] = nc.alloc_semaphore(f"D_{q}_{slot}")

        per_engine = {e: [] for e in ENGINES}
        for o in self.ops:
            per_engine[o.engine].append(o)

        blk = nc.Block()
        block = blk.__enter__()

        def run_engine(eng_handle, ename):
            issued_guard = {}
            for o in per_engine[ename]:
                for (kind, key, val) in emit_waits[o.idx]:
                    eng_handle.wait_ge(sems[(kind, key)], val)
                if o.is_dma:
                    prev_val = o.dma_val - 16
                    if prev_val > 0:
                        g = issued_guard.get(o.dma_sem, 0)
                        if g < prev_val:
                            eng_handle.wait_ge(sems[("dma", o.dma_sem)], prev_val)
                            issued_guard[o.dma_sem] = prev_val
                    ins = o.emit(eng_handle)
                    ins.then_inc(sems[("dma", o.dma_sem)], 16)
                else:
                    ins = o.emit(eng_handle)
                    if o.inc_needed:
                        ins.then_inc(sems[("eng", o.engine)], 1)

        if final_waits:
            finals = {}
            for o in self.ops:
                if o.is_dma:
                    finals[o.dma_sem] = o.dma_val

        @block.tensor
        def _(t):
            run_engine(t, "tensor")

        @block.vector
        def _(v):
            run_engine(v, "vector")

        @block.scalar
        def _(s):
            run_engine(s, "scalar")

        @block.gpsimd
        def _(g):
            run_engine(g, "gpsimd")

        @block.sync
        def _(s):
            run_engine(s, "sync")
            if final_waits:
                for sem_key, val in sorted(finals.items()):
                    s.wait_ge(sems[("dma", sem_key)], val)

        blk.__exit__(None, None, None)



import numpy as np
import concourse.bass as bass
import concourse.mybir as mybir
from concourse.ap import AP
from concourse.bass import IndirectOffsetOnAxis


F32 = mybir.dt.float32
BF16 = mybir.dt.bfloat16
I32 = mybir.dt.int32
AF = mybir.ActivationFunctionType
ALU = mybir.AluOpType
AX = mybir.AxisListType


class Cfg:
    def __init__(self, mini=False):
        if mini:
            self.T, self.D, self.H, self.NQ, self.NKV = 512, 256, 2, 128, 256
        else:
            self.T, self.D, self.H, self.NQ, self.NKV = 4096, 1024, 8, 1024, 2048
        self.NS = 64
        self.Q = 128
        self.DH = 64
        self.WIN = 64
        self.DT = 0.1
        self.NIT = 50
        self.HDH = self.H * self.DH
        self.NCH = self.T // self.Q
        self.NT = self.T // 128
        self.NW = self.T // self.WIN
        self.DC = self.D // 128
        self.KEFFQ = float(min(self.NQ * 9.0 / 8.0, 8192.0))
        self.KEFFKV = float(min(self.NKV * 9.0 / 8.0, 8192.0))
        self.CTX = ((self.NKV + 1 + 127) // 128) * 128
        self.NTC = self.CTX // 128
        self.SSMG = 16
        self.NSG = self.D // self.SSMG


def ts(i, n):
    return slice(i * n, (i + 1) * n)


def mkap(t_ap, offset, dims):
    return AP(tensor=t_ap.tensor, offset=offset,
              ap=[[step, count] for (count, step) in dims])


class KB:
    def __init__(self, cfg, nc, dbg=()):
        self.c = cfg
        self.nc = nc
        self.p = Prog(nc)
        self.dbg_spec = dbg
        self.q_x = "sync"
        self.q_w = "scalar"
        self.q_g = "gpsimd"

    def op(self, *a, **k):
        return self.p.op(*a, **k)

    def dma(self, *a, **k):
        return self.p.dma(*a, **k)

    def declare(self):
        c, p = self.c, self.p
        self.din = {}

        def dri(name, shape, dtype=F32):
            self.din[name] = p.dram(name, shape, dtype, kind="ExternalInput")
            return self.din[name]

        self.x_in = dri("x", [c.T, c.D])
        self.logA_nat_in = dri("logA_nat", [128, c.D * c.NS // 128])
        self.logA_sc_in = dri("logA_sc", [64, c.D])
        self.B_sc_in = dri("B_sc", [64, c.D])
        self.C_sc_in = dri("C_sc", [64, c.D])
        self.D_in = dri("Dvec", [1, c.D])
        self.lng_in = dri("ln_g", [1, c.D])
        self.lnb_in = dri("ln_b", [1, c.D])
        self.wqkv_in = dri("wqkv", [c.D, 3 * c.HDH])
        self.woutl_in = dri("woutl", [c.HDH, c.D])
        self.nullq_in = dri("nullq", [1, c.D])
        self.rt_in = dri("rt", [2, c.D])
        self.wqh_in = dri("wqh", [c.D, c.HDH])
        self.wkvh_in = dri("wkvh", [c.D, c.H * 128])
        self.nullkv_in = dri("nullkv", [2, c.H * c.DH])
        self.wouth_in = dri("wouth", [c.HDH, c.D])
        self.jcolU_in = dri("jcolU", [128, 1])
        self.lramp_in = dri("lramp", [128, c.Q])
        self.tokid_in = dri("tokid", [128, c.T // 128], I32)
        self.tri_in = dri("tri", [128, 128], BF16)
        self.ones_in = dri("onesc", [128, 2], BF16)
        self.onesrow_in = dri("onesrow", [1, 128], BF16)
        self.onesrowf_in = dri("onesrowf", [1, 128], F32)
        self.iden_in = dri("iden", [128, 128], F32)
        self.idenb_in = dri("idenb", [128, 128], BF16)
        self.lngc_in = dri("ln_g_col", [128, c.DC], F32)
        self.lnbc_in = dri("ln_b_col", [128, c.DC], BF16)
        self.rmgc_in = dri("rms_g_col", [128, c.DC], F32)

        self.out_d = p.dram("out", [c.T, c.D], F32, kind="ExternalOutput")
        self.dbg_d = {}
        for (nm, shp, dt) in self.dbg_spec:
            self.dbg_d[nm] = p.dram(nm, shp, dt, kind="ExternalOutput")

        dko = getattr(c, "dbg_out", ())
        def dk(nm):
            return "ExternalOutput" if nm in dko else "Internal"
        self.x1_d = p.dram("x1_d", [c.T, c.D], F32, kind=dk("x1_d"))
        self.xn0_d = p.dram("xn0_d", [c.T, c.D], BF16, kind=dk("xn0_d"))
        self.tdiag_d = p.dram("tdiag_d", [c.D, c.Q, 2 * c.Q], BF16, kind=dk("tdiag_d"))
        self.labf_d = p.dram("labf_d", [1, c.D * c.NS], F32, kind=dk("labf_d"))
        self.iq_d = p.dram("iq_d", [1, c.NQ + 128], I32, kind=dk("iq_d"))
        self.ikv_d = p.dram("ikv_d", [1, c.NKV + 128], I32, kind=dk("ikv_d"))

        BASE = 16640
        self.BASE = BASE
        RF = BASE + 28544
        S1 = RF + 35968
        S2 = S1 + 65536
        TOP = S2 + 65536
        self.RF, self.S1, self.S2, self.TOP = RF, S1, S2, TOP
        sb = p.sbuf
        self.junk = sb("junk", [128, 1024], BF16, at=BASE + 0)
        self.junk2 = sb("junk2", [128, 1024], BF16, at=BASE + 2048)
        A0 = BASE + 4096
        stp = max(128, c.NT * 4)
        for i, nm in enumerate(("sx", "ssq", "m1", "r1", "m2", "r2", "rp",
                                "scr", "scr2", "sx2", "ssq2", "pq", "pkv")):
            setattr(self, nm, sb(nm, [128, c.NT], F32, at=A0 + i * stp))
        self.iden_s = sb("iden_s", [128, 128], F32, at=BASE + 6144)
        self.idenb_s = sb("idenb_s", [128, 128], BF16, at=BASE + 5760)
        self.lramp_s = sb("lramp_s", [128, c.Q], F32, at=BASE + 6656)
        self.tri_s = sb("tri_s", [128, 128], BF16, at=BASE + 7168)
        self.jcolU_s = sb("jcolU_s", [128, 1], F32, at=BASE + 7424)
        self.ones_s = sb("ones_s", [128, 2], BF16, at=BASE + 7456)
        self.onesrow_s = sb("onesrow_s", [1, 128], BF16, at=BASE + 7488)
        self.onesrowf_s = sb("onesrowf_s", [1, 128], F32, at=BASE + 7744)
        self.lngc = sb("lngc", [128, c.DC], F32, at=BASE + 8256)
        self.lnbc = sb("lnbc", [128, c.DC], BF16, at=BASE + 8288)
        self.rmgc = sb("rmgc", [128, c.DC], F32, at=BASE + 8320)
        self.rcol2 = sb("rcol2", [128, 2], F32, at=BASE + 8352)
        self.rrow = sb("rrow", [1, 2], F32, at=BASE + 8384)
        self.bW = sb("bW", [128, 3 * c.HDH // 128], F32, at=BASE + 8416)
        self.zt = sb("zt", [128, 2 * c.Q], BF16, at=BASE + 8480)
        self.d_bc = sb("d_bc", [128, c.D], F32, at=BASE + 10112)
        self.dsb = [sb(f"dsb{i}", [1, 512], F32, at=BASE + 10112 + i * 2048)
                    for i in range(2)]
        self.rbh = sb("rbh", [128, 512], F32, at=BASE + 10112)
        self.dsbh = sb("dsbh", [1, 512], F32, at=BASE + 12160)
        self.woutl_bf = sb("woutl_bf", [128, (c.HDH // 128) * c.D], BF16,
                           at=BASE + 13312)
        self.lab_sc = sb("lab_sc", [64, c.D], F32, at=BASE + 14208)
        self.a128_sc = sb("a128_sc", [64, c.D], F32, at=BASE + 18304)
        self.cbc_sc = sb("cbc_sc", [64, c.D], F32, at=BASE + 22400)
        self.cbcb_sc = sb("cbcb_sc", [64, c.D], BF16, at=BASE + 26496)
        self.ftc = [sb(f"ftc{i}", [128, c.D], F32, at=BASE + 14208 + i * 4096)
                    for i in range(3)]
        self.wouth_bf = sb("wouth_bf", [128, (c.HDH // 128) * c.D], BF16,
                           at=BASE + 14208)
        self.null_bc = sb("null_bc", [128, c.D], F32, at=BASE + 22400)
        self.nullq_bf = sb("nullq_bf", [1, c.D], BF16, at=BASE + 26496)
        self.ft = [sb(f"ft{i}", [128, c.D], F32, at=RF + i * 4096) for i in range(3)]
        self.bt = [sb(f"bt{i}", [128, c.D], BF16, at=RF + 12288 + i * 2048)
                   for i in range(3)]
        self.ft2 = [sb(f"ft2_{i}", [128, c.D], F32, at=RF + 18432 + i * 4096)
                    for i in range(3)]
        self.la_nat = sb("la_nat", [128, c.D * c.NS // 128], F32, at=RF + 30720)
        self.tnat = sb("tnat", [128, c.D * c.NS // 128], F32, at=RF + 32768)
        self.la_sc = sb("la_sc", [64, c.D], F32, at=RF + 24576)
        self.labrow = [sb(f"labrow{i}", [1, c.SSMG * c.NS], BF16, at=TOP + i * 2048)
                       for i in range(2)]
        self.e_sc = sb("e_sc", [64, c.D], F32, at=RF)
        self.ab_sc = sb("ab_sc", [64, c.D], F32, at=RF + 4096)
        self.tmpa = sb("tmpa", [64, c.D], F32, at=RF + 8192)
        self.tmpb = sb("tmpb", [64, c.D], F32, at=RF + 12288)
        self.bsc_s = sb("bsc_s", [64, c.D], F32, at=RF + 16384)
        self.csc_s = sb("csc_s", [64, c.D], F32, at=RF + 20480)
        G = c.SSMG
        self.t_tab = [sb(f"t_tab{i}", [128, G * c.Q], BF16, at=RF + i * 4096)
                      for i in range(2)]
        self.u_tab = [sb(f"u_tab{i}", [128, G * c.NS], BF16, at=RF + 8192 + i * 2048)
                      for i in range(2)]
        self.v_tab = [sb(f"v_tab{i}", [64, G * c.Q], BF16, at=RF + 12288 + i * 4096)
                      for i in range(3)]
        self.hraw = [sb(f"hraw{i}", [64, G * c.NCH], F32, at=RF + 24576)
                     for i in range(1)]
        self.hgrp = [sb(f"hgrp{i}", [64, G * c.NCH], BF16, at=RF + 26624 + i * 1024)
                     for i in range(3)]
        self.kcol_sb = [sb(f"kcol_sb{i}", [128, G], F32, at=RF + 29696 + i * 64)
                        for i in range(2)]
        self.krow_sb = [sb(f"krow_sb{i}", [G, 130], BF16, at=RF + 29824 + i * 288)
                        for i in range(2)]
        self.varg = [sb(f"varg{i}", [64, G * c.Q], BF16, at=RF + 30400)
                     for i in range(1)]
        self.arep = sb("arep", [64, G * c.NCH], BF16, at=RF + 34496)
        W3 = 3 * c.HDH
        self.wqkv_bf = sb("wqkv_bf", [128, c.DC * W3], BF16, at=RF)
        NM = c.NW // 2
        self.abufA = sb("abufA", [128, NM * 256], BF16, at=RF)
        self.abufB = sb("abufB", [128, NM * 256], BF16, at=RF + NM * 512)
        self.zb = sb("zb", [128, 512], BF16, at=RF + NM * 1024)
        self.wqh_bf = sb("wqh_bf", [128, c.DC * c.HDH], BF16, at=RF)
        self.oHF = sb("oHF", [128, (c.HDH // 128) * c.NQ], BF16, at=RF + 8192)
        self.wkvh_bf = sb("wkvh_bf", [128, c.DC * c.H * 128], BF16, at=RF + 16384)
        self.hab = [sb(f"hab{i}", [128, 512], BF16, at=RF + 32768 + i * 1024)
                    for i in range(3)]
        self.x_ssm = sb("x_ssm", [128, c.D * c.NCH], BF16, at=S1)
        self.rtq_bc = sb("rtq_bc", [128, c.D], F32, at=S1)
        self.rtkv_bc = sb("rtkv_bc", [128, c.D], F32, at=S1 + 4096)
        self.XR = min(16, c.NT)
        self.TRX = self.XR * 128
        self.xnF = sb("xnF", [128, c.DC * self.TRX], BF16, at=S1)
        WV = c.H * 65
        self.vT = sb("vT", [128, c.NT * WV], BF16, at=S1)
        self.rb = [sb(f"rb{i}", [128, 512], F32, at=S1 + 33280 + i * 2048)
                   for i in range(2)]
        HCk = c.HDH // 128
        PSZ = c.T * 2
        self.kFp = [sb(f"kFp{i}", [128, c.T], BF16,
                       at=(S1 + 49152 + i * PSZ) if i < 2 else (TOP + (i - 2) * PSZ))
                    for i in range(HCk)]
        self.kvF = sb("kvF", [128, c.H * c.CTX], BF16, at=S1)
        WVH = c.H * 65
        self.vTh = sb("vTh", [128, c.NTC * WVH], BF16, at=S1 + 34816)
        self.qhF = sb("qhF", [128, (c.HDH // 128) * c.NQ], BF16, at=S1 + 53248)
        self.y_t = sb("y_t", [128, c.NT * c.D], BF16, at=S2)
        self.wqt = [sb(f"wqt{i}", [128, 3 * c.HDH], F32, at=S2 + i * 12288)
                    for i in range(2)]
        self.wqr = [sb(f"wqr{i}", [128, 3 * c.HDH], BF16, at=S2 + 24576 + i * 6144)
                    for i in range(2)]
        self.qF = sb("qF", [128, (c.HDH // 128) * c.T], BF16, at=S2)
        self.vF = sb("vF", [128, (c.HDH // 128) * c.T], BF16, at=S2 + 32768)
        self.oF = sb("oF", [128, (c.HDH // 128) * c.T], BF16, at=S2 + 32768)
        self.wotmp = sb("wotmp", [128, c.D], F32, at=S2)
        self.xqT = sb("xqT", [128, (c.NQ // 128) * c.D], BF16, at=S2)
        self.xqF = sb("xqF", [128, c.DC * c.NQ], BF16, at=S2 + 32768)
        self.xkT = sb("xkT", [128, (c.NKV // 128) * c.D], BF16, at=S2)
        self.xkF = sb("xkF", [128, c.DC * c.NKV], BF16, at=S2 + 32768)
        self.hwt = [sb(f"hwt{i}", [128, max(c.HDH, c.H * 128)], F32,
                       at=S2 + 49152 + i * 4096) for i in range(2)]
        self.rfx = [sb(f"rfx{i}", [128, c.D], F32, at=S2 + 57344 + i * 4096)
                    for i in range(2)]

        self.pp = [p.psum(f"pp{i}", [128, 512], F32) for i in range(6)]
        self.ppb = p.psum("ppb", [128, 1024], BF16)
        self.ppb2 = p.psum("ppb2", [128, 1024], BF16)

    def phase_A(self):
        c, p = self.c, self.p
        for i in range(c.NT):
            b = self.ft[i % 3]
            self.dma(self.q_x, lambda e, b=b, i=i: e.dma_start(
                out=b[:, :], in_=self.x_in[ts(i, 128), :]),
                reads=[self.x_in.r()], writes=[b.all()], tag=f"ldxA{i}")
            self.op("scalar", lambda e, b=b, i=i: e.activation(
                self.junk[:, 0:min(c.D, 1024)], b[:, 0:min(c.D, 1024)], AF.Square,
                accum_out=self.ssq[:, i:i + 1]),
                reads=[b.all()], writes=[self.junk.r(), self.ssq.r(f0=i, f1=i + 1)],
                tag=f"sq{i}")
            self.op("vector", lambda e, b=b, i=i: e.tensor_scalar(
                self.junk2[:, 0:min(c.D, 1024)], b[:, 0:min(c.D, 1024)], 1.0, None,
                ALU.mult, ALU.add, accum_out=self.sx[:, i:i + 1]),
                reads=[b.all()], writes=[self.junk2.r(), self.sx.r(f0=i, f1=i + 1)],
                tag=f"sxa{i}")
        self._stats(self.sx, self.ssq, self.m1, self.r1)

    def _stats(self, sxb, ssqb, mb, rb):
        c = self.c
        self.op("vector", lambda e: e.tensor_scalar_mul(mb[:, :], sxb[:, :], 1.0 / c.D),
                reads=[sxb.all()], writes=[mb.all()])
        self.op("vector", lambda e: e.tensor_scalar_mul(rb[:, :], ssqb[:, :], 1.0 / c.D),
                reads=[ssqb.all()], writes=[rb.all()])
        self.op("vector", lambda e: e.tensor_mul(self.scr[:, :], mb[:, :], mb[:, :]),
                reads=[mb.all()], writes=[self.scr.all()])
        self.op("vector", lambda e: e.tensor_sub(rb[:, :], rb[:, :], self.scr[:, :]),
                reads=[rb.all(), self.scr.all()], writes=[rb.all()])
        self.op("vector", lambda e: e.tensor_scalar_add(rb[:, :], rb[:, :], 1e-5),
                reads=[rb.all()], writes=[rb.all()])
        self.op("scalar", lambda e: e.activation(rb[:, :], rb[:, :], AF.Ln),
                reads=[rb.all()], writes=[rb.all()])
        self.op("scalar", lambda e: e.activation(rb[:, :], rb[:, :], AF.Exp, scale=-0.5),
                reads=[rb.all()], writes=[rb.all()])

    def phase_A2(self):
        c, p = self.c, self.p
        half = c.D // 2
        for i in range(c.NT):
            b = self.ft[i % 3]
            u = self.bt[i % 3]
            self.dma(self.q_x, lambda e, b=b, i=i: e.dma_start(
                out=b[:, :], in_=self.x_in[ts(i, 128), :]),
                reads=[self.x_in.r()], writes=[b.all()], tag=f"ldxB{i}")
            self.op("vector", lambda e, b=b, u=u, i=i: e.tensor_scalar(
                u[:, :], b[:, :], self.m1[:, i:i + 1], self.r1[:, i:i + 1],
                ALU.subtract, ALU.mult),
                reads=[b.all(), self.m1.r(f0=i, f1=i + 1), self.r1.r(f0=i, f1=i + 1)],
                writes=[u.all()], tag=f"nrm1_{i}")
            for hh, eng in ((0, "vector"), (1, "gpsimd")):
                def emit(e, u=u, i=i, hh=hh):
                    dst = mkap(self.x_ssm.ap(), hh * half * c.NCH + i,
                               [(128, c.D * c.NCH), (half, c.NCH)])
                    return e.tensor_copy(dst, u[:, hh * half:(hh + 1) * half])
                self.op(eng, emit, reads=[u.all()],
                        writes=[self.x_ssm.r(f0=hh * half * c.NCH,
                                             f1=(hh + 1) * half * c.NCH)],
                        tag=f"rearr{i}_{hh}")

    def ssm_prep(self):
        c = self.c
        A = self.op

        def ld(name, dst, src, queue=None):
            self.dma(queue or self.q_w, lambda e, dst=dst, src=src: e.dma_start(
                out=dst[:, :], in_=src[:, :]),
                reads=[src.r()], writes=[dst.all()], tag=f"ld_{name}")

        for (nm, dst, src) in [("iden", self.iden_s, self.iden_in),
                               ("idenb", self.idenb_s, self.idenb_in),
                               ("jcolU", self.jcolU_s, self.jcolU_in),
                               ("lramp", self.lramp_s, self.lramp_in),
                               ("ones", self.ones_s, self.ones_in),
                               ("onesrow", self.onesrow_s, self.onesrow_in),
                               ("onesrowf", self.onesrowf_s, self.onesrowf_in),
                               ("tri", self.tri_s, self.tri_in),
                               ("lasc", self.la_sc, self.logA_sc_in),
                               ("bsc", self.bsc_s, self.B_sc_in),
                               ("csc", self.csc_s, self.C_sc_in),
                               ("lanat", self.la_nat, self.logA_nat_in)]:
            ld(nm, dst, src)

        A("scalar", lambda e: e.activation(self.e_sc[:, :], self.la_sc[:, :], AF.Exp),
          reads=[self.la_sc.all()], writes=[self.e_sc.all()])
        A("scalar", lambda e: e.activation(self.ab_sc[:, :], self.e_sc[:, :], AF.Exp,
                                           scale=-c.DT),
          reads=[self.e_sc.all()], writes=[self.ab_sc.all()])
        A("scalar", lambda e: e.activation(self.lab_sc[:, :], self.ab_sc[:, :], AF.Ln),
          reads=[self.ab_sc.all()], writes=[self.lab_sc.all()])
        A("scalar", lambda e: e.activation(self.a128_sc[:, :], self.lab_sc[:, :], AF.Exp,
                                           scale=float(c.Q)),
          reads=[self.lab_sc.all()], writes=[self.a128_sc.all()])
        A("vector", lambda e: e.tensor_scalar_add(self.tmpa[:, :], self.ab_sc[:, :], -1.0),
          reads=[self.ab_sc.all()], writes=[self.tmpa.all()])
        A("vector", lambda e: e.tensor_scalar_mul(self.tmpb[:, :], self.e_sc[:, :], -1.0),
          reads=[self.e_sc.all()], writes=[self.tmpb.all()])
        A("vector", lambda e: e.reciprocal(self.tmpb[:, :], self.tmpb[:, :]),
          reads=[self.tmpb.all()], writes=[self.tmpb.all()])
        A("vector", lambda e: e.tensor_mul(self.tmpa[:, :], self.tmpa[:, :],
                                           self.tmpb[:, :]),
          reads=[self.tmpa.all(), self.tmpb.all()], writes=[self.tmpa.all()])
        A("vector", lambda e: e.tensor_mul(self.tmpa[:, :], self.tmpa[:, :], self.bsc_s[:, :]),
          reads=[self.tmpa.all(), self.bsc_s.all()], writes=[self.tmpa.all()])
        A("vector", lambda e: e.tensor_mul(self.cbc_sc[:, :], self.tmpa[:, :], self.csc_s[:, :]),
          reads=[self.tmpa.all(), self.csc_s.all()], writes=[self.cbc_sc.all()])
        A("vector", lambda e: e.tensor_copy(self.cbcb_sc[:, :], self.cbc_sc[:, :]),
          reads=[self.cbc_sc.all()], writes=[self.cbcb_sc.all()])
        A("scalar", lambda e: e.activation(self.tnat[:, :], self.la_nat[:, :], AF.Exp),
          reads=[self.la_nat.all()], writes=[self.tnat.all()])
        A("scalar", lambda e: e.activation(self.tnat[:, :], self.tnat[:, :], AF.Exp,
                                           scale=-c.DT),
          reads=[self.tnat.all()], writes=[self.tnat.all()])
        A("scalar", lambda e: e.activation(self.tnat[:, :], self.tnat[:, :], AF.Ln),
          reads=[self.tnat.all()], writes=[self.tnat.all()])
        NF = c.D * c.NS // 128

        def emit_labw(e):
            dst = mkap(self.labf_d.ap(), 0, [(128, c.NS), (c.DC, 128 * c.NS), (c.NS, 1)])
            src = mkap(self.tnat.ap(), 0, [(128, NF), (c.DC, c.NS), (c.NS, 1)])
            return e.dma_start(out=dst, in_=src)
        self.dma(self.q_w, emit_labw, reads=[self.tnat.all()], writes=[self.labf_d.r()],
                 tag="labflat")
        zt = self.zt
        A("gpsimd", lambda e: e.memset(zt[:, :], 0.0), reads=[], writes=[zt.all()])
        for cb in range(c.D // 128):
            def emit_zf(e, cb=cb):
                dst = mkap(self.tdiag_d.ap(), cb * 128 * c.Q * 2 * c.Q,
                           [(128, c.Q * 2 * c.Q), (c.Q, 2 * c.Q), (c.Q, 1)])
                src = AP(tensor=zt.ap().tensor, offset=0,
                         ap=[[2 * c.Q, 128], [0, c.Q], [1, c.Q]])
                return e.dma_start(out=dst, in_=src)
            self.dma(self.q_x, emit_zf, reads=[zt.all()],
                     writes=[self.tdiag_d.r(p0=cb * 128, p1=(cb + 1) * 128)],
                     tag=f"zfill{cb}")

    def ssm_loop(self):
        c = self.c
        A = self.op
        G = c.SSMG
        nhw = c.NCH

        def group_front(g):
            par = g % 2
            lr = self.labrow[par]
            utab = self.u_tab[par]
            vtab = self.v_tab[g % 3]
            va = self.varg[0]
            hr = self.hraw[0]
            hg = self.hgrp[g % 3]
            kcol = self.kcol_sb[par]
            krow = self.krow_sb[par]
            sps = self.pp[2 + par]
            kps = self.pp[4]
            ch0 = g * G
            self.dma(self.q_g, lambda e, lr=lr, ch0=ch0: e.dma_start(
                out=lr[0:1, :], in_=self.labf_d[0:1, ch0 * c.NS:(ch0 + G) * c.NS]),
                reads=[self.labf_d.r()], writes=[lr.all()], tag=f"lrow{g}")
            UW = G * c.NS
            for ci in range(UW // 512):
                A("tensor", lambda e, lr=lr, ci=ci: e.matmul(
                    self.pp[5][:, 0:512], self.onesrow_s[0:1, :], lr[0:1, ts(ci, 512)],
                    start=True, stop=True),
                  reads=[self.onesrow_s.r(), lr.all()], writes=[self.pp[5].r()],
                  tag=f"ubc{g}_{ci}")
                A("scalar", lambda e, utab=utab, ci=ci: e.activation(
                    utab[:, ts(ci, 512)], self.pp[5][:, 0:512], AF.Exp,
                    scale=self.jcolU_s[:, 0:1]),
                  reads=[self.pp[5].r(), self.jcolU_s.all()],
                  writes=[utab.r(f0=ci * 512, f1=(ci + 1) * 512)], tag=f"uexp{g}_{ci}")
            def emit_varg(e, va=va, ch0=ch0):
                i0 = mkap(self.lab_sc.ap(), ch0, [(64, c.D), (G, 1), (c.Q, 0)])
                i1 = mkap(self.lramp_s.ap(), 0, [(64, c.Q), (G, 0), (c.Q, 1)])
                return e.tensor_tensor(va[:, :], i0, i1, ALU.mult)
            A("vector", emit_varg, reads=[self.lab_sc.r(), self.lramp_s.r()],
              writes=[va.all()], tag=f"varg{g}")
            A("scalar", lambda e, vtab=vtab, va=va: e.activation(vtab[:, :], va[:, :],
                                                                 AF.Exp),
              reads=[va.all()], writes=[vtab.all()], tag=f"vexp{g}")
            for k in range(G):
                ch = ch0 + k
                A("tensor", lambda e, utab=utab, k=k, ch=ch, sps=sps: e.matmul(
                    sps[0:64, ts(k, nhw)], utab[:, ts(k, c.NS)],
                    self.x_ssm[:, ts(ch, c.NCH)], start=True, stop=True),
                  reads=[utab.r(f0=k * c.NS, f1=(k + 1) * c.NS),
                         self.x_ssm.r(f0=ch * c.NCH, f1=(ch + 1) * c.NCH)],
                  writes=[sps.r(p0=0, p1=64, f0=k * nhw, f1=(k + 1) * nhw)],
                  tag=f"umm{g}_{k}")
            def emit_arep(e, ch0=ch0):
                src_ = mkap(self.a128_sc.ap(), ch0, [(64, c.D), (G, 1), (c.NCH, 0)])
                return e.tensor_copy(self.arep[:, :], src_)
            A("gpsimd", emit_arep, reads=[self.a128_sc.r()], writes=[self.arep.all()],
              tag=f"arepc{g}")
            def emit_arz(e):
                dst = mkap(self.arep.ap(), 0, [(64, G * c.NCH), (G, c.NCH), (1, 1)])
                return e.memset(dst, 0.0)
            A("gpsimd", emit_arz, reads=[], writes=[self.arep.all()], tag=f"arz{g}")
            A("vector", lambda e, hr=hr, sps=sps: e.tensor_tensor_scan(
                hr[:, :], self.arep[0:64, :], sps[0:64, 0:G * nhw], 0.0,
                ALU.mult, ALU.add),
              reads=[self.arep.all(), sps.r(p0=0, p1=64)], writes=[hr.all()],
              tag=f"scan{g}")
            def emit_h(e, hg=hg, hr=hr, ch0=ch0):
                i1 = mkap(self.cbc_sc.ap(), ch0, [(64, c.D), (G, 1), (c.NCH, 0)])
                return e.tensor_tensor(hg[:, :], hr[:, :], i1, ALU.mult)
            A("vector", emit_h, reads=[hr.all(), self.cbc_sc.r()], writes=[hg.all()],
              tag=f"hmul{g}")
            for k in range(G):
                ch = ch0 + k
                A("tensor", lambda e, vtab=vtab, k=k, ch=ch, kps=kps: e.matmul(
                    kps[:, k:k + 1], vtab[0:64, ts(k, c.Q)],
                    self.cbcb_sc[0:64, ch:ch + 1], start=True, stop=True),
                  reads=[vtab.r(f0=k * c.Q, f1=(k + 1) * c.Q),
                         self.cbcb_sc.r(f0=ch, f1=ch + 1)],
                  writes=[kps.r(f0=k, f1=k + 1)], tag=f"kmm{g}_{k}")
            A("vector", lambda e, kcol=kcol, kps=kps: e.tensor_copy(kcol[:, :],
                                                                    kps[:, 0:G]),
              reads=[kps.r(f0=0, f1=G)], writes=[kcol.all()], tag=f"kev{g}")
            A("tensor", lambda e, kcol=kcol: e.transpose(
                self.pp[5][0:G, 0:128], kcol[:, :], self.iden_s[:, :]),
              reads=[kcol.all(), self.iden_s.all()], writes=[self.pp[5].r(p0=0, p1=G)],
              tag=f"ktr{g}")
            A("vector", lambda e, krow=krow: e.tensor_copy(krow[:, 1:129],
                                                           self.pp[5][0:G, 0:128]),
              reads=[self.pp[5].r(p0=0, p1=G)], writes=[krow.all()], tag=f"ktev{g}")
            A("tensor", lambda e, ch0=ch0: e.matmul(
                self.pp[5][0:G, 128:129], self.cbcb_sc[0:64, ch0:ch0 + G],
                self.ones_s[0:64, 0:1], start=True, stop=True),
              reads=[self.cbcb_sc.r(f0=ch0, f1=ch0 + G), self.ones_s.r()],
              writes=[self.pp[5].r(p0=0, p1=G, f0=128, f1=129)], tag=f"k0mmg{g}")
            A("vector", lambda e, krow=krow: e.tensor_copy(krow[:, 0:1],
                                                           self.pp[5][0:G, 128:129]),
              reads=[self.pp[5].r(p0=0, p1=G, f0=128, f1=129)],
              writes=[krow.all()], tag=f"k0evg{g}")
            def emit_kd(e, krow=krow, ch0=ch0):
                dst = mkap(self.tdiag_d.ap(), ch0 * c.Q * 2 * c.Q,
                           [(G, c.Q * 2 * c.Q), (c.Q, 2 * c.Q + 1), (c.Q, 1)])
                src_ = AP(tensor=krow.ap().tensor, offset=0,
                          ap=[[130, G], [0, c.Q], [1, c.Q]])
                return e.dma_start(out=dst, in_=src_)
            self.dma(self.q_x, emit_kd, reads=[krow.all()],
                     writes=[self.tdiag_d.r(p0=ch0, p1=ch0 + G,
                                            f0=0, f1=c.Q * 2 * c.Q - 1)],
                     tag=f"kdiag{g}")

        def group_back(g):
            par = g % 2
            ttab = self.t_tab[par]
            vtab = self.v_tab[g % 3]
            hg = self.hgrp[g % 3]
            yps = self.pp[par]
            ch0 = g * G
            def emit_tr(e, ttab=ttab, ch0=ch0):
                src_ = mkap(self.tdiag_d.ap(), ch0 * c.Q * 2 * c.Q,
                            [(c.Q, 2 * c.Q), (G, c.Q * 2 * c.Q), (c.Q, 1)])
                dst = mkap(ttab.ap(), 0, [(c.Q, G * c.Q), (G, c.Q), (c.Q, 1)])
                return e.dma_start(out=dst, in_=src_)
            self.dma(self.q_w, emit_tr, reads=[self.tdiag_d.r(p0=ch0, p1=ch0 + G)],
                     writes=[ttab.all()], tag=f"tread{g}")
            for k in range(G):
                ch = ch0 + k
                A("tensor", lambda e, ttab=ttab, k=k, ch=ch, yps=yps: e.matmul(
                    yps[:, ts(k, c.NCH)], ttab[:, ts(k, c.Q)],
                    self.x_ssm[:, ts(ch, c.NCH)], start=True, stop=False),
                  reads=[ttab.r(f0=k * c.Q, f1=(k + 1) * c.Q),
                         self.x_ssm.r(f0=ch * c.NCH, f1=(ch + 1) * c.NCH)],
                  writes=[yps.r(f0=k * c.NCH, f1=(k + 1) * c.NCH)], tag=f"tmm{g}_{k}")
                A("tensor", lambda e, vtab=vtab, k=k, hg=hg, yps=yps: e.matmul(
                    yps[:, k * c.NCH + 1:(k + 1) * c.NCH], vtab[0:64, ts(k, c.Q)],
                    hg[0:64, k * c.NCH: (k + 1) * c.NCH - 1], start=False, stop=True),
                  reads=[vtab.r(f0=k * c.Q, f1=(k + 1) * c.Q),
                         hg.r(f0=k * c.NCH, f1=(k + 1) * c.NCH)],
                  writes=[yps.r(f0=k * c.NCH, f1=(k + 1) * c.NCH)], tag=f"vmm{g}_{k}")
            def emit_yev(e, yps=yps, ch0=ch0):
                dst = mkap(self.y_t.ap(), ch0, [(128, c.NT * c.D), (c.NCH, c.D), (G, 1)])
                src_ = mkap(yps.ap(), 0, [(128, 512), (c.NCH, 1), (G, c.NCH)])
                return e.tensor_copy(dst, src_)
            A("vector", emit_yev, reads=[yps.r()], writes=[self.y_t.r()],
              tag=f"yev{g}")

        LAG = 2
        for g in range(c.NSG + LAG):
            if g < c.NSG:
                group_front(g)
            if g >= LAG:
                group_back(g - LAG)

    def bcast_rows(self):
        c = self.c
        for (nm, dst, src, row) in [("dbc", self.d_bc, self.D_in, 0)]:
            def emit(e, dst=dst, src=src, row=row):
                sap = mkap(src.ap(), row * c.D, [(128, 0), (c.D, 1)])
                return e.dma_start(out=dst[:, :], in_=sap)
            self.dma(self.q_x, emit, reads=[src.r()], writes=[dst.all()], tag=f"bc_{nm}")

    def phase_C1(self):
        c, p = self.c, self.p
        for i in range(c.NT):
            b = self.ft[i % 3]
            t = self.ftc[i % 3]
            x1 = self.ft2[i % 3]
            self.dma(self.q_x, lambda e, b=b, i=i: e.dma_start(
                out=b[:, :], in_=self.x_in[ts(i, 128), :]),
                reads=[self.x_in.r()], writes=[b.all()], tag=f"ldxC{i}")
            self.op("vector", lambda e, b=b, t=t, i=i: e.scalar_tensor_tensor(
                t[:, :], b[:, :], self.m1[:, i:i + 1], self.d_bc[:, :],
                ALU.subtract, ALU.mult),
                reads=[b.all(), self.m1.r(f0=i, f1=i + 1), self.d_bc.all()],
                writes=[t.all()], tag=f"c1a{i}")
            self.op("vector", lambda e, b=b, t=t, x1=x1, i=i: e.scalar_tensor_tensor(
                x1[:, :], t[:, :], self.r1[:, i:i + 1], b[:, :], ALU.mult, ALU.add),
                reads=[t.all(), self.r1.r(f0=i, f1=i + 1), b.all()],
                writes=[x1.all()], tag=f"c1b{i}")
            self.op("gpsimd", lambda e, x1=x1, i=i: e.tensor_tensor(
                x1[:, :], x1[:, :], self.y_t[:, ts(i, c.D)], ALU.add),
                reads=[x1.all(), self.y_t.r(f0=i * c.D, f1=(i + 1) * c.D)],
                writes=[x1.all()], tag=f"c1y{i}")
            self.dma(self.q_x, lambda e, x1=x1, i=i: e.dma_start(
                out=self.x1_d[ts(i, 128), :], in_=x1[:, :]),
                reads=[x1.all()], writes=[self.x1_d.r(p0=i * 128, p1=(i + 1) * 128)],
                tag=f"stx1_{i}")
            self.op("scalar", lambda e, x1=x1, i=i: e.activation(
                self.junk[:, 0:min(c.D, 1024)], x1[:, 0:min(c.D, 1024)], AF.Square,
                accum_out=self.ssq2[:, i:i + 1]),
                reads=[x1.all()], writes=[self.junk.r(), self.ssq2.r(f0=i, f1=i + 1)],
                tag=f"c1sq{i}")
            self.op("vector", lambda e, x1=x1, i=i: e.tensor_scalar(
                self.junk2[:, 0:min(c.D, 1024)], x1[:, 0:min(c.D, 1024)], 1.0, None,
                ALU.mult, ALU.add, accum_out=self.sx2[:, i:i + 1]),
                reads=[x1.all()], writes=[self.junk2.r(), self.sx2.r(f0=i, f1=i + 1)],
                tag=f"c1sx{i}")
        self._stats(self.sx2, self.ssq2, self.m2, self.r2)
        A = self.op
        A("vector", lambda e: e.tensor_scalar_mul(self.scr[:, :], self.ssq2[:, :], 1.0 / c.D),
          reads=[self.ssq2.all()], writes=[self.scr.all()])
        A("vector", lambda e: e.tensor_mul(self.scr2[:, :], self.m2[:, :], self.m2[:, :]),
          reads=[self.m2.all()], writes=[self.scr2.all()])
        A("vector", lambda e: e.tensor_sub(self.scr[:, :], self.scr[:, :], self.scr2[:, :]),
          reads=[self.scr.all(), self.scr2.all()], writes=[self.scr.all()])
        A("vector", lambda e: e.tensor_scalar_add(self.scr2[:, :], self.scr[:, :], 1e-5),
          reads=[self.scr.all()], writes=[self.scr2.all()])
        A("vector", lambda e: e.reciprocal(self.scr2[:, :], self.scr2[:, :]),
          reads=[self.scr2.all()], writes=[self.scr2.all()])
        A("vector", lambda e: e.tensor_mul(self.scr[:, :], self.scr[:, :], self.scr2[:, :]),
          reads=[self.scr.all(), self.scr2.all()], writes=[self.scr.all()])
        A("vector", lambda e: e.tensor_scalar_add(self.scr[:, :], self.scr[:, :], 1e-5),
          reads=[self.scr.all()], writes=[self.scr.all()])
        A("scalar", lambda e: e.activation(self.scr[:, :], self.scr[:, :], AF.Ln),
          reads=[self.scr.all()], writes=[self.scr.all()])
        A("scalar", lambda e: e.activation(self.scr[:, :], self.scr[:, :], AF.Exp, scale=-0.5),
          reads=[self.scr.all()], writes=[self.scr.all()])
        A("vector", lambda e: e.tensor_mul(self.rp[:, :], self.r2[:, :], self.scr[:, :]),
          reads=[self.r2.all(), self.scr.all()], writes=[self.rp.all()])

    def phase_C2(self):
        c, p = self.c, self.p
        for (nm, dst, src_, row) in [("rtqbc", self.rtq_bc, self.rt_in, 0),
                                     ("rtkvbc", self.rtkv_bc, self.rt_in, 1)]:
            def emitb(e, dst=dst, src_=src_, row=row):
                sap = mkap(src_.ap(), row * c.D, [(128, 0), (c.D, 1)])
                return e.dma_start(out=dst[:, :], in_=sap)
            self.dma(self.q_x, emitb, reads=[src_.r()], writes=[dst.all()],
                     tag=f"bc_{nm}")
        for i in range(c.NT):
            x1 = self.ft[i % 3]
            u = self.bt[i % 3]
            self.dma(self.q_x, lambda e, x1=x1, i=i: e.dma_start(
                out=x1[:, :], in_=self.x1_d[ts(i, 128), :]),
                reads=[self.x1_d.r(p0=i * 128, p1=(i + 1) * 128)], writes=[x1.all()],
                tag=f"ldx1_{i}")
            self.op("vector", lambda e, x1=x1, u=u, i=i: e.tensor_scalar(
                u[:, :], x1[:, :], self.m2[:, i:i + 1], self.rp[:, i:i + 1],
                ALU.subtract, ALU.mult),
                reads=[x1.all(), self.m2.r(f0=i, f1=i + 1), self.rp.r(f0=i, f1=i + 1)],
                writes=[u.all()], tag=f"xn0_{i}")
            self.dma(self.q_x, lambda e, u=u, i=i: e.dma_start(
                out=self.xn0_d[ts(i, 128), :], in_=u[:, :]),
                reads=[u.all()], writes=[self.xn0_d.r(p0=i * 128, p1=(i + 1) * 128)],
                tag=f"stxn_{i}")
            t0 = self.ftc[i % 3]
            t1 = self.ft2[i % 3]
            self.op("vector", lambda e, x1=x1, t0=t0: e.tensor_mul(
                t0[:, :], x1[:, :], self.rtq_bc[:, :]),
                reads=[x1.all(), self.rtq_bc.all()], writes=[t0.all()], tag=f"pq_{i}")
            self.op("scalar", lambda e, t0=t0, i=i: e.activation(
                self.junk[:, 0:min(c.D, 1024)], t0[:, 0:min(c.D, 1024)], AF.Copy,
                accum_out=self.pq[:, i:i + 1]),
                reads=[t0.all()], writes=[self.junk.r(), self.pq.r(f0=i, f1=i + 1)],
                tag=f"pqa_{i}")
            self.op("gpsimd", lambda e, x1=x1, t1=t1: e.tensor_tensor(
                t1[:, :], x1[:, :], self.rtkv_bc[:, :], ALU.mult),
                reads=[x1.all(), self.rtkv_bc.all()], writes=[t1.all()], tag=f"pkv_{i}")
            self.op("scalar", lambda e, t1=t1, i=i: e.activation(
                self.junk2[:, 0:min(c.D, 1024)], t1[:, 0:min(c.D, 1024)], AF.Copy,
                accum_out=self.pkv[:, i:i + 1]),
                reads=[t1.all()], writes=[self.junk2.r(), self.pkv.r(f0=i, f1=i + 1)],
                tag=f"pkva_{i}")
        rrow = self.rrow
        for route in range(2):
            self.op("vector", lambda e, route=route: e.reduce_sum(
                rrow[0:1, route:route + 1],
                (self.rtq_bc if route == 0 else self.rtkv_bc)[0:1, :],
                axis=AX.X),
                reads=[(self.rtq_bc if route == 0 else self.rtkv_bc).r(p0=0, p1=1)],
                writes=[rrow.r(f0=route, f1=route + 1)], tag=f"rsum{route}")
        self.op("tensor", lambda e: e.matmul(self.pp[5][:, 0:2], self.onesrowf_s[0:1, :],
                                             rrow[0:1, 0:2], start=True, stop=True),
                reads=[rrow.all()], writes=[self.pp[5].r(f0=0, f1=2)], tag="rbc")
        self.op("vector", lambda e: e.tensor_copy(self.rcol2[:, :], self.pp[5][:, 0:2]),
                reads=[self.pp[5].r(f0=0, f1=2)], writes=[self.rcol2.all()], tag="rbce")


    def light_prep(self):
        c, p = self.c, self.p
        for ci in range(min(8, c.NT)):
            self._xnF_transpose(ci)
        for nm, dst, srcb in [("lngc", self.lngc, self.lngc_in),
                              ("lnbc", self.lnbc, self.lnbc_in),
                              ("rmgc", self.rmgc, self.rmgc_in)]:
            self.dma(self.q_w, lambda e, dst=dst, srcb=srcb: e.dma_start(
                out=dst[:, :], in_=srcb[:, :]),
                reads=[srcb.r()], writes=[dst.all()], tag=f"ld_{nm}")
        W3 = 3 * c.HDH
        NFC = W3 // 128
        for dc in range(c.DC):
            wt = self.wqt[dc % 2]
            wr = self.wqr[dc % 2]
            self.dma(self.q_w, lambda e, wt=wt, dc=dc: e.dma_start(
                out=wt[:, :], in_=self.wqkv_in[ts(dc, 128), :]),
                reads=[self.wqkv_in.r()], writes=[wt.all()], tag=f"ldwqkv{dc}")
            self.op("vector", lambda e, wt=wt, wr=wr: e.tensor_copy(wr[:, :], wt[:, :]),
                    reads=[wt.all()], writes=[wr.all()], tag=f"wc{dc}")
            self.op("vector", lambda e, wt=wt, dc=dc: e.tensor_scalar_mul(
                self.wqkv_bf[:, ts(dc, W3)], wt[:, :], self.lngc[:, dc:dc + 1]),
                reads=[wt.all(), self.lngc.r(f0=dc, f1=dc + 1)],
                writes=[self.wqkv_bf.r(f0=dc * W3, f1=(dc + 1) * W3)], tag=f"wg{dc}")
            for fc in range(NFC):
                self.op("tensor", lambda e, wr=wr, dc=dc, fc=fc: e.matmul(
                    self.pp[5][:, fc:fc + 1], wr[:, ts(fc, 128)],
                    self.lnbc[:, dc:dc + 1], start=(dc == 0 and fc == 0),
                    stop=(dc == c.DC - 1 and fc == NFC - 1)),
                    reads=[wr.all(), self.lnbc.r(f0=dc, f1=dc + 1)],
                    writes=[self.pp[5].r(f0=fc, f1=fc + 1)], tag=f"bw{dc}_{fc}")
        self.op("vector", lambda e: e.tensor_copy(self.bW[:, :], self.pp[5][:, 0:NFC]),
                reads=[self.pp[5].r(f0=0, f1=NFC)], writes=[self.bW.all()], tag="bwev")

    def _xnF_transpose(self, ci):
        c = self.c
        TRX = self.TRX
        def emit_tr(e, ci=ci):
            out = mkap(self.xnF.ap(), (ci % self.XR) * 128,
                       [(128, c.DC * TRX), (c.DC, TRX), (128, 1)])
            inp = mkap(self.xn0_d.ap(), ci * 128 * c.D,
                       [(128, c.D), (c.DC, 128), (128, 1)])
            return e.dma_start_transpose(out, inp)
        sl = (ci % self.XR) * 128
        self.dma(self.q_w, emit_tr, reads=[self.xn0_d.r(p0=ci * 128, p1=(ci + 1) * 128)],
                 writes=[self.xnF.r(f0=dc * TRX + sl, f1=dc * TRX + sl + 128)
                         for dc in range(c.DC)], tag=f"xnF_tr{ci}")

    def light_qkv(self):
        c, p = self.c, self.p
        W3 = 3 * c.HDH
        NFC = W3 // 128
        HC = c.HDH // 128
        TT512 = c.T // 512
        TRX = self.TRX
        for tt in range(TT512):
            for ci in range(4 * (tt + 2), min(4 * (tt + 3), c.NT)):
                self._xnF_transpose(ci)
            x0 = (tt * 512) % TRX
            for fc in range(NFC):
                kind = fc // HC
                hc = fc % HC
                dstb = (self.qF, None, self.vF)[kind]
                ps = self.pp[fc % 2]
                for dc in range(c.DC):
                    self.op("tensor", lambda e, ps=ps, fc=fc, dc=dc, x0=x0: e.matmul(
                        ps[:, 0:512], self.wqkv_bf[:, dc * W3 + fc * 128: dc * W3 + (fc + 1) * 128],
                        self.xnF[:, dc * TRX + x0: dc * TRX + x0 + 512],
                        start=(dc == 0), stop=(dc == c.DC - 1)),
                        reads=[self.wqkv_bf.r(f0=dc * W3 + fc * 128, f1=dc * W3 + (fc + 1) * 128),
                               self.xnF.r(f0=dc * TRX + x0, f1=dc * TRX + x0 + 512)],
                        writes=[ps.r()], tag=f"qkv{fc}_{tt}_{dc}")
                if kind == 1:
                    kf = self.kFp[hc]
                    self.op("vector", lambda e, ps=ps, kf=kf, tt=tt, fc=fc: e.tensor_scalar_add(
                        kf[:, tt * 512:(tt + 1) * 512],
                        ps[:, 0:512], self.bW[:, fc:fc + 1]),
                        reads=[ps.r(), self.bW.r(f0=fc, f1=fc + 1)],
                        writes=[kf.r(f0=tt * 512, f1=(tt + 1) * 512)],
                        tag=f"qkvev{fc}_{tt}")
                else:
                    self.op("vector", lambda e, ps=ps, dstb=dstb, hc=hc, tt=tt, fc=fc: e.tensor_scalar_add(
                        dstb[:, hc * c.T + tt * 512: hc * c.T + (tt + 1) * 512],
                        ps[:, 0:512], self.bW[:, fc:fc + 1]),
                        reads=[ps.r(), self.bW.r(f0=fc, f1=fc + 1)],
                        writes=[dstb.r(f0=hc * c.T + tt * 512, f1=hc * c.T + (tt + 1) * 512)],
                        tag=f"qkvev{fc}_{tt}")

    def light_vT(self):
        c, p = self.c, self.p
        HC = c.HDH // 128
        WV = c.H * 65
        def emit_ones(e):
            dst = mkap(self.vT.ap(), 64, [(128, c.NT * WV), (c.NT, WV), (c.H, 65)])
            return e.memset(dst, 1.0)
        self.op("gpsimd", emit_ones, reads=[], writes=[self.vT.r()], tag="vT_ones")
        for vc in range(HC):
            for m in range(c.NT):
                ps = (self.ppb, self.ppb2)[m % 2]
                self.op("tensor", lambda e, vc=vc, m=m, ps=ps: e.transpose(
                    ps[:, 0:128],
                    self.vF[:, vc * c.T + m * 128: vc * c.T + (m + 1) * 128],
                    self.idenb_s[:, :]),
                    reads=[self.vF.r(f0=vc * c.T + m * 128, f1=vc * c.T + (m + 1) * 128),
                           self.idenb_s.all()],
                    writes=[ps.r(f0=0, f1=128)], tag=f"vTtr{vc}_{m}")
                for hh in range(2):
                    co = m * WV + (2 * vc + hh) * 65
                    self.op("vector", lambda e, co=co, hh=hh, ps=ps: e.tensor_copy(
                        self.vT[:, co:co + 64], ps[:, hh * 64: hh * 64 + 64]),
                        reads=[ps.r(f0=hh * 64, f1=hh * 64 + 64)],
                        writes=[self.vT.r(f0=co, f1=co + 64)], tag=f"vTev{vc}_{m}_{hh}")

    def light_attn(self):
        c, p = self.c, self.p
        HC = c.HDH // 128
        NM = c.NW // 2
        WV = c.H * 65
        TT512 = c.T // 512
        self.op("gpsimd", lambda e: e.memset(self.zb[:, :], 0.0), reads=[],
                writes=[self.zb.all()], tag="zbm")
        abufA = self.abufA
        abufB = self.abufB
        rb = self.rb
        dsb = self.dsb
        for fc in range(HC):
            for m in range(NM):
                s0 = max(0, 128 * m - 64)
                s1 = min(c.T, 128 * m + 192)
                wq = s1 - s0
                co = s0 - (128 * m - 64)
                for par, ab in ((0, abufA), (1, abufB)):
                    ps = self.pp[2 + par]
                    self.op("tensor", lambda e, fc=fc, m=m, par=par, ps=ps, s0=s0, wq=wq: e.matmul(
                        ps[:, 0:wq],
                        self.kFp[fc][64 * par:64 * par + 64, 128 * m: 128 * (m + 1)],
                        self.qF[64 * par:64 * par + 64, fc * c.T + s0: fc * c.T + s0 + wq],
                        start=True, stop=True),
                        reads=[self.kFp[fc].r(p0=64 * par, p1=64 * par + 64),
                               self.qF.r(p0=64 * par, p1=64 * par + 64)],
                        writes=[ps.r(f0=0, f1=wq)], tag=f"sim{fc}_{m}_{par}")
                    self.op("scalar", lambda e, ab=ab, ps=ps, m=m, co=co, wq=wq: e.activation(
                        ab[:, m * 256 + co: m * 256 + co + wq], ps[:, 0:wq], AF.Exp,
                        scale=0.125),
                        reads=[ps.r(f0=0, f1=wq)],
                        writes=[ab.r(f0=m * 256, f1=(m + 1) * 256)], tag=f"exp{fc}_{m}_{par}")
                    v0a, v1a = max(0, 128 * m - 64), min(c.T, 128 * m + 128)
                    v0b, v1b = 128 * m, min(c.T, 128 * m + 256)
                    for (r0, r1, v0, v1) in ((0, 64, v0a, v1a), (64, 128, v0b, v1b)):
                        for (z0, z1) in ((s0, v0), (v1, s1)):
                            if z1 > z0:
                                c0 = m * 256 + co + (z0 - s0)
                                c1 = c0 + (z1 - z0)
                                self.op("vector", lambda e, ab=ab, r0=r0, r1=r1, c0=c0, c1=c1: e.memset(
                                    ab[r0:r1, c0:c1], 0.0),
                                    reads=[], writes=[ab.r(p0=r0, p1=r1, f0=c0, f1=c1)],
                                    tag=f"zc{fc}_{m}_{par}_{r0}_{c0}")
            for par, ab in ((0, abufA), (1, abufB)):
                h = 2 * fc + par
                for qg in range(TT512):
                    ps_o = self.pp[4]
                    g0, g1 = qg * 512, (qg + 1) * 512
                    ms = [m for m in range(NM)
                          if min(c.T, 128 * m + 192) > g0 and max(0, 128 * m - 64) < g1]
                    self.op("tensor", lambda e, h=h, ps_o=ps_o: e.matmul(
                        ps_o[0:65, 0:512],
                        self.vT[:, h * 65:(h + 1) * 65], self.zb[:, 0:512],
                        start=True, stop=False),
                        reads=[self.vT.r(), self.zb.all()],
                        writes=[ps_o.r(p0=0, p1=65)], tag=f"ozz{h}_{qg}")
                    for im, m in enumerate(ms):
                        s0 = max(0, 128 * m - 64)
                        s1 = min(c.T, 128 * m + 192)
                        co = s0 - (128 * m - 64)
                        a0, a1 = max(s0, g0), min(s1, g1)
                        self.op("tensor", lambda e, m=m, h=h, ab=ab, im=im, ms=ms,
                                a0=a0, a1=a1, s0=s0, co=co, g0=g0, ps_o=ps_o: e.matmul(
                            ps_o[0:65, a0 - g0:a1 - g0],
                            self.vT[:, m * (c.H * 65) + h * 65: m * (c.H * 65) + (h + 1) * 65],
                            ab[:, m * 256 + co + (a0 - s0): m * 256 + co + (a1 - s0)],
                            start=False, stop=(im == len(ms) - 1)),
                            reads=[self.vT.r(), ab.r(f0=m * 256, f1=(m + 1) * 256)],
                            writes=[ps_o.r(p0=0, p1=65, f0=a0 - g0, f1=a1 - g0)],
                            tag=f"omm{h}_{qg}_{m}")
                    d_ = dsb[qg % 2]
                    self.op("vector", lambda e, ps_o=ps_o, d_=d_: e.tensor_copy(
                        d_[0:1, 0:512], ps_o[64:65, 0:512]),
                        reads=[ps_o.r(p0=64, p1=65)], writes=[d_.all()],
                        tag=f"dene{h}_{qg}")
                    self.op("tensor", lambda e, d_=d_: e.matmul(
                        self.pp[5][0:64, 0:512], self.onesrowf_s[0:1, 0:64],
                        d_[0:1, 0:512], start=True, stop=True),
                        reads=[d_.all()], writes=[self.pp[5].r(p0=0, p1=64)],
                        tag=f"denb{h}_{qg}")
                    r = rb[qg % 2]
                    self.op("vector", lambda e, r=r: e.reciprocal(r[0:64, 0:512],
                                                                  self.pp[5][0:64, 0:512]),
                            reads=[self.pp[5].r(p0=0, p1=64)], writes=[r.r(p0=0, p1=64)],
                            tag=f"rec{h}_{qg}")
                    hc2, hh = h // 2, h % 2
                    self.op("vector", lambda e, r=r, ps_o=ps_o, hc2=hc2, hh=hh, g0=g0: e.tensor_tensor(
                        self.oF[64 * hh:64 * hh + 64, hc2 * c.T + g0: hc2 * c.T + g0 + 512],
                        ps_o[0:64, 0:512], r[0:64, 0:512], ALU.mult),
                        reads=[ps_o.r(p0=0, p1=64), r.r(p0=0, p1=64)],
                        writes=[self.oF.r(p0=64 * hh, p1=64 * hh + 64,
                                          f0=hc2 * c.T + g0, f1=hc2 * c.T + g0 + 512)],
                        tag=f"oev{h}_{qg}")

    def light_out(self):
        c, p = self.c, self.p
        HC = c.HDH // 128
        self.dma(self.q_g, lambda e: e.dma_start(out=self.nullq_bf[0:1, :],
                                                 in_=self.nullq_in[0:1, :]),
                 reads=[self.nullq_in.r()], writes=[self.nullq_bf.all()], tag="nqbf")
        def emitnb(e):
            sap = mkap(self.nullq_in.ap(), 0, [(128, 0), (c.D, 1)])
            return e.dma_start(out=self.null_bc[:, :], in_=sap)
        self.dma(self.q_x, emitnb, reads=[self.nullq_in.r()],
                 writes=[self.null_bc.all()], tag="bc_nullbc")
        wtmp = self.wotmp
        for hc in range(HC):
            self.dma(self.q_w, lambda e, hc=hc: e.dma_start(
                out=wtmp[:, :], in_=self.woutl_in[ts(hc, 128), :]),
                reads=[self.woutl_in.r()], writes=[wtmp.all()], tag=f"ldwo{hc}")
            self.op("vector", lambda e, hc=hc: e.tensor_copy(
                self.woutl_bf[:, ts(hc, c.D)], wtmp[:, :]),
                reads=[wtmp.all()], writes=[self.woutl_bf.r(f0=hc * c.D, f1=(hc + 1) * c.D)],
                tag=f"woc{hc}")
        WD = min(512, c.D)
        for i in range(c.NT):
            x1 = self.ft[i % 3]
            o = self.ft2[i % 3]
            self.dma(self.q_x, lambda e, x1=x1, i=i: e.dma_start(
                out=x1[:, :], in_=self.x1_d[ts(i, 128), :]),
                reads=[self.x1_d.r(p0=i * 128, p1=(i + 1) * 128)], writes=[x1.all()],
                tag=f"ldx1f_{i}")
            for dh in range(c.D // WD):
                ps = self.pp[2 + (i % 2)]
                for hc in range(HC):
                    self.op("tensor", lambda e, ps=ps, i=i, dh=dh, hc=hc: e.matmul(
                        ps[:, 0:WD], self.oF[:, hc * c.T + i * 128: hc * c.T + (i + 1) * 128],
                        self.woutl_bf[:, hc * c.D + dh * WD: hc * c.D + (dh + 1) * WD],
                        start=(hc == 0), stop=False),
                        reads=[self.oF.r(f0=hc * c.T + i * 128, f1=hc * c.T + (i + 1) * 128),
                               self.woutl_bf.r(f0=hc * c.D + dh * WD, f1=hc * c.D + (dh + 1) * WD)],
                        writes=[ps.r()], tag=f"lo{i}_{dh}_{hc}")
                self.op("tensor", lambda e, ps=ps, i=i, dh=dh: e.matmul(
                    ps[:, 0:WD], self.onesrow_s[0:1, :],
                    self.nullq_bf[0:1, dh * WD:(dh + 1) * WD], start=False, stop=True),
                    reads=[self.onesrow_s.r(), self.nullq_bf.r(f0=dh * WD, f1=(dh + 1) * WD)],
                    writes=[ps.r()], tag=f"lon{i}_{dh}")
                self.op("vector", lambda e, ps=ps, x1=x1, o=o, dh=dh, i=i: e.tensor_add(
                    o[:, dh * WD:(dh + 1) * WD], x1[:, dh * WD:(dh + 1) * WD], ps[:, 0:WD]),
                    reads=[x1.all(), ps.r()], writes=[o.r(f0=dh * WD, f1=(dh + 1) * WD)],
                    tag=f"cmp{i}_{dh}")
            self.dma(self.q_x, lambda e, o=o, i=i: e.dma_start(
                out=self.out_d[ts(i, 128), :], in_=o[:, :]),
                reads=[o.all()], writes=[self.out_d.r(p0=i * 128, p1=(i + 1) * 128)],
                tag=f"stout{i}")


    def routing(self):
        c, p = self.c, self.p
        NT = c.NT
        RTB = [self.RF]
        def rsb(name, shape, dtype):
            es = {F32: 4, BF16: 2, I32: 4}[dtype]
            at = (RTB[0] + 31) // 32 * 32
            RTB[0] = at + shape[1] * es
            return p.sbuf(name, shape, dtype, at=at)
        rcol2 = self.rcol2
        self.slotbufs = {}
        for route, (pbuf, keff, NSEL, seld) in enumerate(
                [(self.pq, c.KEFFQ, c.NQ, self.iq_d),
                 (self.pkv, c.KEFFKV, c.NKV, self.ikv_d)]):
            logk = float(np.log(np.float32(keff)))
            s_ = rsb(f"rts_{route}", [128, NT], F32)
            b_ = rsb(f"rtb_{route}", [128, NT], F32)
            sb_ = rsb(f"rtsb_{route}", [128, NT], F32)
            e_ = rsb(f"rte_{route}", [128, NT], F32)
            col = rsb(f"rtcol{route}", [128, 1], F32)
            colm = rsb(f"rtcolm{route}", [128, 1], F32)
            row1 = rsb(f"rtrow1{route}", [1, 128], F32)
            sc1 = rsb(f"rtsc1{route}", [1, 2], F32)
            A = self.op
            A("vector", lambda e, s_=s_, pbuf=pbuf, route=route: e.scalar_tensor_tensor(
                s_[:, :], self.m2[:, :], rcol2[:, route:route + 1], pbuf[:, :],
                ALU.mult, ALU.subtract),
              reads=[self.m2.all(), rcol2.r(f0=route, f1=route + 1), pbuf.all()],
              writes=[s_.all()], tag=f"s0_{route}")
            A("vector", lambda e, s_=s_: e.tensor_mul(s_[:, :], s_[:, :], self.r2[:, :]),
              reads=[s_.all(), self.r2.all()], writes=[s_.all()], tag=f"s1_{route}")
            A("vector", lambda e, s_=s_: e.tensor_scalar_mul(s_[:, :], s_[:, :], -1.0),
              reads=[s_.all()], writes=[s_.all()], tag=f"s2_{route}")
            A("vector", lambda e, s_=s_, b_=b_: e.tensor_scalar_mul(b_[:, :], s_[:, :], -1.0),
              reads=[s_.all()], writes=[b_.all()], tag=f"binit{route}")
            acol = rsb(f"acol_{route}", [128, 1], F32)
            for it in range(c.NIT):
                t = (route, it)
                A("vector", lambda e, sb_=sb_, s_=s_, b_=b_: e.tensor_add(
                    sb_[:, :], s_[:, :], b_[:, :]),
                  reads=[s_.all(), b_.all()], writes=[sb_.all()], tag=f"cd_sb{t}")
                A("vector", lambda e, sb_=sb_, col=col: e.reduce_max(
                    col[:, 0:1], sb_[:, :], axis=AX.X),
                  reads=[sb_.all()], writes=[col.all()], tag=f"cd_m1{t}")
                A("tensor", lambda e, col=col: e.transpose(
                    self.pp[5][0:1, 0:128], col[:, 0:1], self.iden_s[:, :]),
                  reads=[col.all(), self.iden_s.all()],
                  writes=[self.pp[5].r(p0=0, p1=1)], tag=f"cd_tr{t}")
                A("vector", lambda e, row1=row1: e.tensor_copy(row1[0:1, :],
                                                               self.pp[5][0:1, 0:128]),
                  reads=[self.pp[5].r(p0=0, p1=1)], writes=[row1.all()], tag=f"cd_tre{t}")
                A("vector", lambda e, row1=row1, sc1=sc1: e.reduce_max(
                    sc1[0:1, 0:1], row1[0:1, :], axis=AX.X),
                  reads=[row1.all()], writes=[sc1.r(f0=0, f1=1)], tag=f"cd_m2{t}")
                A("tensor", lambda e, sc1=sc1: e.matmul(
                    self.pp[5][:, 2:3], self.onesrowf_s[0:1, :], sc1[0:1, 0:1],
                    start=True, stop=True),
                  reads=[sc1.r(f0=0, f1=1)], writes=[self.pp[5].r(f0=2, f1=3)],
                  tag=f"cd_mb{t}")
                A("vector", lambda e, colm=colm: e.tensor_scalar_mul(
                    colm[:, 0:1], self.pp[5][:, 2:3], -1.0),
                  reads=[self.pp[5].r(f0=2, f1=3)], writes=[colm.all()], tag=f"cd_mbe{t}")
                A("scalar", lambda e, e_=e_, sb_=sb_, colm=colm, col=col: e.activation(
                    e_[:, :], sb_[:, :], AF.Exp, bias=colm[:, 0:1],
                    accum_out=col[:, 0:1]),
                  reads=[sb_.all(), colm.all()], writes=[e_.all(), col.all()],
                  tag=f"cd_exp{t}")
                A("tensor", lambda e, col=col: e.transpose(
                    self.pp[5][0:1, 0:128], col[:, 0:1], self.iden_s[:, :]),
                  reads=[col.all(), self.iden_s.all()],
                  writes=[self.pp[5].r(p0=0, p1=1)], tag=f"cd_tr2{t}")
                A("vector", lambda e, row1=row1: e.tensor_copy(row1[0:1, :],
                                                               self.pp[5][0:1, 0:128]),
                  reads=[self.pp[5].r(p0=0, p1=1)], writes=[row1.all()], tag=f"cd_tr2e{t}")
                A("vector", lambda e, row1=row1, sc1=sc1: e.reduce_sum(
                    sc1[0:1, 1:2], row1[0:1, :], axis=AX.X),
                  reads=[row1.all()], writes=[sc1.r(f0=1, f1=2)], tag=f"cd_s2{t}")
                A("scalar", lambda e, sc1=sc1: e.activation(sc1[0:1, 1:2], sc1[0:1, 1:2],
                                                            AF.Ln),
                  reads=[sc1.r(f0=1, f1=2)], writes=[sc1.r(f0=1, f1=2)], tag=f"cd_ln{t}")
                A("vector", lambda e, sc1=sc1: e.tensor_add(
                    sc1[0:1, 1:2], sc1[0:1, 1:2], sc1[0:1, 0:1]),
                  reads=[sc1.all()], writes=[sc1.r(f0=1, f1=2)], tag=f"cd_lse{t}")
                A("vector", lambda e, sc1=sc1, logk=logk: e.tensor_scalar(
                    sc1[0:1, 1:2], sc1[0:1, 1:2], -1.0, logk, ALU.mult, ALU.add),
                  reads=[sc1.r(f0=1, f1=2)], writes=[sc1.r(f0=1, f1=2)], tag=f"cd_a{t}")
                A("tensor", lambda e, sc1=sc1: e.matmul(
                    self.pp[5][:, 2:3], self.onesrowf_s[0:1, :], sc1[0:1, 1:2],
                    start=True, stop=True),
                  reads=[sc1.r(f0=1, f1=2)], writes=[self.pp[5].r(f0=2, f1=3)],
                  tag=f"cd_ab{t}")
                A("vector", lambda e, acol=acol: e.tensor_copy(acol[:, 0:1],
                                                               self.pp[5][:, 2:3]),
                  reads=[self.pp[5].r(f0=2, f1=3)], writes=[acol.all()], tag=f"cd_abe{t}")
                A("vector", lambda e, sb_=sb_, s_=s_, acol=acol: e.tensor_scalar_add(
                    sb_[:, :], s_[:, :], acol[:, 0:1]),
                  reads=[s_.all(), acol.all()], writes=[sb_.all()], tag=f"cd_sa{t}")
                A("scalar", lambda e, b_=b_, sb_=sb_: e.activation(b_[:, :], sb_[:, :],
                                                                   AF.Relu),
                  reads=[sb_.all()], writes=[b_.all()], tag=f"cd_rl{t}")
                A("vector", lambda e, b_=b_: e.tensor_scalar_mul(b_[:, :], b_[:, :], -1.0),
                  reads=[b_.all()], writes=[b_.all()], tag=f"cd_ng{t}")
            mask = rsb(f"mask_{route}", [128, NT], F32)
            maskb = rsb(f"maskb_{route}", [128, NT], BF16)
            A("vector", lambda e, mask=mask, sb_=sb_: e.tensor_scalar(
                mask[:, :], sb_[:, :], 0.0, None, ALU.is_ge),
              reads=[sb_.all()], writes=[mask.all()], tag=f"msk{route}")
            A("vector", lambda e, mask=mask, maskb=maskb: e.tensor_copy(
                maskb[:, :], mask[:, :]),
              reads=[mask.all()], writes=[maskb.all()], tag=f"mskb{route}")
            rkps = self.pp[5]
            A("tensor", lambda e, maskb=maskb, rkps=rkps: e.matmul(
                rkps[:, 0:NT], self.tri_s[:, :], maskb[:, :], start=True, stop=False),
              reads=[self.tri_s.all(), maskb.all()], writes=[rkps.r(f0=0, f1=NT)],
              tag=f"trimm{route}")
            cnt = rsb(f"cnt_{route}", [1, NT], F32)
            A("tensor", lambda e, maskb=maskb: e.matmul(
                self.pp[4][0:1, 0:NT], self.ones_s[0:128, 0:1], maskb[:, :],
                start=True, stop=True),
              reads=[self.ones_s.r(), maskb.all()], writes=[self.pp[4].r(p0=0, p1=1)],
              tag=f"cntmm{route}")
            A("vector", lambda e, cnt=cnt: e.tensor_copy(cnt[0:1, :],
                                                         self.pp[4][0:1, 0:NT]),
              reads=[self.pp[4].r(p0=0, p1=1)], writes=[cnt.all()], tag=f"cnte{route}")
            pref = rsb(f"pref_{route}", [1, NT], F32)
            A("vector", lambda e, pref=pref: e.memset(pref[0:1, 0:1], 0.0),
              reads=[], writes=[pref.r(f0=0, f1=1)], tag=f"pz{route}")
            A("vector", lambda e, pref=pref, cnt=cnt: e.tensor_copy(
                pref[0:1, 1:NT], cnt[0:1, 0:NT - 1]),
              reads=[cnt.all()], writes=[pref.r(f0=1, f1=NT)], tag=f"psh{route}")
            sh = 1
            while sh < NT:
                A("vector", lambda e, pref=pref, sh=sh: e.tensor_add(
                    pref[0:1, sh:NT], pref[0:1, sh:NT], pref[0:1, 0:NT - sh]),
                  reads=[pref.all()], writes=[pref.r(f0=sh, f1=NT)], tag=f"pd{route}_{sh}")
                sh *= 2
            A("tensor", lambda e, pref=pref, rkps=rkps: e.matmul(
                rkps[:, 0:NT], self.onesrowf_s[0:1, :], pref[0:1, :],
                start=False, stop=True),
              reads=[pref.all()], writes=[rkps.r(f0=0, f1=NT)], tag=f"prbc{route}")
            rank = rsb(f"rank_{route}", [128, NT], F32)
            A("vector", lambda e, rank=rank, rkps=rkps: e.tensor_copy(rank[:, :],
                                                                      rkps[:, 0:NT]),
              reads=[rkps.r(f0=0, f1=NT)], writes=[rank.all()], tag=f"rke{route}")
            slotf = rsb(f"slotf_{route}", [128, NT], F32)
            sloti = rsb(f"sloti_{route}", [128, NT], I32)
            A("vector", lambda e, slotf=slotf, mask=mask, rank=rank: e.scalar_tensor_tensor(
                slotf[:, :], mask[:, :], -1e7, rank[:, :], ALU.mult, ALU.add),
              reads=[mask.all(), rank.all()], writes=[slotf.all()], tag=f"sl1{route}")
            A("vector", lambda e, slotf=slotf: e.tensor_scalar_add(slotf[:, :],
                                                                   slotf[:, :], 1e7),
              reads=[slotf.all()], writes=[slotf.all()], tag=f"sl2{route}")
            A("vector", lambda e, slotf=slotf, NSEL=NSEL: e.tensor_scalar(
                slotf[:, :], slotf[:, :], float(NSEL + 127), None, ALU.min),
              reads=[slotf.all()], writes=[slotf.all()], tag=f"slc{route}")
            A("vector", lambda e, sloti=sloti, slotf=slotf: e.tensor_copy(
                sloti[:, :], slotf[:, :]),
              reads=[slotf.all()], writes=[sloti.all()], tag=f"sl3{route}")
            tok_s = rsb(f"tok_s{route}", [128, NT], I32)
            self.dma(self.q_g, lambda e, tok_s=tok_s: e.dma_start(
                out=tok_s[:, :], in_=self.tokid_in[:, :]),
                reads=[self.tokid_in.r()], writes=[tok_s.all()], tag=f"tok{route}")
            zi = rsb(f"zi_{route}", [128, (NSEL + 128) // 128], I32)
            A("gpsimd", lambda e, zi=zi: e.memset(zi[:, :], 0), reads=[],
              writes=[zi.all()], tag=f"zi{route}")
            def emit_z(e, zi=zi, seld=seld, NSEL=NSEL):
                dst = mkap(seld.ap(), 0, [(128, (NSEL + 128) // 128), ((NSEL + 128) // 128, 1)])
                return e.dma_start(out=dst, in_=zi[:, :])
            self.dma(self.q_g, emit_z, reads=[zi.all()], writes=[seld.r()],
                     tag=f"zid{route}")
            for f in range(NT):
                def emit_sc(e, sloti=sloti, tok_s=tok_s, f=f, seld=seld, NSEL=NSEL):
                    return e.indirect_dma_start(
                        out=seld.ap().rearrange("a b -> (a b)").unsqueeze(1) if False
                        else seld[0:1, :].transpose() if False else
                        mkap(seld.ap(), 0, [(NSEL + 128, 1), (1, 1)]),
                        out_offset=IndirectOffsetOnAxis(ap=sloti[:, f:f + 1], axis=0),
                        in_=tok_s[:, f:f + 1], in_offset=None)
                self.dma(self.q_g, emit_sc, reads=[sloti.r(f0=f, f1=f + 1),
                                                   tok_s.r(f0=f, f1=f + 1)],
                         writes=[seld.r()], tag=f"scat{route}_{f}")

    def heavy(self):
        c, p = self.c, self.p
        A = self.op
        NQT = c.NQ // 128
        NKT = c.NKV // 128
        HC = c.HDH // 128
        wt = self.hwt
        for (win, dstb, W_) in ((self.wqh_in, self.wqh_bf, c.HDH),
                                (self.wkvh_in, self.wkvh_bf, c.H * 128)):
            for dc in range(c.DC):
                w_ = wt[dc % 2]
                self.dma(self.q_w, lambda e, w_=w_, dc=dc, win=win, W_=W_: e.dma_start(
                    out=w_[:, 0:W_], in_=win[ts(dc, 128), :]),
                    reads=[win.r()], writes=[w_.all()], tag=f"ldw_{win.name}_{dc}")
                A("vector", lambda e, w_=w_, dstb=dstb, dc=dc, W_=W_: e.tensor_scalar_mul(
                    dstb[:, dc * W_:(dc + 1) * W_], w_[:, 0:W_], self.rmgc[:, dc:dc + 1]),
                  reads=[w_.all(), self.rmgc.r(f0=dc, f1=dc + 1)],
                  writes=[dstb.r(f0=dc * W_, f1=(dc + 1) * W_)], tag=f"wfold_{win.name}_{dc}")
        idxq = [p.sbuf(f"idxq{k}", [128, 1], I32, at=self.BASE + 9216 + k * 32)
                for k in range(NQT)]
        idxk = [p.sbuf(f"idxk{k}", [128, 1], I32,
                       at=self.BASE + 9216 + NQT * 32 + k * 32)
                for k in range(NKT)]
        rq = p.sbuf("rq_h", [128, NQT + NKT], F32,
                    at=self.BASE + 9216 + (NQT + NKT) * 32)
        SQD = float(np.sqrt(np.float32(c.D)))

        def gather_side(N_, seld, idxs, srcb, dstb, NN, off):
            for k in range(N_):
                ix = idxs[k]
                def emit_ld(e, ix=ix, k=k, seld=seld):
                    src = mkap(seld.ap(), k * 128, [(128, 1), (1, 1)])
                    return e.dma_start(out=ix[:, :], in_=src)
                self.dma(self.q_g, emit_ld, reads=[seld.r()], writes=[ix.all()],
                         tag=f"ldix_{seld.name}_{k}")
                def emit_g(e, ix=ix, k=k, srcb=srcb):
                    return e.indirect_dma_start(
                        out=srcb[:, ts(k, c.D)], out_offset=None,
                        in_=self.xn0_d[:, :],
                        in_offset=IndirectOffsetOnAxis(ap=ix[:, 0:1], axis=0))
                self.dma(self.q_g, emit_g, reads=[ix.all(), self.xn0_d.r()],
                         writes=[srcb.r(f0=k * c.D, f1=(k + 1) * c.D)],
                         tag=f"gath_{srcb.name}_{k}")
                kk = off + k
                A("scalar", lambda e, srcb=srcb, k=k, kk=kk: e.activation(
                    self.junk[:, 0:min(c.D, 1024)],
                    srcb[:, k * c.D:k * c.D + min(c.D, 1024)],
                    AF.Square, accum_out=rq[:, kk:kk + 1]),
                  reads=[srcb.r(f0=k * c.D, f1=(k + 1) * c.D)],
                  writes=[self.junk.r(), rq.r(f0=kk, f1=kk + 1)], tag=f"hsq{kk}")
            o0, o1 = off, off + N_
            A("scalar", lambda e: e.activation(rq[:, o0:o1], rq[:, o0:o1], AF.Ln),
              reads=[rq.r(f0=o0, f1=o1)], writes=[rq.r(f0=o0, f1=o1)], tag=f"hln{off}")
            A("scalar", lambda e: e.activation(rq[:, o0:o1], rq[:, o0:o1], AF.Exp,
                                               scale=0.5),
              reads=[rq.r(f0=o0, f1=o1)], writes=[rq.r(f0=o0, f1=o1)], tag=f"hsqr{off}")
            A("vector", lambda e: e.tensor_scalar_max(rq[:, o0:o1], rq[:, o0:o1], 1e-12),
              reads=[rq.r(f0=o0, f1=o1)], writes=[rq.r(f0=o0, f1=o1)], tag=f"hmx{off}")
            A("vector", lambda e: e.reciprocal(rq[:, o0:o1], rq[:, o0:o1]),
              reads=[rq.r(f0=o0, f1=o1)], writes=[rq.r(f0=o0, f1=o1)], tag=f"hrc{off}")
            A("vector", lambda e: e.tensor_scalar_mul(rq[:, o0:o1], rq[:, o0:o1], SQD),
              reads=[rq.r(f0=o0, f1=o1)], writes=[rq.r(f0=o0, f1=o1)], tag=f"hsd{off}")
            for k in range(N_):
                kk = off + k
                A("vector", lambda e, srcb=srcb, k=k, kk=kk: e.tensor_scalar_mul(
                    srcb[:, ts(k, c.D)], srcb[:, ts(k, c.D)], rq[:, kk:kk + 1]),
                  reads=[srcb.r(f0=k * c.D, f1=(k + 1) * c.D), rq.r(f0=kk, f1=kk + 1)],
                  writes=[srcb.r(f0=k * c.D, f1=(k + 1) * c.D)], tag=f"hnm{kk}")
            for k in range(N_):
                def emit_tr(e, srcb=srcb, dstb=dstb, k=k, NN=NN):
                    out = mkap(dstb.ap(), k * 128,
                               [(128, c.DC * NN), (c.DC, NN), (128, 1)])
                    inp = mkap(srcb.ap(), k * c.D,
                               [(128, srcb.shape[1]), (c.DC, 128), (128, 1)])
                    return e.dma_start_transpose(out, inp)
                self.dma(self.q_w, emit_tr,
                         reads=[srcb.r(f0=k * c.D, f1=(k + 1) * c.D)],
                         writes=[dstb.r(f0=dc * NN + k * 128, f1=dc * NN + (k + 1) * 128)
                                 for dc in range(c.DC)],
                         tag=f"htr_{dstb.name}_{k}")

        gather_side(NQT, self.iq_d, idxq, self.xqT, self.xqF, c.NQ, 0)
        WQT = min(512, c.NQ)
        for hc in range(HC):
            for tq in range(c.NQ // WQT):
                ps = self.pp[0]
                for dc in range(c.DC):
                    A("tensor", lambda e, ps=ps, hc=hc, tq=tq, dc=dc: e.matmul(
                        ps[:, 0:WQT],
                        self.wqh_bf[:, dc * c.HDH + hc * 128: dc * c.HDH + (hc + 1) * 128],
                        self.xqF[:, dc * c.NQ + tq * WQT: dc * c.NQ + (tq + 1) * WQT],
                        start=(dc == 0), stop=(dc == c.DC - 1)),
                      reads=[self.wqh_bf.r(f0=dc * c.HDH + hc * 128,
                                           f1=dc * c.HDH + (hc + 1) * 128),
                             self.xqF.r(f0=dc * c.NQ + tq * WQT,
                                        f1=dc * c.NQ + (tq + 1) * WQT)],
                      writes=[ps.r()], tag=f"qh{hc}_{tq}_{dc}")
                A("vector", lambda e, ps=ps, hc=hc, tq=tq: e.tensor_copy(
                    self.qhF[:, hc * c.NQ + tq * WQT: hc * c.NQ + (tq + 1) * WQT],
                    ps[:, 0:WQT]),
                  reads=[ps.r()],
                  writes=[self.qhF.r(f0=hc * c.NQ + tq * WQT, f1=hc * c.NQ + (tq + 1) * WQT)],
                  tag=f"qhe{hc}_{tq}")
        gather_side(NKT, self.ikv_d, idxk, self.xkT, self.xkF, c.NKV, NQT)
        A("gpsimd", lambda e: e.memset(self.kvF[:, :], 0.0), reads=[],
          writes=[self.kvF.all()], tag="kvz")
        WKT = min(512, c.NKV)
        for h in range(c.H):
            for tq in range(c.NKV // WKT):
                ps = self.pp[1]
                for dc in range(c.DC):
                    A("tensor", lambda e, ps=ps, h=h, tq=tq, dc=dc: e.matmul(
                        ps[:, 0:WKT],
                        self.wkvh_bf[:, dc * c.H * 128 + h * 128: dc * c.H * 128 + (h + 1) * 128],
                        self.xkF[:, dc * c.NKV + tq * WKT: dc * c.NKV + (tq + 1) * WKT],
                        start=(dc == 0), stop=(dc == c.DC - 1)),
                      reads=[self.wkvh_bf.r(f0=dc * c.H * 128 + h * 128,
                                            f1=dc * c.H * 128 + (h + 1) * 128),
                             self.xkF.r(f0=dc * c.NKV + tq * WKT,
                                        f1=dc * c.NKV + (tq + 1) * WKT)],
                      writes=[ps.r()], tag=f"kvh{h}_{tq}_{dc}")
                A("vector", lambda e, ps=ps, h=h, tq=tq: e.tensor_copy(
                    self.kvF[:, h * c.CTX + tq * WKT: h * c.CTX + (tq + 1) * WKT],
                    ps[:, 0:WKT]),
                  reads=[ps.r()],
                  writes=[self.kvF.r(f0=h * c.CTX + tq * WKT, f1=h * c.CTX + (tq + 1) * WKT)],
                  tag=f"kvhe{h}_{tq}")
        for h in range(c.H):
            def emit_nk(e, h=h):
                dst = mkap(self.kvF.ap(),
                           64 * (h % 2) * self.kvF.shape[1] + h * c.CTX + c.CTX - 1,
                           [(64, self.kvF.shape[1]), (1, 1)])
                src = mkap(self.nullkv_in.ap(), h * c.DH, [(64, 1), (1, 1)])
                return e.dma_start(out=dst, in_=src)
            self.dma(self.q_g, emit_nk, reads=[self.nullkv_in.r()],
                     writes=[self.kvF.r(p0=64 * (h % 2), p1=64 * (h % 2) + 64,
                                        f0=h * c.CTX + c.CTX - 1,
                                        f1=h * c.CTX + c.CTX)], tag=f"nk{h}")
        WVH = c.H * 65
        A("gpsimd", lambda e: e.memset(self.vTh[:, :], 0.0), reads=[],
          writes=[self.vTh.all()], tag="vthz")
        def emit_ones(e):
            dst = mkap(self.vTh.ap(), 64, [(128, c.NTC * WVH), (c.NTC, WVH), (c.H, 65)])
            return e.memset(dst, 1.0)
        A("vector", emit_ones, reads=[], writes=[self.vTh.r()], tag="vth_ones")
        for h in range(c.H):
            vh = 64 * (1 - h % 2)
            for tc in range(c.NKV // 128):
                ps = (self.ppb, self.ppb2)[tc % 2]
                self.op("tensor", lambda e, h=h, tc=tc, ps=ps: e.transpose(
                    ps[:, 0:128],
                    self.kvF[:, h * c.CTX + tc * 128: h * c.CTX + (tc + 1) * 128],
                    self.idenb_s[:, :]),
                    reads=[self.kvF.r(f0=h * c.CTX + tc * 128,
                                      f1=h * c.CTX + (tc + 1) * 128),
                           self.idenb_s.all()],
                    writes=[ps.r(f0=0, f1=128)], tag=f"vthtr{h}_{tc}")
                co = tc * WVH + h * 65
                self.op("vector", lambda e, co=co, vh=vh, ps=ps: e.tensor_copy(
                    self.vTh[:, co:co + 64], ps[:, vh: vh + 64]),
                    reads=[ps.r(f0=vh, f1=vh + 64)],
                    writes=[self.vTh.r(f0=co, f1=co + 64)], tag=f"vthev{h}_{tc}")
        def emit_nv(e):
            dst = mkap(self.vTh.ap(), 127 * self.vTh.shape[1] + (c.NTC - 1) * WVH,
                       [(1, self.vTh.shape[1]), (c.H, 65), (64, 1)])
            src = mkap(self.nullkv_in.ap(), c.H * c.DH, [(1, 1), (c.H, 64), (64, 1)])
            return e.dma_start(out=dst, in_=src)
        self.dma(self.q_g, emit_nv, reads=[self.nullkv_in.r()],
                 writes=[self.vTh.r(p0=127, p1=128,
                                    f0=(c.NTC - 1) * WVH,
                                    f1=(c.NTC - 1) * WVH + c.H * 65)],
                 tag="nv")
        ab = self.hab
        rbh = self.rbh
        dsbh = self.dsbh
        NPAD0 = c.NKV % 128
        for h in range(c.H):
            for qg in range(c.NQ // WQT):
                ps_o = self.pp[2 + (qg % 2)]
                for tc in range(c.NTC):
                    pss = self.pp[4 + (tc % 2)]
                    a_ = ab[tc % 3]
                    hp = 64 * (h % 2)
                    A("tensor", lambda e, h=h, qg=qg, tc=tc, pss=pss, hp=hp: e.matmul(
                        pss[:, 0:WQT],
                        self.kvF[hp:hp + 64, h * c.CTX + tc * 128: h * c.CTX + (tc + 1) * 128],
                        self.qhF[hp:hp + 64,
                                 (h // 2) * c.NQ + qg * WQT: (h // 2) * c.NQ + (qg + 1) * WQT],
                        start=True, stop=True),
                      reads=[self.kvF.r(p0=hp, p1=hp + 64), self.qhF.r(p0=hp, p1=hp + 64)],
                      writes=[pss.r()], tag=f"hsim{h}_{qg}_{tc}")
                    A("scalar", lambda e, a_=a_, pss=pss: e.activation(
                        a_[:, 0:WQT], pss[:, 0:WQT], AF.Exp, scale=0.125),
                      reads=[pss.r()], writes=[a_.all()], tag=f"hexp{h}_{qg}_{tc}")
                    if tc == c.NTC - 1:
                        z0, z1 = NPAD0, 127
                        for (r0, r1) in ((z0, min(64, z1)), (max(64, z0), z1)):
                            if r1 > r0:
                                A("vector", lambda e, a_=a_, r0=r0, r1=r1: e.memset(
                                    a_[r0:r1, 0:WQT], 0.0),
                                  reads=[], writes=[a_.r(p0=r0, p1=r1)],
                                  tag=f"hzp{h}_{qg}_{r0}")
                    A("tensor", lambda e, a_=a_, h=h, tc=tc, ps_o=ps_o: e.matmul(
                        ps_o[0:65, 0:WQT],
                        self.vTh[:, tc * WVH + h * 65: tc * WVH + (h + 1) * 65],
                        a_[:, 0:WQT], start=(tc == 0), stop=(tc == c.NTC - 1)),
                      reads=[self.vTh.r(), a_.all()],
                      writes=[ps_o.r(p0=0, p1=65)], tag=f"homm{h}_{qg}_{tc}")
                A("vector", lambda e, ps_o=ps_o: e.tensor_copy(
                    dsbh[0:1, 0:WQT], ps_o[64:65, 0:WQT]),
                  reads=[ps_o.r(p0=64, p1=65)], writes=[dsbh.all()], tag=f"hde{h}_{qg}")
                A("tensor", lambda e: e.matmul(
                    self.pp[1][0:64, 0:WQT], self.onesrowf_s[0:1, 0:64],
                    dsbh[0:1, 0:WQT], start=True, stop=True),
                  reads=[dsbh.all()], writes=[self.pp[1].r(p0=0, p1=64)],
                  tag=f"hdb{h}_{qg}")
                A("vector", lambda e: e.reciprocal(rbh[0:64, 0:WQT], self.pp[1][0:64, 0:WQT]),
                  reads=[self.pp[1].r(p0=0, p1=64)], writes=[rbh.r(p0=0, p1=64)],
                  tag=f"hrc{h}_{qg}")
                hc2, hh = h // 2, h % 2
                A("vector", lambda e, ps_o=ps_o, hc2=hc2, hh=hh, qg=qg: e.tensor_tensor(
                    self.oHF[64 * hh:64 * hh + 64,
                             hc2 * c.NQ + qg * WQT: hc2 * c.NQ + (qg + 1) * WQT],
                    ps_o[0:64, 0:WQT], rbh[0:64, 0:WQT], ALU.mult),
                  reads=[ps_o.r(p0=0, p1=64), rbh.r(p0=0, p1=64)],
                  writes=[self.oHF.r(p0=64 * hh, p1=64 * hh + 64,
                                     f0=hc2 * c.NQ + qg * WQT,
                                     f1=hc2 * c.NQ + (qg + 1) * WQT)],
                  tag=f"hoev{h}_{qg}")
        wtmp2 = self.wotmp
        for hc in range(HC):
            self.dma(self.q_w, lambda e, hc=hc: e.dma_start(
                out=wtmp2[:, :], in_=self.wouth_in[ts(hc, 128), :]),
                reads=[self.wouth_in.r()], writes=[wtmp2.all()], tag=f"ldwoh{hc}")
            A("vector", lambda e, hc=hc: e.tensor_copy(
                self.wouth_bf[:, ts(hc, c.D)], wtmp2[:, :]),
              reads=[wtmp2.all()], writes=[self.wouth_bf.r(f0=hc * c.D, f1=(hc + 1) * c.D)],
              tag=f"wohc{hc}")
        rf = self.rfx
        WD = min(512, c.D)
        for k in range(NQT):
            o = rf[k % 2]
            for dh in range(c.D // WD):
                ps = self.pp[0]
                for hc in range(HC):
                    A("tensor", lambda e, ps=ps, k=k, dh=dh, hc=hc: e.matmul(
                        ps[:, 0:WD], self.oHF[:, hc * c.NQ + k * 128: hc * c.NQ + (k + 1) * 128],
                        self.wouth_bf[:, hc * c.D + dh * WD: hc * c.D + (dh + 1) * WD],
                        start=(hc == 0), stop=(hc == HC - 1)),
                      reads=[self.oHF.r(f0=hc * c.NQ + k * 128, f1=hc * c.NQ + (k + 1) * 128),
                             self.wouth_bf.r(f0=hc * c.D + dh * WD,
                                             f1=hc * c.D + (dh + 1) * WD)],
                      writes=[ps.r()], tag=f"hop{k}_{dh}_{hc}")
                A("vector", lambda e, ps=ps, o=o, dh=dh: e.tensor_sub(
                    o[:, dh * WD:(dh + 1) * WD], ps[:, 0:WD],
                    self.null_bc[:, dh * WD:(dh + 1) * WD]),
                  reads=[ps.r(), self.null_bc.all()],
                  writes=[o.r(f0=dh * WD, f1=(dh + 1) * WD)], tag=f"hfx{k}_{dh}")
            def emit_sc(e, o=o, k=k):
                return e.indirect_dma_start(
                    out=self.out_d[:, :],
                    out_offset=IndirectOffsetOnAxis(ap=idxq[k][:, 0:1], axis=0),
                    in_=o[:, :], in_offset=None,
                    compute_op=ALU.add)
            self.dma(self.q_g, emit_sc, reads=[o.all(), idxq[k].all(), self.out_d.r()],
                     writes=[self.out_d.r()], tag=f"hscat{k}")
    def finalize(self):
        self.p.resolve_and_emit()


import ml_dtypes

BF = ml_dtypes.bfloat16


def prep_consts(c):
    out = {}
    out["jcolU"] = (127 - np.arange(128, dtype=np.float32))[:, None]
    out["lramp"] = np.tile(np.arange(1, c.Q + 1, dtype=np.float32)[None, :], (128, 1))
    tid = (np.arange(c.T // 128)[None, :] * 128 + np.arange(128)[:, None])
    out["tokid"] = tid.astype(np.int32)
    out["tri"] = np.triu(np.ones((128, 128), np.float32), 1).astype(BF)
    out["onesc"] = np.ones((128, 2), BF)
    out["onesrow"] = np.ones((1, 128), BF)
    out["onesrowf"] = np.ones((1, 128), np.float32)
    out["iden"] = np.eye(128, dtype=np.float32)
    out["idenb"] = np.eye(128, dtype=np.float32).astype(BF)
    return out


def prep_core_inputs(c, x_b, w):
    d = {}
    d["x"] = np.ascontiguousarray(x_b)
    logA = w["L0_attn_log_A"]
    d["logA_nat"] = np.ascontiguousarray(
        logA.reshape(c.DC, 128, c.NS).transpose(1, 0, 2).reshape(128, c.DC * c.NS))
    d["logA_sc"] = np.ascontiguousarray(logA.T)
    d["B_sc"] = np.ascontiguousarray(w["L0_attn_B_param"].T)
    d["C_sc"] = np.ascontiguousarray(w["L0_attn_C"].T)
    d["Dvec"] = np.ascontiguousarray(w["L0_attn_D"][None, :])
    d["ln_g"] = np.ascontiguousarray(w["L0_ffn_ln_g"][None, :])
    d["ln_b"] = np.ascontiguousarray(w["L0_ffn_ln_b"][None, :])
    d["wqkv"] = np.ascontiguousarray(w["L0_ffn_w_qkv_l"])
    d["woutl"] = np.ascontiguousarray(w["L0_ffn_w_out_l"])
    d["nullq"] = np.ascontiguousarray(w["L0_ffn_null_q"][None, :])
    d["rt"] = np.ascontiguousarray(
        np.concatenate([w["L0_ffn_rt_q"], w["L0_ffn_rt_kv"]], axis=0))
    d["wqh"] = np.ascontiguousarray(w["L0_ffn_w_q_h"])
    wkv = w["L0_ffn_w_kv_h"]
    blocks = []
    for h in range(c.H):
        kb_ = wkv[:, h, :64]
        vb_ = wkv[:, h, 64:]
        if h % 2 == 0:
            blocks.append(np.concatenate([kb_, vb_], axis=1))
        else:
            blocks.append(np.concatenate([vb_, kb_], axis=1))
    d["wkvh"] = np.ascontiguousarray(np.concatenate(blocks, axis=1))
    d["nullkv"] = np.ascontiguousarray(w["L0_ffn_null_kv"].reshape(2, c.H * c.DH))
    d["wouth"] = np.ascontiguousarray(w["L0_ffn_w_out_h"])
    d["ln_g_col"] = np.ascontiguousarray(w["L0_ffn_ln_g"].reshape(c.DC, 128).T)
    d["ln_b_col"] = np.ascontiguousarray(w["L0_ffn_ln_b"].reshape(c.DC, 128).T.astype(BF))
    d["rms_g_col"] = np.ascontiguousarray(w["L0_ffn_rms_g"].reshape(c.DC, 128).T)
    d.update(prep_consts(c))
    return d


def _build_program():
    cfg = Cfg(mini=False)
    nc = bass.Bass("TRN2", target_bir_lowering=False, debug=False)
    kb = KB(cfg, nc, dbg=[])
    kb.declare()
    kb.bcast_rows()
    kb.phase_A()
    kb.phase_A2()
    kb.ssm_prep()
    kb.ssm_loop()
    kb.phase_C1()
    kb.phase_C2()
    kb.light_prep()
    kb.light_qkv()
    kb.light_vT()
    kb.light_attn()
    kb.routing()
    kb.light_out()
    kb.heavy()
    kb.finalize()
    return cfg, nc


def kernel(**inputs):
    from concourse.bass_utils import run_bass_kernel_spmd
    cfg, nc = _build_program()
    x = np.asarray(inputs["L0_attn_x"])
    w = {k: np.asarray(v) for k, v in inputs.items()}
    base = prep_core_inputs(cfg, x[0], w)
    in_maps = []
    for b in range(8):
        m = dict(base)
        m["x"] = np.ascontiguousarray(x[b])
        in_maps.append(m)
    res = run_bass_kernel_spmd(nc, in_maps, core_ids=list(range(8)))
    out = np.stack([res.results[b]["out"] for b in range(8)], axis=0)
    return out.astype(np.float32)


# version 05
# speedup vs baseline: 1.2049x; 1.2049x over previous
import concourse.bass as bass
import concourse.mybir as mybir

ENGINES = ("tensor", "vector", "scalar", "gpsimd", "sync")
DMA_RING = 10


class Region:
    __slots__ = ("buf", "p0", "p1", "f0", "f1", "tname")

    def __init__(self, buf, p0, p1, f0, f1, tname=None):
        self.buf = buf
        self.p0, self.p1 = p0, p1
        self.f0, self.f1 = f0, f1
        self.tname = tname

    def overlaps(self, o):
        return (self.buf == o.buf and self.p0 < o.p1 and o.p0 < self.p1
                and self.f0 < o.f1 and o.f0 < self.f1)


class Buf:

    def __init__(self, name, handle, shape, addr=None, esize=4, space=None):
        self.name = name
        self.h = handle
        self.shape = shape
        self.addr = addr
        self.esize = esize
        self.space = space or name

    def __getitem__(self, key):
        return self.ap()[key]

    def ap(self):
        return self.h.ap() if hasattr(self.h, "ap") else self.h

    def r(self, p0=0, p1=None, f0=0, f1=None):
        if p1 is None:
            p1 = self.shape[0]
        if f1 is None:
            f1 = self.shape[-1] if len(self.shape) == 2 else 1 << 40
        if self.addr is not None:
            return Region(self.space, p0, p1,
                          self.addr + f0 * self.esize, self.addr + f1 * self.esize,
                          tname=self.name)
        return Region(self.space, p0, p1, f0, f1, tname=self.name)

    def all(self):
        return self.r()


class Op:
    __slots__ = ("idx", "engine", "emit", "reads", "writes", "is_dma",
                 "queue", "tag", "inc_needed", "dma_sem", "dma_val")

    def __init__(self, idx, engine, emit, reads, writes, is_dma, queue, tag):
        self.idx = idx
        self.engine = engine
        self.emit = emit
        self.reads = reads
        self.writes = writes
        self.is_dma = is_dma
        self.queue = queue
        self.tag = tag
        self.inc_needed = False
        self.dma_sem = None
        self.dma_val = None


class Prog:
    def __init__(self, nc: bass.Bass):
        self.nc = nc
        self.ops = []
        self.bufs = {}
        self._anon = 0

    def sbuf(self, name, shape, dtype, at=None):
        es = mybir.dt.size(dtype)
        if at is not None:
            h = self.nc.alloc_sbuf_tensor_at(name, shape, dtype, offset=at)
            addr = at
        else:
            h = self.nc.alloc_sbuf_tensor(name, shape, dtype)
            addr = None
        b = Buf(name, h, shape, addr=addr, esize=es,
                space=("SB" if at is not None else name))
        self.bufs[name] = b
        return b

    def psum(self, name, shape, dtype=mybir.dt.float32):
        h = self.nc.alloc_psum_tensor(name, shape, dtype)
        b = Buf(name, h, shape, esize=mybir.dt.size(dtype))
        self.bufs[name] = b
        return b

    def dram(self, name, shape, dtype, kind="Internal"):
        h = self.nc.dram_tensor(name, shape, dtype, kind=kind)
        b = Buf(name, h, list(shape))
        self.bufs[name] = b
        return b

    def op(self, engine, emit, reads=(), writes=(), tag=""):
        o = Op(len(self.ops), engine, emit, list(reads), list(writes),
               False, None, tag)
        self.ops.append(o)
        return o

    def dma(self, queue, emit, reads=(), writes=(), tag=""):
        o = Op(len(self.ops), queue, emit, list(reads), list(writes),
               True, queue, tag)
        self.ops.append(o)
        return o

    def _find_deps(self):
        BLK = 4096
        hist = {}

        def keys(rg):
            if rg.buf == "SB":
                b0 = max(0, int(rg.f0)) // BLK
                b1 = max(0, int(min(rg.f1, 1 << 40)) - 1) // BLK
                return [("SB", b) for b in range(b0, b1 + 1)]
            return [(rg.buf,)]

        deps = [set() for _ in self.ops]
        for o in self.ops:
            dd = deps[o.idx]
            for rg, want_reads in [(r_, False) for r_ in o.reads] +                                   [(w_, True) for w_ in o.writes]:
                for k in keys(rg):
                    lst = hist.get(k)
                    if not lst:
                        continue
                    for (pi, kind, prg) in reversed(lst):
                        if not rg.overlaps(prg):
                            continue
                        if kind == "w":
                            dd.add(pi)
                            if (prg.p0 <= rg.p0 and prg.p1 >= rg.p1 and
                                    prg.f0 <= rg.f0 and prg.f1 >= rg.f1):
                                break
                        elif want_reads:
                            dd.add(pi)
            for rg in o.reads:
                for k in keys(rg):
                    hist.setdefault(k, []).append((o.idx, "r", rg))
            for rg in o.writes:
                for k in keys(rg):
                    hist.setdefault(k, []).append((o.idx, "w", rg))
        for i, ds in enumerate(deps):
            by_eng = {}
            for d in ds:
                pp = self.ops[d]
                if pp.is_dma:
                    continue
                if pp.engine not in by_eng or by_eng[pp.engine] < d:
                    by_eng[pp.engine] = d
            pruned = set(d for d in ds if self.ops[d].is_dma)
            pruned.update(by_eng.values())
            deps[i] = pruned
        return deps

    def resolve_and_emit(self, final_waits=True):
        nc = self.nc
        deps = self._find_deps()

        for i, ds in enumerate(deps):
            for d in ds:
                if not self.ops[d].is_dma:
                    self.ops[d].inc_needed = True
        qcount = {}
        for o in self.ops:
            if o.is_dma:
                k = qcount.get(o.queue, 0)
                qcount[o.queue] = k + 1
                slot = k % DMA_RING
                use = k // DMA_RING
                o.dma_sem = (o.queue, slot)
                o.dma_val = 16 * (use + 1)

        inc_prefix = {}
        ecount = {e: 0 for e in ENGINES}
        for o in self.ops:
            if not o.is_dma and o.inc_needed:
                ecount[o.engine] += 1
                inc_prefix[o.idx] = ecount[o.engine]

        waits = [[] for _ in self.ops]
        for i, ds in enumerate(deps):
            for d in sorted(ds):
                p = self.ops[d]
                if p.is_dma:
                    waits[i].append(("dma", p.dma_sem, p.dma_val, d))
                else:
                    waits[i].append(("eng", p.engine, inc_prefix[d], d))

        know = {e: {} for e in ENGINES}
        op_know = {}
        emit_waits = [[] for _ in self.ops]
        for o in self.ops:
            st = know[o.engine]
            for (kind, key, val, d) in waits[o.idx]:
                k = (kind, key)
                if val > st.get(k, 0):
                    emit_waits[o.idx].append((kind, key, val))
                    st[k] = val
                    pk = op_know.get(d)
                    if pk:
                        for kk, vv in pk.items():
                            if st.get(kk, 0) < vv:
                                st[kk] = vv
            k = dict(st)
            if not o.is_dma and o.inc_needed:
                k[("eng", o.engine)] = inc_prefix[o.idx]
            elif o.is_dma:
                k[("dma", o.dma_sem)] = o.dma_val
            op_know[o.idx] = k

        sems = {}
        for e in ENGINES:
            sems[("eng", e)] = nc.alloc_semaphore(f"S_{e}")
        for q in set(o.queue for o in self.ops if o.is_dma):
            for slot in range(DMA_RING):
                sems[("dma", (q, slot))] = nc.alloc_semaphore(f"D_{q}_{slot}")

        per_engine = {e: [] for e in ENGINES}
        for o in self.ops:
            per_engine[o.engine].append(o)

        blk = nc.Block()
        block = blk.__enter__()

        def run_engine(eng_handle, ename):
            issued_guard = {}
            for o in per_engine[ename]:
                for (kind, key, val) in emit_waits[o.idx]:
                    eng_handle.wait_ge(sems[(kind, key)], val)
                if o.is_dma:
                    prev_val = o.dma_val - 16
                    if prev_val > 0:
                        g = issued_guard.get(o.dma_sem, 0)
                        if g < prev_val:
                            eng_handle.wait_ge(sems[("dma", o.dma_sem)], prev_val)
                            issued_guard[o.dma_sem] = prev_val
                    ins = o.emit(eng_handle)
                    ins.then_inc(sems[("dma", o.dma_sem)], 16)
                else:
                    ins = o.emit(eng_handle)
                    if o.inc_needed:
                        ins.then_inc(sems[("eng", o.engine)], 1)

        if final_waits:
            finals = {}
            for o in self.ops:
                if o.is_dma:
                    finals[o.dma_sem] = o.dma_val

        @block.tensor
        def _(t):
            run_engine(t, "tensor")

        @block.vector
        def _(v):
            run_engine(v, "vector")

        @block.scalar
        def _(s):
            run_engine(s, "scalar")

        @block.gpsimd
        def _(g):
            run_engine(g, "gpsimd")

        @block.sync
        def _(s):
            run_engine(s, "sync")
            if final_waits:
                for sem_key, val in sorted(finals.items()):
                    s.wait_ge(sems[("dma", sem_key)], val)

        blk.__exit__(None, None, None)



import numpy as np
import concourse.bass as bass
import concourse.mybir as mybir
from concourse.ap import AP
from concourse.bass import IndirectOffsetOnAxis


F32 = mybir.dt.float32
BF16 = mybir.dt.bfloat16
I32 = mybir.dt.int32
AF = mybir.ActivationFunctionType
ALU = mybir.AluOpType
AX = mybir.AxisListType


class Cfg:
    def __init__(self, mini=False):
        if mini:
            self.T, self.D, self.H, self.NQ, self.NKV = 512, 256, 2, 128, 256
        else:
            self.T, self.D, self.H, self.NQ, self.NKV = 4096, 1024, 8, 1024, 2048
        self.NS = 64
        self.Q = 128
        self.DH = 64
        self.WIN = 64
        self.DT = 0.1
        self.NIT = 50
        self.HDH = self.H * self.DH
        self.NCH = self.T // self.Q
        self.NT = self.T // 128
        self.NW = self.T // self.WIN
        self.DC = self.D // 128
        self.KEFFQ = float(min(self.NQ * 9.0 / 8.0, 8192.0))
        self.KEFFKV = float(min(self.NKV * 9.0 / 8.0, 8192.0))
        self.CTX = ((self.NKV + 1 + 127) // 128) * 128
        self.NTC = self.CTX // 128
        self.SSMG = 16
        self.NSG = self.D // self.SSMG


def ts(i, n):
    return slice(i * n, (i + 1) * n)


def mkap(t_ap, offset, dims):
    return AP(tensor=t_ap.tensor, offset=offset,
              ap=[[step, count] for (count, step) in dims])


class KB:
    def __init__(self, cfg, nc, dbg=()):
        self.c = cfg
        self.nc = nc
        self.p = Prog(nc)
        self.dbg_spec = dbg
        self.q_x = "sync"
        self.q_w = "scalar"
        self.q_g = "gpsimd"

    def op(self, *a, **k):
        return self.p.op(*a, **k)

    def dma(self, *a, **k):
        return self.p.dma(*a, **k)

    def declare(self):
        c, p = self.c, self.p
        self.din = {}

        def dri(name, shape, dtype=F32):
            self.din[name] = p.dram(name, shape, dtype, kind="ExternalInput")
            return self.din[name]

        self.x_in = dri("x", [c.T, c.D])
        self.logA_nat_in = dri("logA_nat", [128, c.D * c.NS // 128])
        self.logA_sc_in = dri("logA_sc", [64, c.D])
        self.B_sc_in = dri("B_sc", [64, c.D])
        self.C_sc_in = dri("C_sc", [64, c.D])
        self.D_in = dri("Dvec", [1, c.D])
        self.lng_in = dri("ln_g", [1, c.D])
        self.lnb_in = dri("ln_b", [1, c.D])
        self.wqkv_in = dri("wqkv", [c.D, 3 * c.HDH])
        self.woutl_in = dri("woutl", [c.HDH, c.D])
        self.nullq_in = dri("nullq", [1, c.D])
        self.rt_in = dri("rt", [2, c.D])
        self.wqh_in = dri("wqh", [c.D, c.HDH])
        self.wkvh_in = dri("wkvh", [c.D, c.H * 128])
        self.nullkv_in = dri("nullkv", [2, c.H * c.DH])
        self.wouth_in = dri("wouth", [c.HDH, c.D])
        self.jcolU_in = dri("jcolU", [128, 1])
        self.lramp_in = dri("lramp", [128, c.Q])
        self.tokid_in = dri("tokid", [128, c.T // 128], I32)
        self.tri_in = dri("tri", [128, 128], BF16)
        self.ones_in = dri("onesc", [128, 2], BF16)
        self.onesrow_in = dri("onesrow", [1, 128], BF16)
        self.onesrowf_in = dri("onesrowf", [1, 128], F32)
        self.iden_in = dri("iden", [128, 128], F32)
        self.idenb_in = dri("idenb", [128, 128], BF16)
        self.lngc_in = dri("ln_g_col", [128, c.DC], F32)
        self.lnbc_in = dri("ln_b_col", [128, c.DC], BF16)
        self.rmgc_in = dri("rms_g_col", [128, c.DC], F32)

        self.out_d = p.dram("out", [c.T, c.D], F32, kind="ExternalOutput")
        self.dbg_d = {}
        for (nm, shp, dt) in self.dbg_spec:
            self.dbg_d[nm] = p.dram(nm, shp, dt, kind="ExternalOutput")

        dko = getattr(c, "dbg_out", ())
        def dk(nm):
            return "ExternalOutput" if nm in dko else "Internal"
        self.x1_d = p.dram("x1_d", [c.T, c.D], F32, kind=dk("x1_d"))
        self.xn0_d = p.dram("xn0_d", [c.T, c.D], BF16, kind=dk("xn0_d"))
        self.tdiag_d = p.dram("tdiag_d", [c.D, c.Q, 2 * c.Q], BF16, kind=dk("tdiag_d"))
        self.labf_d = p.dram("labf_d", [1, c.D * c.NS], F32, kind=dk("labf_d"))
        self.iq_d = p.dram("iq_d", [1, c.NQ + 128], I32, kind=dk("iq_d"))
        self.ikv_d = p.dram("ikv_d", [1, c.NKV + 128], I32, kind=dk("ikv_d"))

        BASE = 16640
        self.BASE = BASE
        RF = BASE + 28544
        S1 = RF + 35968
        S2 = S1 + 65536
        TOP = S2 + 65536
        self.RF, self.S1, self.S2, self.TOP = RF, S1, S2, TOP
        sb = p.sbuf
        self.junk = sb("junk", [128, 1024], BF16, at=BASE + 0)
        self.junk2 = sb("junk2", [128, 1024], BF16, at=BASE + 2048)
        A0 = BASE + 4096
        stp = max(128, c.NT * 4)
        for i, nm in enumerate(("sx", "ssq", "m1", "r1", "m2", "r2", "rp",
                                "scr", "scr2", "sx2", "ssq2", "pq", "pkv")):
            setattr(self, nm, sb(nm, [128, c.NT], F32, at=A0 + i * stp))
        self.iden_s = sb("iden_s", [128, 128], F32, at=BASE + 6144)
        self.idenb_s = sb("idenb_s", [128, 128], BF16, at=BASE + 5760)
        self.lramp_s = sb("lramp_s", [128, c.Q], F32, at=BASE + 6656)
        self.tri_s = sb("tri_s", [128, 128], BF16, at=BASE + 7168)
        self.jcolU_s = sb("jcolU_s", [128, 1], F32, at=BASE + 7424)
        self.ones_s = sb("ones_s", [128, 2], BF16, at=BASE + 7456)
        self.onesrow_s = sb("onesrow_s", [1, 128], BF16, at=BASE + 7488)
        self.onesrowf_s = sb("onesrowf_s", [1, 128], F32, at=BASE + 7744)
        self.lngc = sb("lngc", [128, c.DC], F32, at=BASE + 8256)
        self.lnbc = sb("lnbc", [128, c.DC], BF16, at=BASE + 8288)
        self.rmgc = sb("rmgc", [128, c.DC], F32, at=BASE + 8320)
        self.rcol2 = sb("rcol2", [128, 2], F32, at=BASE + 8352)
        self.rrow = sb("rrow", [1, 2], F32, at=BASE + 8384)
        self.bW = sb("bW", [128, 3 * c.HDH // 128], F32, at=BASE + 8416)
        self.zt = sb("zt", [128, 2 * c.Q], BF16, at=BASE + 8480)
        self.d_bc = sb("d_bc", [128, c.D], F32, at=BASE + 10112)
        self.dsb = [sb(f"dsb{i}", [1, 512], F32, at=BASE + 10112 + i * 2048)
                    for i in range(2)]
        self.rbh = sb("rbh", [128, 512], F32, at=BASE + 10112)
        self.dsbh = sb("dsbh", [1, 512], F32, at=BASE + 12160)
        self.woutl_bf = sb("woutl_bf", [128, (c.HDH // 128) * c.D], BF16,
                           at=BASE + 13312)
        self.lab_sc = sb("lab_sc", [64, c.D], F32, at=BASE + 14208)
        self.a128_sc = sb("a128_sc", [64, c.D], F32, at=BASE + 18304)
        self.cbc_sc = sb("cbc_sc", [64, c.D], F32, at=BASE + 22400)
        self.cbcb_sc = sb("cbcb_sc", [64, c.D], BF16, at=BASE + 26496)
        self.ftc = [sb(f"ftc{i}", [128, c.D], F32, at=BASE + 14208 + i * 4096)
                    for i in range(3)]
        self.wouth_bf = sb("wouth_bf", [128, (c.HDH // 128) * c.D], BF16,
                           at=BASE + 14208)
        self.null_bc = sb("null_bc", [128, c.D], F32, at=BASE + 22400)
        self.nullq_bf = sb("nullq_bf", [1, c.D], BF16, at=BASE + 26496)
        self.ft = [sb(f"ft{i}", [128, c.D], F32, at=RF + i * 4096) for i in range(3)]
        self.bt = [sb(f"bt{i}", [128, c.D], BF16, at=RF + 12288 + i * 2048)
                   for i in range(3)]
        self.ft2 = [sb(f"ft2_{i}", [128, c.D], F32, at=RF + 18432 + i * 4096)
                    for i in range(3)]
        self.la_nat = sb("la_nat", [128, c.D * c.NS // 128], F32, at=RF + 30720)
        self.tnat = sb("tnat", [128, c.D * c.NS // 128], F32, at=RF + 32768)
        self.la_sc = sb("la_sc", [64, c.D], F32, at=RF + 24576)
        self.labrow = [sb(f"labrow{i}", [1, c.SSMG * c.NS], BF16, at=TOP + i * 2048)
                       for i in range(2)]
        self.e_sc = sb("e_sc", [64, c.D], F32, at=RF)
        self.ab_sc = sb("ab_sc", [64, c.D], F32, at=RF + 4096)
        self.tmpa = sb("tmpa", [64, c.D], F32, at=RF + 8192)
        self.tmpb = sb("tmpb", [64, c.D], F32, at=RF + 12288)
        self.bsc_s = sb("bsc_s", [64, c.D], F32, at=RF + 16384)
        self.csc_s = sb("csc_s", [64, c.D], F32, at=RF + 20480)
        G = c.SSMG
        self.t_tab = [sb(f"t_tab{i}", [128, G * c.Q], BF16, at=RF + i * 4096)
                      for i in range(2)]
        self.u_tab = [sb(f"u_tab{i}", [128, G * c.NS], BF16, at=RF + 8192 + i * 2048)
                      for i in range(2)]
        self.v_tab = [sb(f"v_tab{i}", [64, G * c.Q], BF16, at=RF + 12288 + i * 4096)
                      for i in range(3)]
        self.hraw = [sb(f"hraw{i}", [64, G * c.NCH], F32, at=RF + 24576)
                     for i in range(1)]
        self.hgrp = [sb(f"hgrp{i}", [64, G * c.NCH], BF16, at=RF + 26624 + i * 1024)
                     for i in range(3)]
        self.kcol_sb = [sb(f"kcol_sb{i}", [128, G], F32, at=RF + 29696 + i * 64)
                        for i in range(2)]
        self.krow_sb = [sb(f"krow_sb{i}", [G, 130], BF16, at=RF + 29824 + i * 288)
                        for i in range(2)]
        self.varg = [sb(f"varg{i}", [64, G * c.Q], BF16, at=RF + 30400)
                     for i in range(1)]
        self.arep = sb("arep", [64, G * c.NCH], BF16, at=RF + 34496)
        W3 = 3 * c.HDH
        self.wqkv_bf = sb("wqkv_bf", [128, c.DC * W3], BF16, at=RF)
        NM = c.NW // 2
        self.abufA = sb("abufA", [128, NM * 256], BF16, at=RF)
        self.abufB = sb("abufB", [128, NM * 256], BF16, at=RF + NM * 512)
        self.zb = sb("zb", [128, 512], BF16, at=RF + NM * 1024)
        self.wqh_bf = sb("wqh_bf", [128, c.DC * c.HDH], BF16, at=RF)
        self.oHF = sb("oHF", [128, (c.HDH // 128) * c.NQ], BF16, at=RF + 8192)
        self.wkvh_bf = sb("wkvh_bf", [128, c.DC * c.H * 128], BF16, at=RF + 16384)
        self.hab = [sb(f"hab{i}", [128, 512], BF16, at=RF + 32768 + i * 1024)
                    for i in range(3)]
        self.x_ssm = sb("x_ssm", [128, c.D * c.NCH], BF16, at=S1)
        self.rtq_bc = sb("rtq_bc", [128, c.D], F32, at=S1 + 32768)
        self.rtkv_bc = sb("rtkv_bc", [128, c.D], F32, at=S1 + 36864)
        self.c2x = [sb(f"c2x{i}", [128, c.D], F32, at=S1 + 40960 + i * 4096)
                    for i in range(2)]
        self.c2u = [sb(f"c2u{i}", [128, c.D], BF16, at=RF + 24576 + i * 2048)
                    for i in range(2)]
        self.c2t0 = [sb(f"c2t0_{i}", [128, c.D], F32, at=BASE + 14208 + i * 4096)
                     for i in range(2)]
        self.c2t1 = [sb(f"c2t1_{i}", [128, c.D], F32,
                        at=(RF + 28672 if i == 0 else BASE + 22400))
                     for i in range(2)]
        self.XR = min(16, c.NT)
        self.TRX = self.XR * 128
        self.xnF = sb("xnF", [128, c.DC * self.TRX], BF16, at=S1)
        WV = c.H * 65
        self.vT = sb("vT", [128, c.NT * WV], BF16, at=S1)
        self.rb = [sb(f"rb{i}", [128, 512], F32, at=S1 + 33280 + i * 2048)
                   for i in range(2)]
        HCk = c.HDH // 128
        PSZ = c.T * 2
        self.kFp = [sb(f"kFp{i}", [128, c.T], BF16,
                       at=(S1 + 49152 + i * PSZ) if i < 2 else (TOP + (i - 2) * PSZ))
                    for i in range(HCk)]
        self.kvF = sb("kvF", [128, c.H * c.CTX], BF16, at=S1)
        WVH = c.H * 65
        self.vTh = sb("vTh", [128, c.NTC * WVH], BF16, at=S1 + 34816)
        self.qhF = sb("qhF", [128, (c.HDH // 128) * c.NQ], BF16, at=S1 + 53248)
        self.y_t = sb("y_t", [128, c.NT * c.D], BF16, at=S2)
        self.wqt = [sb(f"wqt{i}", [128, 3 * c.HDH], F32, at=S2 + i * 12288)
                    for i in range(2)]
        self.wqr = [sb(f"wqr{i}", [128, 3 * c.HDH], BF16, at=S2 + 24576 + i * 6144)
                    for i in range(2)]
        self.qF = sb("qF", [128, (c.HDH // 128) * c.T], BF16, at=S2)
        self.vF = sb("vF", [128, (c.HDH // 128) * c.T], BF16, at=S2 + 32768)
        self.oF = sb("oF", [128, (c.HDH // 128) * c.T], BF16, at=S2 + 32768)
        self.wotmp = sb("wotmp", [128, c.D], F32, at=S2)
        self.xqT = sb("xqT", [128, (c.NQ // 128) * c.D], BF16, at=S2)
        self.xqF = sb("xqF", [128, c.DC * c.NQ], BF16, at=S2 + 32768)
        self.xkT = sb("xkT", [128, (c.NKV // 128) * c.D], BF16, at=S2)
        self.xkF = sb("xkF", [128, c.DC * c.NKV], BF16, at=S2 + 32768)
        self.hwt = [sb(f"hwt{i}", [128, max(c.HDH, c.H * 128)], F32,
                       at=S2 + 49152 + i * 4096) for i in range(2)]
        self.rfx = [sb(f"rfx{i}", [128, c.D], F32, at=S2 + 57344 + i * 4096)
                    for i in range(2)]

        self.pp = [p.psum(f"pp{i}", [128, 512], F32) for i in range(6)]
        self.ppb = p.psum("ppb", [128, 1024], BF16)
        self.ppb2 = p.psum("ppb2", [128, 1024], BF16)

    def phase_A(self):
        c, p = self.c, self.p
        for i in range(c.NT):
            b = self.ft[i % 3]
            self.dma(self.q_x, lambda e, b=b, i=i: e.dma_start(
                out=b[:, :], in_=self.x_in[ts(i, 128), :]),
                reads=[self.x_in.r()], writes=[b.all()], tag=f"ldxA{i}")
            self.op("scalar", lambda e, b=b, i=i: e.activation(
                self.junk[:, 0:min(c.D, 1024)], b[:, 0:min(c.D, 1024)], AF.Square,
                accum_out=self.ssq[:, i:i + 1]),
                reads=[b.all()], writes=[self.junk.r(), self.ssq.r(f0=i, f1=i + 1)],
                tag=f"sq{i}")
            self.op("vector", lambda e, b=b, i=i: e.tensor_scalar(
                self.junk2[:, 0:min(c.D, 1024)], b[:, 0:min(c.D, 1024)], 1.0, None,
                ALU.mult, ALU.add, accum_out=self.sx[:, i:i + 1]),
                reads=[b.all()], writes=[self.junk2.r(), self.sx.r(f0=i, f1=i + 1)],
                tag=f"sxa{i}")
        self._stats(self.sx, self.ssq, self.m1, self.r1)

    def _stats(self, sxb, ssqb, mb, rb):
        c = self.c
        self.op("vector", lambda e: e.tensor_scalar_mul(mb[:, :], sxb[:, :], 1.0 / c.D),
                reads=[sxb.all()], writes=[mb.all()])
        self.op("vector", lambda e: e.tensor_scalar_mul(rb[:, :], ssqb[:, :], 1.0 / c.D),
                reads=[ssqb.all()], writes=[rb.all()])
        self.op("vector", lambda e: e.tensor_mul(self.scr[:, :], mb[:, :], mb[:, :]),
                reads=[mb.all()], writes=[self.scr.all()])
        self.op("vector", lambda e: e.tensor_sub(rb[:, :], rb[:, :], self.scr[:, :]),
                reads=[rb.all(), self.scr.all()], writes=[rb.all()])
        self.op("vector", lambda e: e.tensor_scalar_add(rb[:, :], rb[:, :], 1e-5),
                reads=[rb.all()], writes=[rb.all()])
        self.op("scalar", lambda e: e.activation(rb[:, :], rb[:, :], AF.Ln),
                reads=[rb.all()], writes=[rb.all()])
        self.op("scalar", lambda e: e.activation(rb[:, :], rb[:, :], AF.Exp, scale=-0.5),
                reads=[rb.all()], writes=[rb.all()])

    def phase_A2(self):
        c, p = self.c, self.p
        half = c.D // 2
        for i in range(c.NT):
            b = self.ft[i % 3]
            u = self.bt[i % 3]
            self.dma(self.q_x, lambda e, b=b, i=i: e.dma_start(
                out=b[:, :], in_=self.x_in[ts(i, 128), :]),
                reads=[self.x_in.r()], writes=[b.all()], tag=f"ldxB{i}")
            self.op("vector", lambda e, b=b, u=u, i=i: e.tensor_scalar(
                u[:, :], b[:, :], self.m1[:, i:i + 1], self.r1[:, i:i + 1],
                ALU.subtract, ALU.mult),
                reads=[b.all(), self.m1.r(f0=i, f1=i + 1), self.r1.r(f0=i, f1=i + 1)],
                writes=[u.all()], tag=f"nrm1_{i}")
            for hh, eng in ((0, "vector"), (1, "gpsimd")):
                def emit(e, u=u, i=i, hh=hh):
                    dst = mkap(self.x_ssm.ap(), hh * half * c.NCH + i,
                               [(128, c.D * c.NCH), (half, c.NCH)])
                    return e.tensor_copy(dst, u[:, hh * half:(hh + 1) * half])
                self.op(eng, emit, reads=[u.all()],
                        writes=[self.x_ssm.r(f0=hh * half * c.NCH,
                                             f1=(hh + 1) * half * c.NCH)],
                        tag=f"rearr{i}_{hh}")

    def ssm_prep(self):
        c = self.c
        A = self.op

        def ld(name, dst, src, queue=None):
            self.dma(queue or self.q_w, lambda e, dst=dst, src=src: e.dma_start(
                out=dst[:, :], in_=src[:, :]),
                reads=[src.r()], writes=[dst.all()], tag=f"ld_{name}")

        for (nm, dst, src) in [("iden", self.iden_s, self.iden_in),
                               ("idenb", self.idenb_s, self.idenb_in),
                               ("jcolU", self.jcolU_s, self.jcolU_in),
                               ("lramp", self.lramp_s, self.lramp_in),
                               ("ones", self.ones_s, self.ones_in),
                               ("onesrow", self.onesrow_s, self.onesrow_in),
                               ("onesrowf", self.onesrowf_s, self.onesrowf_in),
                               ("tri", self.tri_s, self.tri_in),
                               ("lasc", self.la_sc, self.logA_sc_in),
                               ("bsc", self.bsc_s, self.B_sc_in),
                               ("csc", self.csc_s, self.C_sc_in),
                               ("lanat", self.la_nat, self.logA_nat_in)]:
            ld(nm, dst, src)

        A("scalar", lambda e: e.activation(self.e_sc[:, :], self.la_sc[:, :], AF.Exp),
          reads=[self.la_sc.all()], writes=[self.e_sc.all()])
        A("scalar", lambda e: e.activation(self.ab_sc[:, :], self.e_sc[:, :], AF.Exp,
                                           scale=-c.DT),
          reads=[self.e_sc.all()], writes=[self.ab_sc.all()])
        A("scalar", lambda e: e.activation(self.lab_sc[:, :], self.ab_sc[:, :], AF.Ln),
          reads=[self.ab_sc.all()], writes=[self.lab_sc.all()])
        A("scalar", lambda e: e.activation(self.a128_sc[:, :], self.lab_sc[:, :], AF.Exp,
                                           scale=float(c.Q)),
          reads=[self.lab_sc.all()], writes=[self.a128_sc.all()])
        A("vector", lambda e: e.tensor_scalar_add(self.tmpa[:, :], self.ab_sc[:, :], -1.0),
          reads=[self.ab_sc.all()], writes=[self.tmpa.all()])
        A("vector", lambda e: e.tensor_scalar_mul(self.tmpb[:, :], self.e_sc[:, :], -1.0),
          reads=[self.e_sc.all()], writes=[self.tmpb.all()])
        A("vector", lambda e: e.reciprocal(self.tmpb[:, :], self.tmpb[:, :]),
          reads=[self.tmpb.all()], writes=[self.tmpb.all()])
        A("vector", lambda e: e.tensor_mul(self.tmpa[:, :], self.tmpa[:, :],
                                           self.tmpb[:, :]),
          reads=[self.tmpa.all(), self.tmpb.all()], writes=[self.tmpa.all()])
        A("vector", lambda e: e.tensor_mul(self.tmpa[:, :], self.tmpa[:, :], self.bsc_s[:, :]),
          reads=[self.tmpa.all(), self.bsc_s.all()], writes=[self.tmpa.all()])
        A("vector", lambda e: e.tensor_mul(self.cbc_sc[:, :], self.tmpa[:, :], self.csc_s[:, :]),
          reads=[self.tmpa.all(), self.csc_s.all()], writes=[self.cbc_sc.all()])
        A("vector", lambda e: e.tensor_copy(self.cbcb_sc[:, :], self.cbc_sc[:, :]),
          reads=[self.cbc_sc.all()], writes=[self.cbcb_sc.all()])
        A("scalar", lambda e: e.activation(self.tnat[:, :], self.la_nat[:, :], AF.Exp),
          reads=[self.la_nat.all()], writes=[self.tnat.all()])
        A("scalar", lambda e: e.activation(self.tnat[:, :], self.tnat[:, :], AF.Exp,
                                           scale=-c.DT),
          reads=[self.tnat.all()], writes=[self.tnat.all()])
        A("scalar", lambda e: e.activation(self.tnat[:, :], self.tnat[:, :], AF.Ln),
          reads=[self.tnat.all()], writes=[self.tnat.all()])
        NF = c.D * c.NS // 128

        def emit_labw(e):
            dst = mkap(self.labf_d.ap(), 0, [(128, c.NS), (c.DC, 128 * c.NS), (c.NS, 1)])
            src = mkap(self.tnat.ap(), 0, [(128, NF), (c.DC, c.NS), (c.NS, 1)])
            return e.dma_start(out=dst, in_=src)
        self.dma(self.q_w, emit_labw, reads=[self.tnat.all()], writes=[self.labf_d.r()],
                 tag="labflat")
        zt = self.zt
        A("gpsimd", lambda e: e.memset(zt[:, :], 0.0), reads=[], writes=[zt.all()])
        for cb in range(c.D // 128):
            def emit_zf(e, cb=cb):
                dst = mkap(self.tdiag_d.ap(), cb * 128 * c.Q * 2 * c.Q,
                           [(128, c.Q * 2 * c.Q), (c.Q, 2 * c.Q), (c.Q, 1)])
                src = AP(tensor=zt.ap().tensor, offset=0,
                         ap=[[2 * c.Q, 128], [0, c.Q], [1, c.Q]])
                return e.dma_start(out=dst, in_=src)
            self.dma(self.q_x, emit_zf, reads=[zt.all()],
                     writes=[self.tdiag_d.r(p0=cb * 128, p1=(cb + 1) * 128)],
                     tag=f"zfill{cb}")

    def ssm_loop(self):
        c = self.c
        A = self.op
        G = c.SSMG
        nhw = c.NCH

        def group_front(g):
            par = g % 2
            lr = self.labrow[par]
            utab = self.u_tab[par]
            vtab = self.v_tab[g % 3]
            va = self.varg[0]
            hr = self.hraw[0]
            hg = self.hgrp[g % 3]
            kcol = self.kcol_sb[par]
            krow = self.krow_sb[par]
            sps = self.pp[2 + par]
            kps = self.pp[4]
            ch0 = g * G
            self.dma(self.q_g, lambda e, lr=lr, ch0=ch0: e.dma_start(
                out=lr[0:1, :], in_=self.labf_d[0:1, ch0 * c.NS:(ch0 + G) * c.NS]),
                reads=[self.labf_d.r()], writes=[lr.all()], tag=f"lrow{g}")
            UW = G * c.NS
            for ci in range(UW // 512):
                A("tensor", lambda e, lr=lr, ci=ci: e.matmul(
                    self.pp[5][:, 0:512], self.onesrow_s[0:1, :], lr[0:1, ts(ci, 512)],
                    start=True, stop=True),
                  reads=[self.onesrow_s.r(), lr.all()], writes=[self.pp[5].r()],
                  tag=f"ubc{g}_{ci}")
                A("scalar", lambda e, utab=utab, ci=ci: e.activation(
                    utab[:, ts(ci, 512)], self.pp[5][:, 0:512], AF.Exp,
                    scale=self.jcolU_s[:, 0:1]),
                  reads=[self.pp[5].r(), self.jcolU_s.all()],
                  writes=[utab.r(f0=ci * 512, f1=(ci + 1) * 512)], tag=f"uexp{g}_{ci}")
            def emit_varg(e, va=va, ch0=ch0):
                i0 = mkap(self.lab_sc.ap(), ch0, [(64, c.D), (G, 1), (c.Q, 0)])
                i1 = mkap(self.lramp_s.ap(), 0, [(64, c.Q), (G, 0), (c.Q, 1)])
                return e.tensor_tensor(va[:, :], i0, i1, ALU.mult)
            A("vector", emit_varg, reads=[self.lab_sc.r(), self.lramp_s.r()],
              writes=[va.all()], tag=f"varg{g}")
            A("scalar", lambda e, vtab=vtab, va=va: e.activation(vtab[:, :], va[:, :],
                                                                 AF.Exp),
              reads=[va.all()], writes=[vtab.all()], tag=f"vexp{g}")
            for k in range(G):
                ch = ch0 + k
                A("tensor", lambda e, utab=utab, k=k, ch=ch, sps=sps: e.matmul(
                    sps[0:64, ts(k, nhw)], utab[:, ts(k, c.NS)],
                    self.x_ssm[:, ts(ch, c.NCH)], start=True, stop=True),
                  reads=[utab.r(f0=k * c.NS, f1=(k + 1) * c.NS),
                         self.x_ssm.r(f0=ch * c.NCH, f1=(ch + 1) * c.NCH)],
                  writes=[sps.r(p0=0, p1=64, f0=k * nhw, f1=(k + 1) * nhw)],
                  tag=f"umm{g}_{k}")
            def emit_arep(e, ch0=ch0):
                src_ = mkap(self.a128_sc.ap(), ch0, [(64, c.D), (G, 1), (c.NCH, 0)])
                return e.tensor_copy(self.arep[:, :], src_)
            A("gpsimd", emit_arep, reads=[self.a128_sc.r()], writes=[self.arep.all()],
              tag=f"arepc{g}")
            def emit_arz(e):
                dst = mkap(self.arep.ap(), 0, [(64, G * c.NCH), (G, c.NCH), (1, 1)])
                return e.memset(dst, 0.0)
            A("gpsimd", emit_arz, reads=[], writes=[self.arep.all()], tag=f"arz{g}")
            A("vector", lambda e, hr=hr, sps=sps: e.tensor_tensor_scan(
                hr[:, :], self.arep[0:64, :], sps[0:64, 0:G * nhw], 0.0,
                ALU.mult, ALU.add),
              reads=[self.arep.all(), sps.r(p0=0, p1=64)], writes=[hr.all()],
              tag=f"scan{g}")
            def emit_h(e, hg=hg, hr=hr, ch0=ch0):
                i1 = mkap(self.cbc_sc.ap(), ch0, [(64, c.D), (G, 1), (c.NCH, 0)])
                return e.tensor_tensor(hg[:, :], hr[:, :], i1, ALU.mult)
            A("vector", emit_h, reads=[hr.all(), self.cbc_sc.r()], writes=[hg.all()],
              tag=f"hmul{g}")
            for k in range(G):
                ch = ch0 + k
                A("tensor", lambda e, vtab=vtab, k=k, ch=ch, kps=kps: e.matmul(
                    kps[:, k:k + 1], vtab[0:64, ts(k, c.Q)],
                    self.cbcb_sc[0:64, ch:ch + 1], start=True, stop=True),
                  reads=[vtab.r(f0=k * c.Q, f1=(k + 1) * c.Q),
                         self.cbcb_sc.r(f0=ch, f1=ch + 1)],
                  writes=[kps.r(f0=k, f1=k + 1)], tag=f"kmm{g}_{k}")
            A("vector", lambda e, kcol=kcol, kps=kps: e.tensor_copy(kcol[:, :],
                                                                    kps[:, 0:G]),
              reads=[kps.r(f0=0, f1=G)], writes=[kcol.all()], tag=f"kev{g}")
            A("tensor", lambda e, kcol=kcol: e.transpose(
                self.pp[5][0:G, 0:128], kcol[:, :], self.iden_s[:, :]),
              reads=[kcol.all(), self.iden_s.all()], writes=[self.pp[5].r(p0=0, p1=G)],
              tag=f"ktr{g}")
            A("vector", lambda e, krow=krow: e.tensor_copy(krow[:, 1:129],
                                                           self.pp[5][0:G, 0:128]),
              reads=[self.pp[5].r(p0=0, p1=G)], writes=[krow.all()], tag=f"ktev{g}")
            A("tensor", lambda e, ch0=ch0: e.matmul(
                self.pp[5][0:G, 128:129], self.cbcb_sc[0:64, ch0:ch0 + G],
                self.ones_s[0:64, 0:1], start=True, stop=True),
              reads=[self.cbcb_sc.r(f0=ch0, f1=ch0 + G), self.ones_s.r()],
              writes=[self.pp[5].r(p0=0, p1=G, f0=128, f1=129)], tag=f"k0mmg{g}")
            A("vector", lambda e, krow=krow: e.tensor_copy(krow[:, 0:1],
                                                           self.pp[5][0:G, 128:129]),
              reads=[self.pp[5].r(p0=0, p1=G, f0=128, f1=129)],
              writes=[krow.all()], tag=f"k0evg{g}")
            def emit_kd(e, krow=krow, ch0=ch0):
                dst = mkap(self.tdiag_d.ap(), ch0 * c.Q * 2 * c.Q,
                           [(G, c.Q * 2 * c.Q), (c.Q, 2 * c.Q + 1), (c.Q, 1)])
                src_ = AP(tensor=krow.ap().tensor, offset=0,
                          ap=[[130, G], [0, c.Q], [1, c.Q]])
                return e.dma_start(out=dst, in_=src_)
            self.dma(self.q_x, emit_kd, reads=[krow.all()],
                     writes=[self.tdiag_d.r(p0=ch0, p1=ch0 + G,
                                            f0=0, f1=c.Q * 2 * c.Q - 1)],
                     tag=f"kdiag{g}")

        def group_back(g):
            par = g % 2
            ttab = self.t_tab[par]
            vtab = self.v_tab[g % 3]
            hg = self.hgrp[g % 3]
            yps = self.pp[par]
            ch0 = g * G
            def emit_tr(e, ttab=ttab, ch0=ch0):
                src_ = mkap(self.tdiag_d.ap(), ch0 * c.Q * 2 * c.Q,
                            [(c.Q, 2 * c.Q), (G, c.Q * 2 * c.Q), (c.Q, 1)])
                dst = mkap(ttab.ap(), 0, [(c.Q, G * c.Q), (G, c.Q), (c.Q, 1)])
                return e.dma_start(out=dst, in_=src_)
            self.dma(self.q_w, emit_tr, reads=[self.tdiag_d.r(p0=ch0, p1=ch0 + G)],
                     writes=[ttab.all()], tag=f"tread{g}")
            for k in range(G):
                ch = ch0 + k
                A("tensor", lambda e, ttab=ttab, k=k, ch=ch, yps=yps: e.matmul(
                    yps[:, ts(k, c.NCH)], ttab[:, ts(k, c.Q)],
                    self.x_ssm[:, ts(ch, c.NCH)], start=True, stop=False),
                  reads=[ttab.r(f0=k * c.Q, f1=(k + 1) * c.Q),
                         self.x_ssm.r(f0=ch * c.NCH, f1=(ch + 1) * c.NCH)],
                  writes=[yps.r(f0=k * c.NCH, f1=(k + 1) * c.NCH)], tag=f"tmm{g}_{k}")
                A("tensor", lambda e, vtab=vtab, k=k, hg=hg, yps=yps: e.matmul(
                    yps[:, k * c.NCH + 1:(k + 1) * c.NCH], vtab[0:64, ts(k, c.Q)],
                    hg[0:64, k * c.NCH: (k + 1) * c.NCH - 1], start=False, stop=True),
                  reads=[vtab.r(f0=k * c.Q, f1=(k + 1) * c.Q),
                         hg.r(f0=k * c.NCH, f1=(k + 1) * c.NCH)],
                  writes=[yps.r(f0=k * c.NCH, f1=(k + 1) * c.NCH)], tag=f"vmm{g}_{k}")
            def emit_yev(e, yps=yps, ch0=ch0):
                dst = mkap(self.y_t.ap(), ch0, [(128, c.NT * c.D), (c.NCH, c.D), (G, 1)])
                src_ = mkap(yps.ap(), 0, [(128, 512), (c.NCH, 1), (G, c.NCH)])
                return e.tensor_copy(dst, src_)
            A("vector", emit_yev, reads=[yps.r()], writes=[self.y_t.r()],
              tag=f"yev{g}")

        LAG = 2
        for g in range(c.NSG + LAG):
            if g < c.NSG:
                group_front(g)
            if g >= LAG:
                group_back(g - LAG)

    def bcast_rows(self):
        c = self.c
        for (nm, dst, src, row) in [("dbc", self.d_bc, self.D_in, 0)]:
            def emit(e, dst=dst, src=src, row=row):
                sap = mkap(src.ap(), row * c.D, [(128, 0), (c.D, 1)])
                return e.dma_start(out=dst[:, :], in_=sap)
            self.dma(self.q_x, emit, reads=[src.r()], writes=[dst.all()], tag=f"bc_{nm}")

    def phase_C1(self):
        c, p = self.c, self.p
        for i in range(c.NT):
            b = self.ft[i % 3]
            t = self.ftc[i % 3]
            x1 = self.ft2[i % 3]
            self.dma(self.q_x, lambda e, b=b, i=i: e.dma_start(
                out=b[:, :], in_=self.x_in[ts(i, 128), :]),
                reads=[self.x_in.r()], writes=[b.all()], tag=f"ldxC{i}")
            self.op("vector", lambda e, b=b, t=t, i=i: e.scalar_tensor_tensor(
                t[:, :], b[:, :], self.m1[:, i:i + 1], self.d_bc[:, :],
                ALU.subtract, ALU.mult),
                reads=[b.all(), self.m1.r(f0=i, f1=i + 1), self.d_bc.all()],
                writes=[t.all()], tag=f"c1a{i}")
            self.op("vector", lambda e, b=b, t=t, x1=x1, i=i: e.scalar_tensor_tensor(
                x1[:, :], t[:, :], self.r1[:, i:i + 1], b[:, :], ALU.mult, ALU.add),
                reads=[t.all(), self.r1.r(f0=i, f1=i + 1), b.all()],
                writes=[x1.all()], tag=f"c1b{i}")
            self.op("gpsimd", lambda e, x1=x1, i=i: e.tensor_tensor(
                x1[:, :], x1[:, :], self.y_t[:, ts(i, c.D)], ALU.add),
                reads=[x1.all(), self.y_t.r(f0=i * c.D, f1=(i + 1) * c.D)],
                writes=[x1.all()], tag=f"c1y{i}")
            self.dma(self.q_x, lambda e, x1=x1, i=i: e.dma_start(
                out=self.x1_d[ts(i, 128), :], in_=x1[:, :]),
                reads=[x1.all()], writes=[self.x1_d.r(p0=i * 128, p1=(i + 1) * 128)],
                tag=f"stx1_{i}")
            self.op("scalar", lambda e, x1=x1, i=i: e.activation(
                self.junk[:, 0:min(c.D, 1024)], x1[:, 0:min(c.D, 1024)], AF.Square,
                accum_out=self.ssq2[:, i:i + 1]),
                reads=[x1.all()], writes=[self.junk.r(), self.ssq2.r(f0=i, f1=i + 1)],
                tag=f"c1sq{i}")
            self.op("vector", lambda e, x1=x1, i=i: e.tensor_scalar(
                self.junk2[:, 0:min(c.D, 1024)], x1[:, 0:min(c.D, 1024)], 1.0, None,
                ALU.mult, ALU.add, accum_out=self.sx2[:, i:i + 1]),
                reads=[x1.all()], writes=[self.junk2.r(), self.sx2.r(f0=i, f1=i + 1)],
                tag=f"c1sx{i}")
        self._stats(self.sx2, self.ssq2, self.m2, self.r2)
        A = self.op
        A("vector", lambda e: e.tensor_scalar_mul(self.scr[:, :], self.ssq2[:, :], 1.0 / c.D),
          reads=[self.ssq2.all()], writes=[self.scr.all()])
        A("vector", lambda e: e.tensor_mul(self.scr2[:, :], self.m2[:, :], self.m2[:, :]),
          reads=[self.m2.all()], writes=[self.scr2.all()])
        A("vector", lambda e: e.tensor_sub(self.scr[:, :], self.scr[:, :], self.scr2[:, :]),
          reads=[self.scr.all(), self.scr2.all()], writes=[self.scr.all()])
        A("vector", lambda e: e.tensor_scalar_add(self.scr2[:, :], self.scr[:, :], 1e-5),
          reads=[self.scr.all()], writes=[self.scr2.all()])
        A("vector", lambda e: e.reciprocal(self.scr2[:, :], self.scr2[:, :]),
          reads=[self.scr2.all()], writes=[self.scr2.all()])
        A("vector", lambda e: e.tensor_mul(self.scr[:, :], self.scr[:, :], self.scr2[:, :]),
          reads=[self.scr.all(), self.scr2.all()], writes=[self.scr.all()])
        A("vector", lambda e: e.tensor_scalar_add(self.scr[:, :], self.scr[:, :], 1e-5),
          reads=[self.scr.all()], writes=[self.scr.all()])
        A("scalar", lambda e: e.activation(self.scr[:, :], self.scr[:, :], AF.Ln),
          reads=[self.scr.all()], writes=[self.scr.all()])
        A("scalar", lambda e: e.activation(self.scr[:, :], self.scr[:, :], AF.Exp, scale=-0.5),
          reads=[self.scr.all()], writes=[self.scr.all()])
        A("vector", lambda e: e.tensor_mul(self.rp[:, :], self.r2[:, :], self.scr[:, :]),
          reads=[self.r2.all(), self.scr.all()], writes=[self.rp.all()])

    def phase_C2(self):
        c, p = self.c, self.p
        W3 = 3 * c.HDH
        NFC = W3 // 128
        HC = c.HDH // 128
        TT512 = c.T // 512
        TRX = self.TRX
        for nm, dst, srcb in [("lngc", self.lngc, self.lngc_in),
                              ("lnbc", self.lnbc, self.lnbc_in),
                              ("rmgc", self.rmgc, self.rmgc_in)]:
            self.dma(self.q_w, lambda e, dst=dst, srcb=srcb: e.dma_start(
                out=dst[:, :], in_=srcb[:, :]),
                reads=[srcb.r()], writes=[dst.all()], tag=f"ld_{nm}")
        for dc in range(c.DC):
            wt = self.wqt[dc % 2]
            wr = self.wqr[dc % 2]
            self.dma(self.q_w, lambda e, wt=wt, dc=dc: e.dma_start(
                out=wt[:, :], in_=self.wqkv_in[ts(dc, 128), :]),
                reads=[self.wqkv_in.r()], writes=[wt.all()], tag=f"ldwqkv{dc}")
            self.op("vector", lambda e, wt=wt, wr=wr: e.tensor_copy(wr[:, :], wt[:, :]),
                    reads=[wt.all()], writes=[wr.all()], tag=f"wc{dc}")
            self.op("vector", lambda e, wt=wt, dc=dc: e.tensor_scalar_mul(
                self.wqkv_bf[:, ts(dc, W3)], wt[:, :], self.lngc[:, dc:dc + 1]),
                reads=[wt.all(), self.lngc.r(f0=dc, f1=dc + 1)],
                writes=[self.wqkv_bf.r(f0=dc * W3, f1=(dc + 1) * W3)], tag=f"wg{dc}")
            for fc in range(NFC):
                self.op("tensor", lambda e, wr=wr, dc=dc, fc=fc: e.matmul(
                    self.pp[5][:, fc:fc + 1], wr[:, ts(fc, 128)],
                    self.lnbc[:, dc:dc + 1], start=(dc == 0 and fc == 0),
                    stop=(dc == c.DC - 1 and fc == NFC - 1)),
                    reads=[wr.all(), self.lnbc.r(f0=dc, f1=dc + 1)],
                    writes=[self.pp[5].r(f0=fc, f1=fc + 1)], tag=f"bw{dc}_{fc}")
        self.op("vector", lambda e: e.tensor_copy(self.bW[:, :], self.pp[5][:, 0:NFC]),
                reads=[self.pp[5].r(f0=0, f1=NFC)], writes=[self.bW.all()], tag="bwev")

        def qkv_block(tt):
            x0 = (tt * 512) % TRX
            for fc in range(NFC):
                kind = fc // HC
                hc = fc % HC
                dstb = (self.qF, None, self.vF)[kind]
                ps = self.pp[fc % 2]
                for dc in range(c.DC):
                    self.op("tensor", lambda e, ps=ps, fc=fc, dc=dc, x0=x0: e.matmul(
                        ps[:, 0:512], self.wqkv_bf[:, dc * W3 + fc * 128: dc * W3 + (fc + 1) * 128],
                        self.xnF[:, dc * TRX + x0: dc * TRX + x0 + 512],
                        start=(dc == 0), stop=(dc == c.DC - 1)),
                        reads=[self.wqkv_bf.r(f0=dc * W3 + fc * 128, f1=dc * W3 + (fc + 1) * 128),
                               self.xnF.r(f0=dc * TRX + x0, f1=dc * TRX + x0 + 512)],
                        writes=[ps.r()], tag=f"qkv{fc}_{tt}_{dc}")
                if kind == 1:
                    kf = self.kFp[hc]
                    self.op("vector", lambda e, ps=ps, kf=kf, tt=tt, fc=fc: e.tensor_scalar_add(
                        kf[:, tt * 512:(tt + 1) * 512],
                        ps[:, 0:512], self.bW[:, fc:fc + 1]),
                        reads=[ps.r(), self.bW.r(f0=fc, f1=fc + 1)],
                        writes=[kf.r(f0=tt * 512, f1=(tt + 1) * 512)],
                        tag=f"qkvev{fc}_{tt}")
                else:
                    self.op("vector", lambda e, ps=ps, dstb=dstb, hc=hc, tt=tt, fc=fc: e.tensor_scalar_add(
                        dstb[:, hc * c.T + tt * 512: hc * c.T + (tt + 1) * 512],
                        ps[:, 0:512], self.bW[:, fc:fc + 1]),
                        reads=[ps.r(), self.bW.r(f0=fc, f1=fc + 1)],
                        writes=[dstb.r(f0=hc * c.T + tt * 512, f1=hc * c.T + (tt + 1) * 512)],
                        tag=f"qkvev{fc}_{tt}")
        self._qkv_emitted = 0

        for (nm, dst, src_, row) in [("rtqbc", self.rtq_bc, self.rt_in, 0),
                                     ("rtkvbc", self.rtkv_bc, self.rt_in, 1)]:
            def emitb(e, dst=dst, src_=src_, row=row):
                sap = mkap(src_.ap(), row * c.D, [(128, 0), (c.D, 1)])
                return e.dma_start(out=dst[:, :], in_=sap)
            self.dma(self.q_x, emitb, reads=[src_.r()], writes=[dst.all()],
                     tag=f"bc_{nm}")
        for i in range(c.NT):
            x1 = self.c2x[i % 2]
            u = self.c2u[i % 2]
            self.dma(self.q_x, lambda e, x1=x1, i=i: e.dma_start(
                out=x1[:, :], in_=self.x1_d[ts(i, 128), :]),
                reads=[self.x1_d.r(p0=i * 128, p1=(i + 1) * 128)], writes=[x1.all()],
                tag=f"ldx1_{i}")
            self.op("vector", lambda e, x1=x1, u=u, i=i: e.tensor_scalar(
                u[:, :], x1[:, :], self.m2[:, i:i + 1], self.rp[:, i:i + 1],
                ALU.subtract, ALU.mult),
                reads=[x1.all(), self.m2.r(f0=i, f1=i + 1), self.rp.r(f0=i, f1=i + 1)],
                writes=[u.all()], tag=f"xn0_{i}")
            self.dma(self.q_x, lambda e, u=u, i=i: e.dma_start(
                out=self.xn0_d[ts(i, 128), :], in_=u[:, :]),
                reads=[u.all()], writes=[self.xn0_d.r(p0=i * 128, p1=(i + 1) * 128)],
                tag=f"stxn_{i}")
            t0 = self.c2t0[i % 2]
            t1 = self.c2t1[i % 2]
            self.op("vector", lambda e, x1=x1, t0=t0: e.tensor_mul(
                t0[:, :], x1[:, :], self.rtq_bc[:, :]),
                reads=[x1.all(), self.rtq_bc.all()], writes=[t0.all()], tag=f"pq_{i}")
            self.op("scalar", lambda e, t0=t0, i=i: e.activation(
                self.junk[:, 0:min(c.D, 1024)], t0[:, 0:min(c.D, 1024)], AF.Copy,
                accum_out=self.pq[:, i:i + 1]),
                reads=[t0.all()], writes=[self.junk.r(), self.pq.r(f0=i, f1=i + 1)],
                tag=f"pqa_{i}")
            self.op("gpsimd", lambda e, x1=x1, t1=t1: e.tensor_tensor(
                t1[:, :], x1[:, :], self.rtkv_bc[:, :], ALU.mult),
                reads=[x1.all(), self.rtkv_bc.all()], writes=[t1.all()], tag=f"pkv_{i}")
            self.op("scalar", lambda e, t1=t1, i=i: e.activation(
                self.junk2[:, 0:min(c.D, 1024)], t1[:, 0:min(c.D, 1024)], AF.Copy,
                accum_out=self.pkv[:, i:i + 1]),
                reads=[t1.all()], writes=[self.junk2.r(), self.pkv.r(f0=i, f1=i + 1)],
                tag=f"pkva_{i}")
            self._xnF_transpose(i)
            if i % 4 == 3 and i >= 11:
                qkv_block(self._qkv_emitted)
                self._qkv_emitted += 1
        while self._qkv_emitted < TT512:
            qkv_block(self._qkv_emitted)
            self._qkv_emitted += 1
        rrow = self.rrow
        for route in range(2):
            self.op("vector", lambda e, route=route: e.reduce_sum(
                rrow[0:1, route:route + 1],
                (self.rtq_bc if route == 0 else self.rtkv_bc)[0:1, :],
                axis=AX.X),
                reads=[(self.rtq_bc if route == 0 else self.rtkv_bc).r(p0=0, p1=1)],
                writes=[rrow.r(f0=route, f1=route + 1)], tag=f"rsum{route}")
        self.op("tensor", lambda e: e.matmul(self.pp[5][:, 0:2], self.onesrowf_s[0:1, :],
                                             rrow[0:1, 0:2], start=True, stop=True),
                reads=[rrow.all()], writes=[self.pp[5].r(f0=0, f1=2)], tag="rbc")
        self.op("vector", lambda e: e.tensor_copy(self.rcol2[:, :], self.pp[5][:, 0:2]),
                reads=[self.pp[5].r(f0=0, f1=2)], writes=[self.rcol2.all()], tag="rbce")


    def light_prep(self):
        pass

    def _xnF_transpose(self, ci):
        c = self.c
        TRX = self.TRX
        def emit_tr(e, ci=ci):
            out = mkap(self.xnF.ap(), (ci % self.XR) * 128,
                       [(128, c.DC * TRX), (c.DC, TRX), (128, 1)])
            inp = mkap(self.xn0_d.ap(), ci * 128 * c.D,
                       [(128, c.D), (c.DC, 128), (128, 1)])
            return e.dma_start_transpose(out, inp)
        sl = (ci % self.XR) * 128
        self.dma(self.q_w, emit_tr, reads=[self.xn0_d.r(p0=ci * 128, p1=(ci + 1) * 128)],
                 writes=[self.xnF.r(f0=dc * TRX + sl, f1=dc * TRX + sl + 128)
                         for dc in range(c.DC)], tag=f"xnF_tr{ci}")

    def light_qkv(self):
        pass

    def light_vT(self):
        c, p = self.c, self.p
        HC = c.HDH // 128
        WV = c.H * 65
        def emit_ones(e):
            dst = mkap(self.vT.ap(), 64, [(128, c.NT * WV), (c.NT, WV), (c.H, 65)])
            return e.memset(dst, 1.0)
        self.op("gpsimd", emit_ones, reads=[], writes=[self.vT.r()], tag="vT_ones")
        for vc in range(HC):
            for m in range(c.NT):
                ps = (self.ppb, self.ppb2)[m % 2]
                self.op("tensor", lambda e, vc=vc, m=m, ps=ps: e.transpose(
                    ps[:, 0:128],
                    self.vF[:, vc * c.T + m * 128: vc * c.T + (m + 1) * 128],
                    self.idenb_s[:, :]),
                    reads=[self.vF.r(f0=vc * c.T + m * 128, f1=vc * c.T + (m + 1) * 128),
                           self.idenb_s.all()],
                    writes=[ps.r(f0=0, f1=128)], tag=f"vTtr{vc}_{m}")
                for hh in range(2):
                    co = m * WV + (2 * vc + hh) * 65
                    self.op("vector", lambda e, co=co, hh=hh, ps=ps: e.tensor_copy(
                        self.vT[:, co:co + 64], ps[:, hh * 64: hh * 64 + 64]),
                        reads=[ps.r(f0=hh * 64, f1=hh * 64 + 64)],
                        writes=[self.vT.r(f0=co, f1=co + 64)], tag=f"vTev{vc}_{m}_{hh}")

    def light_attn(self):
        c, p = self.c, self.p
        HC = c.HDH // 128
        NM = c.NW // 2
        WV = c.H * 65
        TT512 = c.T // 512
        self.op("gpsimd", lambda e: e.memset(self.zb[:, :], 0.0), reads=[],
                writes=[self.zb.all()], tag="zbm")
        self._rg = self.routing_gen()
        def rstep():
            next(self._rg, None)
        abufA = self.abufA
        abufB = self.abufB
        rb = self.rb
        dsb = self.dsb
        for fc in range(HC):
            for m in range(NM):
                s0 = max(0, 128 * m - 64)
                s1 = min(c.T, 128 * m + 192)
                wq = s1 - s0
                co = s0 - (128 * m - 64)
                for par, ab in ((0, abufA), (1, abufB)):
                    ps = self.pp[2 + par]
                    self.op("tensor", lambda e, fc=fc, m=m, par=par, ps=ps, s0=s0, wq=wq: e.matmul(
                        ps[:, 0:wq],
                        self.kFp[fc][64 * par:64 * par + 64, 128 * m: 128 * (m + 1)],
                        self.qF[64 * par:64 * par + 64, fc * c.T + s0: fc * c.T + s0 + wq],
                        start=True, stop=True),
                        reads=[self.kFp[fc].r(p0=64 * par, p1=64 * par + 64),
                               self.qF.r(p0=64 * par, p1=64 * par + 64)],
                        writes=[ps.r(f0=0, f1=wq)], tag=f"sim{fc}_{m}_{par}")
                    self.op("scalar", lambda e, ab=ab, ps=ps, m=m, co=co, wq=wq: e.activation(
                        ab[:, m * 256 + co: m * 256 + co + wq], ps[:, 0:wq], AF.Exp,
                        scale=0.125),
                        reads=[ps.r(f0=0, f1=wq)],
                        writes=[ab.r(f0=m * 256, f1=(m + 1) * 256)], tag=f"exp{fc}_{m}_{par}")
                    v0a, v1a = max(0, 128 * m - 64), min(c.T, 128 * m + 128)
                    v0b, v1b = 128 * m, min(c.T, 128 * m + 256)
                    for (r0, r1, v0, v1) in ((0, 64, v0a, v1a), (64, 128, v0b, v1b)):
                        for (z0, z1) in ((s0, v0), (v1, s1)):
                            if z1 > z0:
                                c0 = m * 256 + co + (z0 - s0)
                                c1 = c0 + (z1 - z0)
                                self.op("vector", lambda e, ab=ab, r0=r0, r1=r1, c0=c0, c1=c1: e.memset(
                                    ab[r0:r1, c0:c1], 0.0),
                                    reads=[], writes=[ab.r(p0=r0, p1=r1, f0=c0, f1=c1)],
                                    tag=f"zc{fc}_{m}_{par}_{r0}_{c0}")
                rstep()
            for par, ab in ((0, abufA), (1, abufB)):
                h = 2 * fc + par
                for qg in range(TT512):
                    ps_o = self.pp[4]
                    g0, g1 = qg * 512, (qg + 1) * 512
                    ms = [m for m in range(NM)
                          if min(c.T, 128 * m + 192) > g0 and max(0, 128 * m - 64) < g1]
                    self.op("tensor", lambda e, h=h, ps_o=ps_o: e.matmul(
                        ps_o[0:65, 0:512],
                        self.vT[:, h * 65:(h + 1) * 65], self.zb[:, 0:512],
                        start=True, stop=False),
                        reads=[self.vT.r(), self.zb.all()],
                        writes=[ps_o.r(p0=0, p1=65)], tag=f"ozz{h}_{qg}")
                    for im, m in enumerate(ms):
                        s0 = max(0, 128 * m - 64)
                        s1 = min(c.T, 128 * m + 192)
                        co = s0 - (128 * m - 64)
                        a0, a1 = max(s0, g0), min(s1, g1)
                        self.op("tensor", lambda e, m=m, h=h, ab=ab, im=im, ms=ms,
                                a0=a0, a1=a1, s0=s0, co=co, g0=g0, ps_o=ps_o: e.matmul(
                            ps_o[0:65, a0 - g0:a1 - g0],
                            self.vT[:, m * (c.H * 65) + h * 65: m * (c.H * 65) + (h + 1) * 65],
                            ab[:, m * 256 + co + (a0 - s0): m * 256 + co + (a1 - s0)],
                            start=False, stop=(im == len(ms) - 1)),
                            reads=[self.vT.r(), ab.r(f0=m * 256, f1=(m + 1) * 256)],
                            writes=[ps_o.r(p0=0, p1=65, f0=a0 - g0, f1=a1 - g0)],
                            tag=f"omm{h}_{qg}_{m}")
                    d_ = dsb[qg % 2]
                    self.op("vector", lambda e, ps_o=ps_o, d_=d_: e.tensor_copy(
                        d_[0:1, 0:512], ps_o[64:65, 0:512]),
                        reads=[ps_o.r(p0=64, p1=65)], writes=[d_.all()],
                        tag=f"dene{h}_{qg}")
                    self.op("tensor", lambda e, d_=d_: e.matmul(
                        self.pp[5][0:64, 0:512], self.onesrowf_s[0:1, 0:64],
                        d_[0:1, 0:512], start=True, stop=True),
                        reads=[d_.all()], writes=[self.pp[5].r(p0=0, p1=64)],
                        tag=f"denb{h}_{qg}")
                    r = rb[qg % 2]
                    self.op("vector", lambda e, r=r: e.reciprocal(r[0:64, 0:512],
                                                                  self.pp[5][0:64, 0:512]),
                            reads=[self.pp[5].r(p0=0, p1=64)], writes=[r.r(p0=0, p1=64)],
                            tag=f"rec{h}_{qg}")
                    hc2, hh = h // 2, h % 2
                    self.op("vector", lambda e, r=r, ps_o=ps_o, hc2=hc2, hh=hh, g0=g0: e.tensor_tensor(
                        self.oF[64 * hh:64 * hh + 64, hc2 * c.T + g0: hc2 * c.T + g0 + 512],
                        ps_o[0:64, 0:512], r[0:64, 0:512], ALU.mult),
                        reads=[ps_o.r(p0=0, p1=64), r.r(p0=0, p1=64)],
                        writes=[self.oF.r(p0=64 * hh, p1=64 * hh + 64,
                                          f0=hc2 * c.T + g0, f1=hc2 * c.T + g0 + 512)],
                        tag=f"oev{h}_{qg}")
                    rstep()

    def light_out(self):
        c, p = self.c, self.p
        HC = c.HDH // 128
        self.dma(self.q_g, lambda e: e.dma_start(out=self.nullq_bf[0:1, :],
                                                 in_=self.nullq_in[0:1, :]),
                 reads=[self.nullq_in.r()], writes=[self.nullq_bf.all()], tag="nqbf")
        def emitnb(e):
            sap = mkap(self.nullq_in.ap(), 0, [(128, 0), (c.D, 1)])
            return e.dma_start(out=self.null_bc[:, :], in_=sap)
        self.dma(self.q_x, emitnb, reads=[self.nullq_in.r()],
                 writes=[self.null_bc.all()], tag="bc_nullbc")
        wtmp = self.wotmp
        for hc in range(HC):
            self.dma(self.q_w, lambda e, hc=hc: e.dma_start(
                out=wtmp[:, :], in_=self.woutl_in[ts(hc, 128), :]),
                reads=[self.woutl_in.r()], writes=[wtmp.all()], tag=f"ldwo{hc}")
            self.op("vector", lambda e, hc=hc: e.tensor_copy(
                self.woutl_bf[:, ts(hc, c.D)], wtmp[:, :]),
                reads=[wtmp.all()], writes=[self.woutl_bf.r(f0=hc * c.D, f1=(hc + 1) * c.D)],
                tag=f"woc{hc}")
        WD = min(512, c.D)
        for i in range(c.NT):
            x1 = self.ft[i % 3]
            o = self.ft2[i % 3]
            self.dma(self.q_x, lambda e, x1=x1, i=i: e.dma_start(
                out=x1[:, :], in_=self.x1_d[ts(i, 128), :]),
                reads=[self.x1_d.r(p0=i * 128, p1=(i + 1) * 128)], writes=[x1.all()],
                tag=f"ldx1f_{i}")
            for dh in range(c.D // WD):
                ps = self.pp[2 + (i % 2)]
                for hc in range(HC):
                    self.op("tensor", lambda e, ps=ps, i=i, dh=dh, hc=hc: e.matmul(
                        ps[:, 0:WD], self.oF[:, hc * c.T + i * 128: hc * c.T + (i + 1) * 128],
                        self.woutl_bf[:, hc * c.D + dh * WD: hc * c.D + (dh + 1) * WD],
                        start=(hc == 0), stop=False),
                        reads=[self.oF.r(f0=hc * c.T + i * 128, f1=hc * c.T + (i + 1) * 128),
                               self.woutl_bf.r(f0=hc * c.D + dh * WD, f1=hc * c.D + (dh + 1) * WD)],
                        writes=[ps.r()], tag=f"lo{i}_{dh}_{hc}")
                self.op("tensor", lambda e, ps=ps, i=i, dh=dh: e.matmul(
                    ps[:, 0:WD], self.onesrow_s[0:1, :],
                    self.nullq_bf[0:1, dh * WD:(dh + 1) * WD], start=False, stop=True),
                    reads=[self.onesrow_s.r(), self.nullq_bf.r(f0=dh * WD, f1=(dh + 1) * WD)],
                    writes=[ps.r()], tag=f"lon{i}_{dh}")
                self.op("vector", lambda e, ps=ps, x1=x1, o=o, dh=dh, i=i: e.tensor_add(
                    o[:, dh * WD:(dh + 1) * WD], x1[:, dh * WD:(dh + 1) * WD], ps[:, 0:WD]),
                    reads=[x1.all(), ps.r()], writes=[o.r(f0=dh * WD, f1=(dh + 1) * WD)],
                    tag=f"cmp{i}_{dh}")
            self.dma(self.q_x, lambda e, o=o, i=i: e.dma_start(
                out=self.out_d[ts(i, 128), :], in_=o[:, :]),
                reads=[o.all()], writes=[self.out_d.r(p0=i * 128, p1=(i + 1) * 128)],
                tag=f"stout{i}")


    def routing(self):
        rg = getattr(self, "_rg", None)
        if rg is not None:
            for _ in rg:
                pass
            self._rg = None

    def routing_gen(self):
        c, p = self.c, self.p
        NT = c.NT
        RTB = [self.S1 + 37376]
        def rsb(name, shape, dtype):
            es = {F32: 4, BF16: 2, I32: 4}[dtype]
            at = (RTB[0] + 31) // 32 * 32
            RTB[0] = at + shape[1] * es
            return p.sbuf(name, shape, dtype, at=at)
        rcol2 = self.rcol2
        self.slotbufs = {}
        for route, (pbuf, keff, NSEL, seld) in enumerate(
                [(self.pq, c.KEFFQ, c.NQ, self.iq_d),
                 (self.pkv, c.KEFFKV, c.NKV, self.ikv_d)]):
            logk = float(np.log(np.float32(keff)))
            s_ = rsb(f"rts_{route}", [128, NT], F32)
            b_ = rsb(f"rtb_{route}", [128, NT], F32)
            sb_ = rsb(f"rtsb_{route}", [128, NT], F32)
            e_ = rsb(f"rte_{route}", [128, NT], F32)
            col = rsb(f"rtcol{route}", [128, 1], F32)
            colm = rsb(f"rtcolm{route}", [128, 1], F32)
            row1 = rsb(f"rtrow1{route}", [1, 128], F32)
            sc1 = rsb(f"rtsc1{route}", [1, 2], F32)
            A = self.op
            A("vector", lambda e, s_=s_, pbuf=pbuf, route=route: e.scalar_tensor_tensor(
                s_[:, :], self.m2[:, :], rcol2[:, route:route + 1], pbuf[:, :],
                ALU.mult, ALU.subtract),
              reads=[self.m2.all(), rcol2.r(f0=route, f1=route + 1), pbuf.all()],
              writes=[s_.all()], tag=f"s0_{route}")
            A("vector", lambda e, s_=s_: e.tensor_mul(s_[:, :], s_[:, :], self.r2[:, :]),
              reads=[s_.all(), self.r2.all()], writes=[s_.all()], tag=f"s1_{route}")
            A("vector", lambda e, s_=s_: e.tensor_scalar_mul(s_[:, :], s_[:, :], -1.0),
              reads=[s_.all()], writes=[s_.all()], tag=f"s2_{route}")
            A("vector", lambda e, s_=s_, b_=b_: e.tensor_scalar_mul(b_[:, :], s_[:, :], -1.0),
              reads=[s_.all()], writes=[b_.all()], tag=f"binit{route}")
            acol = rsb(f"acol_{route}", [128, 1], F32)
            yield
            for it in range(c.NIT):
                t = (route, it)
                A("vector", lambda e, sb_=sb_, s_=s_, b_=b_: e.tensor_add(
                    sb_[:, :], s_[:, :], b_[:, :]),
                  reads=[s_.all(), b_.all()], writes=[sb_.all()], tag=f"cd_sb{t}")
                A("vector", lambda e, sb_=sb_, col=col: e.reduce_max(
                    col[:, 0:1], sb_[:, :], axis=AX.X),
                  reads=[sb_.all()], writes=[col.all()], tag=f"cd_m1{t}")
                A("tensor", lambda e, col=col: e.transpose(
                    self.pp[0][0:1, 0:128], col[:, 0:1], self.iden_s[:, :]),
                  reads=[col.all(), self.iden_s.all()],
                  writes=[self.pp[0].r(p0=0, p1=1)], tag=f"cd_tr{t}")
                A("vector", lambda e, row1=row1: e.tensor_copy(row1[0:1, :],
                                                               self.pp[0][0:1, 0:128]),
                  reads=[self.pp[0].r(p0=0, p1=1)], writes=[row1.all()], tag=f"cd_tre{t}")
                A("vector", lambda e, row1=row1, sc1=sc1: e.reduce_max(
                    sc1[0:1, 0:1], row1[0:1, :], axis=AX.X),
                  reads=[row1.all()], writes=[sc1.r(f0=0, f1=1)], tag=f"cd_m2{t}")
                A("tensor", lambda e, sc1=sc1: e.matmul(
                    self.pp[0][:, 2:3], self.onesrowf_s[0:1, :], sc1[0:1, 0:1],
                    start=True, stop=True),
                  reads=[sc1.r(f0=0, f1=1)], writes=[self.pp[0].r(f0=2, f1=3)],
                  tag=f"cd_mb{t}")
                A("vector", lambda e, colm=colm: e.tensor_scalar_mul(
                    colm[:, 0:1], self.pp[0][:, 2:3], -1.0),
                  reads=[self.pp[0].r(f0=2, f1=3)], writes=[colm.all()], tag=f"cd_mbe{t}")
                A("scalar", lambda e, e_=e_, sb_=sb_, colm=colm, col=col: e.activation(
                    e_[:, :], sb_[:, :], AF.Exp, bias=colm[:, 0:1],
                    accum_out=col[:, 0:1]),
                  reads=[sb_.all(), colm.all()], writes=[e_.all(), col.all()],
                  tag=f"cd_exp{t}")
                A("tensor", lambda e, col=col: e.transpose(
                    self.pp[0][0:1, 0:128], col[:, 0:1], self.iden_s[:, :]),
                  reads=[col.all(), self.iden_s.all()],
                  writes=[self.pp[0].r(p0=0, p1=1)], tag=f"cd_tr2{t}")
                A("vector", lambda e, row1=row1: e.tensor_copy(row1[0:1, :],
                                                               self.pp[0][0:1, 0:128]),
                  reads=[self.pp[0].r(p0=0, p1=1)], writes=[row1.all()], tag=f"cd_tr2e{t}")
                A("vector", lambda e, row1=row1, sc1=sc1: e.reduce_sum(
                    sc1[0:1, 1:2], row1[0:1, :], axis=AX.X),
                  reads=[row1.all()], writes=[sc1.r(f0=1, f1=2)], tag=f"cd_s2{t}")
                A("scalar", lambda e, sc1=sc1: e.activation(sc1[0:1, 1:2], sc1[0:1, 1:2],
                                                            AF.Ln),
                  reads=[sc1.r(f0=1, f1=2)], writes=[sc1.r(f0=1, f1=2)], tag=f"cd_ln{t}")
                A("vector", lambda e, sc1=sc1: e.tensor_add(
                    sc1[0:1, 1:2], sc1[0:1, 1:2], sc1[0:1, 0:1]),
                  reads=[sc1.all()], writes=[sc1.r(f0=1, f1=2)], tag=f"cd_lse{t}")
                A("vector", lambda e, sc1=sc1, logk=logk: e.tensor_scalar(
                    sc1[0:1, 1:2], sc1[0:1, 1:2], -1.0, logk, ALU.mult, ALU.add),
                  reads=[sc1.r(f0=1, f1=2)], writes=[sc1.r(f0=1, f1=2)], tag=f"cd_a{t}")
                A("tensor", lambda e, sc1=sc1: e.matmul(
                    self.pp[0][:, 2:3], self.onesrowf_s[0:1, :], sc1[0:1, 1:2],
                    start=True, stop=True),
                  reads=[sc1.r(f0=1, f1=2)], writes=[self.pp[0].r(f0=2, f1=3)],
                  tag=f"cd_ab{t}")
                A("vector", lambda e, acol=acol: e.tensor_copy(acol[:, 0:1],
                                                               self.pp[0][:, 2:3]),
                  reads=[self.pp[0].r(f0=2, f1=3)], writes=[acol.all()], tag=f"cd_abe{t}")
                A("vector", lambda e, sb_=sb_, s_=s_, acol=acol: e.tensor_scalar_add(
                    sb_[:, :], s_[:, :], acol[:, 0:1]),
                  reads=[s_.all(), acol.all()], writes=[sb_.all()], tag=f"cd_sa{t}")
                A("scalar", lambda e, b_=b_, sb_=sb_: e.activation(b_[:, :], sb_[:, :],
                                                                   AF.Relu),
                  reads=[sb_.all()], writes=[b_.all()], tag=f"cd_rl{t}")
                A("vector", lambda e, b_=b_: e.tensor_scalar_mul(b_[:, :], b_[:, :], -1.0),
                  reads=[b_.all()], writes=[b_.all()], tag=f"cd_ng{t}")
                yield
            mask = rsb(f"mask_{route}", [128, NT], F32)
            maskb = rsb(f"maskb_{route}", [128, NT], BF16)
            A("vector", lambda e, mask=mask, sb_=sb_: e.tensor_scalar(
                mask[:, :], sb_[:, :], 0.0, None, ALU.is_ge),
              reads=[sb_.all()], writes=[mask.all()], tag=f"msk{route}")
            A("vector", lambda e, mask=mask, maskb=maskb: e.tensor_copy(
                maskb[:, :], mask[:, :]),
              reads=[mask.all()], writes=[maskb.all()], tag=f"mskb{route}")
            rkps = self.pp[0]
            rank = rsb(f"rank_{route}", [128, NT], F32)
            A("tensor", lambda e, maskb=maskb, rkps=rkps: e.matmul(
                rkps[:, 0:NT], self.tri_s[:, :], maskb[:, :], start=True, stop=True),
              reads=[self.tri_s.all(), maskb.all()], writes=[rkps.r(f0=0, f1=NT)],
              tag=f"trimm{route}")
            A("vector", lambda e, rank=rank, rkps=rkps: e.tensor_copy(rank[:, :],
                                                                      rkps[:, 0:NT]),
              reads=[rkps.r(f0=0, f1=NT)], writes=[rank.all()], tag=f"rke0{route}")
            cnt = rsb(f"cnt_{route}", [1, NT], F32)
            A("tensor", lambda e, maskb=maskb: e.matmul(
                self.pp[1][0:1, 0:NT], self.ones_s[0:128, 0:1], maskb[:, :],
                start=True, stop=True),
              reads=[self.ones_s.r(), maskb.all()], writes=[self.pp[1].r(p0=0, p1=1)],
              tag=f"cntmm{route}")
            A("vector", lambda e, cnt=cnt: e.tensor_copy(cnt[0:1, :],
                                                         self.pp[1][0:1, 0:NT]),
              reads=[self.pp[1].r(p0=0, p1=1)], writes=[cnt.all()], tag=f"cnte{route}")
            yield
            pref = rsb(f"pref_{route}", [1, NT], F32)
            A("vector", lambda e, pref=pref: e.memset(pref[0:1, 0:1], 0.0),
              reads=[], writes=[pref.r(f0=0, f1=1)], tag=f"pz{route}")
            A("vector", lambda e, pref=pref, cnt=cnt: e.tensor_copy(
                pref[0:1, 1:NT], cnt[0:1, 0:NT - 1]),
              reads=[cnt.all()], writes=[pref.r(f0=1, f1=NT)], tag=f"psh{route}")
            sh = 1
            while sh < NT:
                A("vector", lambda e, pref=pref, sh=sh: e.tensor_add(
                    pref[0:1, sh:NT], pref[0:1, sh:NT], pref[0:1, 0:NT - sh]),
                  reads=[pref.all()], writes=[pref.r(f0=sh, f1=NT)], tag=f"pd{route}_{sh}")
                sh *= 2
            A("tensor", lambda e, pref=pref, rkps=rkps: e.matmul(
                rkps[:, 0:NT], self.onesrowf_s[0:1, :], pref[0:1, :],
                start=True, stop=True),
              reads=[pref.all()], writes=[rkps.r(f0=0, f1=NT)], tag=f"prbc{route}")
            A("vector", lambda e, rank=rank, rkps=rkps: e.tensor_add(
                rank[:, :], rank[:, :], rkps[:, 0:NT]),
              reads=[rank.all(), rkps.r(f0=0, f1=NT)], writes=[rank.all()],
              tag=f"rke{route}")
            slotf = rsb(f"slotf_{route}", [128, NT], F32)
            sloti = rsb(f"sloti_{route}", [128, NT], I32)
            A("vector", lambda e, slotf=slotf, mask=mask, rank=rank: e.scalar_tensor_tensor(
                slotf[:, :], mask[:, :], -1e7, rank[:, :], ALU.mult, ALU.add),
              reads=[mask.all(), rank.all()], writes=[slotf.all()], tag=f"sl1{route}")
            A("vector", lambda e, slotf=slotf: e.tensor_scalar_add(slotf[:, :],
                                                                   slotf[:, :], 1e7),
              reads=[slotf.all()], writes=[slotf.all()], tag=f"sl2{route}")
            A("vector", lambda e, slotf=slotf, NSEL=NSEL: e.tensor_scalar(
                slotf[:, :], slotf[:, :], float(NSEL + 127), None, ALU.min),
              reads=[slotf.all()], writes=[slotf.all()], tag=f"slc{route}")
            A("vector", lambda e, sloti=sloti, slotf=slotf: e.tensor_copy(
                sloti[:, :], slotf[:, :]),
              reads=[slotf.all()], writes=[sloti.all()], tag=f"sl3{route}")
            yield
            tok_s = rsb(f"tok_s{route}", [128, NT], I32)
            self.dma(self.q_g, lambda e, tok_s=tok_s: e.dma_start(
                out=tok_s[:, :], in_=self.tokid_in[:, :]),
                reads=[self.tokid_in.r()], writes=[tok_s.all()], tag=f"tok{route}")
            zi = rsb(f"zi_{route}", [128, (NSEL + 128) // 128], I32)
            A("gpsimd", lambda e, zi=zi: e.memset(zi[:, :], 0), reads=[],
              writes=[zi.all()], tag=f"zi{route}")
            def emit_z(e, zi=zi, seld=seld, NSEL=NSEL):
                dst = mkap(seld.ap(), 0, [(128, (NSEL + 128) // 128), ((NSEL + 128) // 128, 1)])
                return e.dma_start(out=dst, in_=zi[:, :])
            self.dma(self.q_g, emit_z, reads=[zi.all()], writes=[seld.r()],
                     tag=f"zid{route}")
            for f in range(NT):
                def emit_sc(e, sloti=sloti, tok_s=tok_s, f=f, seld=seld, NSEL=NSEL):
                    return e.indirect_dma_start(
                        out=seld.ap().rearrange("a b -> (a b)").unsqueeze(1) if False
                        else seld[0:1, :].transpose() if False else
                        mkap(seld.ap(), 0, [(NSEL + 128, 1), (1, 1)]),
                        out_offset=IndirectOffsetOnAxis(ap=sloti[:, f:f + 1], axis=0),
                        in_=tok_s[:, f:f + 1], in_offset=None)
                self.dma(self.q_g, emit_sc, reads=[sloti.r(f0=f, f1=f + 1),
                                                   tok_s.r(f0=f, f1=f + 1)],
                         writes=[seld.r()], tag=f"scat{route}_{f}")
                if f % 4 == 3:
                    yield

    def heavy(self):
        c, p = self.c, self.p
        A = self.op
        NQT = c.NQ // 128
        NKT = c.NKV // 128
        HC = c.HDH // 128
        wt = self.hwt
        for (win, dstb, W_) in ((self.wqh_in, self.wqh_bf, c.HDH),
                                (self.wkvh_in, self.wkvh_bf, c.H * 128)):
            for dc in range(c.DC):
                w_ = wt[dc % 2]
                self.dma(self.q_w, lambda e, w_=w_, dc=dc, win=win, W_=W_: e.dma_start(
                    out=w_[:, 0:W_], in_=win[ts(dc, 128), :]),
                    reads=[win.r()], writes=[w_.all()], tag=f"ldw_{win.name}_{dc}")
                A("vector", lambda e, w_=w_, dstb=dstb, dc=dc, W_=W_: e.tensor_scalar_mul(
                    dstb[:, dc * W_:(dc + 1) * W_], w_[:, 0:W_], self.rmgc[:, dc:dc + 1]),
                  reads=[w_.all(), self.rmgc.r(f0=dc, f1=dc + 1)],
                  writes=[dstb.r(f0=dc * W_, f1=(dc + 1) * W_)], tag=f"wfold_{win.name}_{dc}")
        idxq = [p.sbuf(f"idxq{k}", [128, 1], I32, at=self.BASE + 9216 + k * 32)
                for k in range(NQT)]
        idxk = [p.sbuf(f"idxk{k}", [128, 1], I32,
                       at=self.BASE + 9216 + NQT * 32 + k * 32)
                for k in range(NKT)]
        rq = p.sbuf("rq_h", [128, NQT + NKT], F32,
                    at=self.BASE + 9216 + (NQT + NKT) * 32)
        SQD = float(np.sqrt(np.float32(c.D)))

        def gather_side(N_, seld, idxs, srcb, dstb, NN, off):
            for k in range(N_):
                ix = idxs[k]
                def emit_ld(e, ix=ix, k=k, seld=seld):
                    src = mkap(seld.ap(), k * 128, [(128, 1), (1, 1)])
                    return e.dma_start(out=ix[:, :], in_=src)
                self.dma(self.q_g, emit_ld, reads=[seld.r()], writes=[ix.all()],
                         tag=f"ldix_{seld.name}_{k}")
                def emit_g(e, ix=ix, k=k, srcb=srcb):
                    return e.indirect_dma_start(
                        out=srcb[:, ts(k, c.D)], out_offset=None,
                        in_=self.xn0_d[:, :],
                        in_offset=IndirectOffsetOnAxis(ap=ix[:, 0:1], axis=0))
                self.dma(self.q_g, emit_g, reads=[ix.all(), self.xn0_d.r()],
                         writes=[srcb.r(f0=k * c.D, f1=(k + 1) * c.D)],
                         tag=f"gath_{srcb.name}_{k}")
                kk = off + k
                A("scalar", lambda e, srcb=srcb, k=k, kk=kk: e.activation(
                    self.junk[:, 0:min(c.D, 1024)],
                    srcb[:, k * c.D:k * c.D + min(c.D, 1024)],
                    AF.Square, accum_out=rq[:, kk:kk + 1]),
                  reads=[srcb.r(f0=k * c.D, f1=(k + 1) * c.D)],
                  writes=[self.junk.r(), rq.r(f0=kk, f1=kk + 1)], tag=f"hsq{kk}")
            o0, o1 = off, off + N_
            A("scalar", lambda e: e.activation(rq[:, o0:o1], rq[:, o0:o1], AF.Ln),
              reads=[rq.r(f0=o0, f1=o1)], writes=[rq.r(f0=o0, f1=o1)], tag=f"hln{off}")
            A("scalar", lambda e: e.activation(rq[:, o0:o1], rq[:, o0:o1], AF.Exp,
                                               scale=0.5),
              reads=[rq.r(f0=o0, f1=o1)], writes=[rq.r(f0=o0, f1=o1)], tag=f"hsqr{off}")
            A("vector", lambda e: e.tensor_scalar_max(rq[:, o0:o1], rq[:, o0:o1], 1e-12),
              reads=[rq.r(f0=o0, f1=o1)], writes=[rq.r(f0=o0, f1=o1)], tag=f"hmx{off}")
            A("vector", lambda e: e.reciprocal(rq[:, o0:o1], rq[:, o0:o1]),
              reads=[rq.r(f0=o0, f1=o1)], writes=[rq.r(f0=o0, f1=o1)], tag=f"hrc{off}")
            A("vector", lambda e: e.tensor_scalar_mul(rq[:, o0:o1], rq[:, o0:o1], SQD),
              reads=[rq.r(f0=o0, f1=o1)], writes=[rq.r(f0=o0, f1=o1)], tag=f"hsd{off}")
            for k in range(N_):
                kk = off + k
                A("vector", lambda e, srcb=srcb, k=k, kk=kk: e.tensor_scalar_mul(
                    srcb[:, ts(k, c.D)], srcb[:, ts(k, c.D)], rq[:, kk:kk + 1]),
                  reads=[srcb.r(f0=k * c.D, f1=(k + 1) * c.D), rq.r(f0=kk, f1=kk + 1)],
                  writes=[srcb.r(f0=k * c.D, f1=(k + 1) * c.D)], tag=f"hnm{kk}")
            for k in range(N_):
                def emit_tr(e, srcb=srcb, dstb=dstb, k=k, NN=NN):
                    out = mkap(dstb.ap(), k * 128,
                               [(128, c.DC * NN), (c.DC, NN), (128, 1)])
                    inp = mkap(srcb.ap(), k * c.D,
                               [(128, srcb.shape[1]), (c.DC, 128), (128, 1)])
                    return e.dma_start_transpose(out, inp)
                self.dma(self.q_w, emit_tr,
                         reads=[srcb.r(f0=k * c.D, f1=(k + 1) * c.D)],
                         writes=[dstb.r(f0=dc * NN + k * 128, f1=dc * NN + (k + 1) * 128)
                                 for dc in range(c.DC)],
                         tag=f"htr_{dstb.name}_{k}")

        gather_side(NQT, self.iq_d, idxq, self.xqT, self.xqF, c.NQ, 0)
        WQT = min(512, c.NQ)
        for hc in range(HC):
            for tq in range(c.NQ // WQT):
                ps = self.pp[0]
                for dc in range(c.DC):
                    A("tensor", lambda e, ps=ps, hc=hc, tq=tq, dc=dc: e.matmul(
                        ps[:, 0:WQT],
                        self.wqh_bf[:, dc * c.HDH + hc * 128: dc * c.HDH + (hc + 1) * 128],
                        self.xqF[:, dc * c.NQ + tq * WQT: dc * c.NQ + (tq + 1) * WQT],
                        start=(dc == 0), stop=(dc == c.DC - 1)),
                      reads=[self.wqh_bf.r(f0=dc * c.HDH + hc * 128,
                                           f1=dc * c.HDH + (hc + 1) * 128),
                             self.xqF.r(f0=dc * c.NQ + tq * WQT,
                                        f1=dc * c.NQ + (tq + 1) * WQT)],
                      writes=[ps.r()], tag=f"qh{hc}_{tq}_{dc}")
                A("vector", lambda e, ps=ps, hc=hc, tq=tq: e.tensor_copy(
                    self.qhF[:, hc * c.NQ + tq * WQT: hc * c.NQ + (tq + 1) * WQT],
                    ps[:, 0:WQT]),
                  reads=[ps.r()],
                  writes=[self.qhF.r(f0=hc * c.NQ + tq * WQT, f1=hc * c.NQ + (tq + 1) * WQT)],
                  tag=f"qhe{hc}_{tq}")
        gather_side(NKT, self.ikv_d, idxk, self.xkT, self.xkF, c.NKV, NQT)
        A("gpsimd", lambda e: e.memset(self.kvF[:, :], 0.0), reads=[],
          writes=[self.kvF.all()], tag="kvz")
        WKT = min(512, c.NKV)
        for h in range(c.H):
            for tq in range(c.NKV // WKT):
                ps = self.pp[1]
                for dc in range(c.DC):
                    A("tensor", lambda e, ps=ps, h=h, tq=tq, dc=dc: e.matmul(
                        ps[:, 0:WKT],
                        self.wkvh_bf[:, dc * c.H * 128 + h * 128: dc * c.H * 128 + (h + 1) * 128],
                        self.xkF[:, dc * c.NKV + tq * WKT: dc * c.NKV + (tq + 1) * WKT],
                        start=(dc == 0), stop=(dc == c.DC - 1)),
                      reads=[self.wkvh_bf.r(f0=dc * c.H * 128 + h * 128,
                                            f1=dc * c.H * 128 + (h + 1) * 128),
                             self.xkF.r(f0=dc * c.NKV + tq * WKT,
                                        f1=dc * c.NKV + (tq + 1) * WKT)],
                      writes=[ps.r()], tag=f"kvh{h}_{tq}_{dc}")
                A("vector", lambda e, ps=ps, h=h, tq=tq: e.tensor_copy(
                    self.kvF[:, h * c.CTX + tq * WKT: h * c.CTX + (tq + 1) * WKT],
                    ps[:, 0:WKT]),
                  reads=[ps.r()],
                  writes=[self.kvF.r(f0=h * c.CTX + tq * WKT, f1=h * c.CTX + (tq + 1) * WKT)],
                  tag=f"kvhe{h}_{tq}")
        for h in range(c.H):
            def emit_nk(e, h=h):
                dst = mkap(self.kvF.ap(),
                           64 * (h % 2) * self.kvF.shape[1] + h * c.CTX + c.CTX - 1,
                           [(64, self.kvF.shape[1]), (1, 1)])
                src = mkap(self.nullkv_in.ap(), h * c.DH, [(64, 1), (1, 1)])
                return e.dma_start(out=dst, in_=src)
            self.dma(self.q_g, emit_nk, reads=[self.nullkv_in.r()],
                     writes=[self.kvF.r(p0=64 * (h % 2), p1=64 * (h % 2) + 64,
                                        f0=h * c.CTX + c.CTX - 1,
                                        f1=h * c.CTX + c.CTX)], tag=f"nk{h}")
        WVH = c.H * 65
        A("gpsimd", lambda e: e.memset(self.vTh[:, :], 0.0), reads=[],
          writes=[self.vTh.all()], tag="vthz")
        def emit_ones(e):
            dst = mkap(self.vTh.ap(), 64, [(128, c.NTC * WVH), (c.NTC, WVH), (c.H, 65)])
            return e.memset(dst, 1.0)
        A("vector", emit_ones, reads=[], writes=[self.vTh.r()], tag="vth_ones")
        for h in range(c.H):
            vh = 64 * (1 - h % 2)
            for tc in range(c.NKV // 128):
                ps = (self.ppb, self.ppb2)[tc % 2]
                self.op("tensor", lambda e, h=h, tc=tc, ps=ps: e.transpose(
                    ps[:, 0:128],
                    self.kvF[:, h * c.CTX + tc * 128: h * c.CTX + (tc + 1) * 128],
                    self.idenb_s[:, :]),
                    reads=[self.kvF.r(f0=h * c.CTX + tc * 128,
                                      f1=h * c.CTX + (tc + 1) * 128),
                           self.idenb_s.all()],
                    writes=[ps.r(f0=0, f1=128)], tag=f"vthtr{h}_{tc}")
                co = tc * WVH + h * 65
                self.op("vector", lambda e, co=co, vh=vh, ps=ps: e.tensor_copy(
                    self.vTh[:, co:co + 64], ps[:, vh: vh + 64]),
                    reads=[ps.r(f0=vh, f1=vh + 64)],
                    writes=[self.vTh.r(f0=co, f1=co + 64)], tag=f"vthev{h}_{tc}")
        def emit_nv(e):
            dst = mkap(self.vTh.ap(), 127 * self.vTh.shape[1] + (c.NTC - 1) * WVH,
                       [(1, self.vTh.shape[1]), (c.H, 65), (64, 1)])
            src = mkap(self.nullkv_in.ap(), c.H * c.DH, [(1, 1), (c.H, 64), (64, 1)])
            return e.dma_start(out=dst, in_=src)
        self.dma(self.q_g, emit_nv, reads=[self.nullkv_in.r()],
                 writes=[self.vTh.r(p0=127, p1=128,
                                    f0=(c.NTC - 1) * WVH,
                                    f1=(c.NTC - 1) * WVH + c.H * 65)],
                 tag="nv")
        ab = self.hab
        rbh = self.rbh
        dsbh = self.dsbh
        NPAD0 = c.NKV % 128
        for h in range(c.H):
            for qg in range(c.NQ // WQT):
                ps_o = self.pp[2 + (qg % 2)]
                for tc in range(c.NTC):
                    pss = self.pp[4 + (tc % 2)]
                    a_ = ab[tc % 3]
                    hp = 64 * (h % 2)
                    A("tensor", lambda e, h=h, qg=qg, tc=tc, pss=pss, hp=hp: e.matmul(
                        pss[:, 0:WQT],
                        self.kvF[hp:hp + 64, h * c.CTX + tc * 128: h * c.CTX + (tc + 1) * 128],
                        self.qhF[hp:hp + 64,
                                 (h // 2) * c.NQ + qg * WQT: (h // 2) * c.NQ + (qg + 1) * WQT],
                        start=True, stop=True),
                      reads=[self.kvF.r(p0=hp, p1=hp + 64), self.qhF.r(p0=hp, p1=hp + 64)],
                      writes=[pss.r()], tag=f"hsim{h}_{qg}_{tc}")
                    A("scalar", lambda e, a_=a_, pss=pss: e.activation(
                        a_[:, 0:WQT], pss[:, 0:WQT], AF.Exp, scale=0.125),
                      reads=[pss.r()], writes=[a_.all()], tag=f"hexp{h}_{qg}_{tc}")
                    if tc == c.NTC - 1:
                        z0, z1 = NPAD0, 127
                        for (r0, r1) in ((z0, min(64, z1)), (max(64, z0), z1)):
                            if r1 > r0:
                                A("vector", lambda e, a_=a_, r0=r0, r1=r1: e.memset(
                                    a_[r0:r1, 0:WQT], 0.0),
                                  reads=[], writes=[a_.r(p0=r0, p1=r1)],
                                  tag=f"hzp{h}_{qg}_{r0}")
                    A("tensor", lambda e, a_=a_, h=h, tc=tc, ps_o=ps_o: e.matmul(
                        ps_o[0:65, 0:WQT],
                        self.vTh[:, tc * WVH + h * 65: tc * WVH + (h + 1) * 65],
                        a_[:, 0:WQT], start=(tc == 0), stop=(tc == c.NTC - 1)),
                      reads=[self.vTh.r(), a_.all()],
                      writes=[ps_o.r(p0=0, p1=65)], tag=f"homm{h}_{qg}_{tc}")
                A("vector", lambda e, ps_o=ps_o: e.tensor_copy(
                    dsbh[0:1, 0:WQT], ps_o[64:65, 0:WQT]),
                  reads=[ps_o.r(p0=64, p1=65)], writes=[dsbh.all()], tag=f"hde{h}_{qg}")
                A("tensor", lambda e: e.matmul(
                    self.pp[1][0:64, 0:WQT], self.onesrowf_s[0:1, 0:64],
                    dsbh[0:1, 0:WQT], start=True, stop=True),
                  reads=[dsbh.all()], writes=[self.pp[1].r(p0=0, p1=64)],
                  tag=f"hdb{h}_{qg}")
                A("vector", lambda e: e.reciprocal(rbh[0:64, 0:WQT], self.pp[1][0:64, 0:WQT]),
                  reads=[self.pp[1].r(p0=0, p1=64)], writes=[rbh.r(p0=0, p1=64)],
                  tag=f"hrc{h}_{qg}")
                hc2, hh = h // 2, h % 2
                A("vector", lambda e, ps_o=ps_o, hc2=hc2, hh=hh, qg=qg: e.tensor_tensor(
                    self.oHF[64 * hh:64 * hh + 64,
                             hc2 * c.NQ + qg * WQT: hc2 * c.NQ + (qg + 1) * WQT],
                    ps_o[0:64, 0:WQT], rbh[0:64, 0:WQT], ALU.mult),
                  reads=[ps_o.r(p0=0, p1=64), rbh.r(p0=0, p1=64)],
                  writes=[self.oHF.r(p0=64 * hh, p1=64 * hh + 64,
                                     f0=hc2 * c.NQ + qg * WQT,
                                     f1=hc2 * c.NQ + (qg + 1) * WQT)],
                  tag=f"hoev{h}_{qg}")
        wtmp2 = self.wotmp
        for hc in range(HC):
            self.dma(self.q_w, lambda e, hc=hc: e.dma_start(
                out=wtmp2[:, :], in_=self.wouth_in[ts(hc, 128), :]),
                reads=[self.wouth_in.r()], writes=[wtmp2.all()], tag=f"ldwoh{hc}")
            A("vector", lambda e, hc=hc: e.tensor_copy(
                self.wouth_bf[:, ts(hc, c.D)], wtmp2[:, :]),
              reads=[wtmp2.all()], writes=[self.wouth_bf.r(f0=hc * c.D, f1=(hc + 1) * c.D)],
              tag=f"wohc{hc}")
        rf = self.rfx
        WD = min(512, c.D)
        for k in range(NQT):
            o = rf[k % 2]
            for dh in range(c.D // WD):
                ps = self.pp[0]
                for hc in range(HC):
                    A("tensor", lambda e, ps=ps, k=k, dh=dh, hc=hc: e.matmul(
                        ps[:, 0:WD], self.oHF[:, hc * c.NQ + k * 128: hc * c.NQ + (k + 1) * 128],
                        self.wouth_bf[:, hc * c.D + dh * WD: hc * c.D + (dh + 1) * WD],
                        start=(hc == 0), stop=(hc == HC - 1)),
                      reads=[self.oHF.r(f0=hc * c.NQ + k * 128, f1=hc * c.NQ + (k + 1) * 128),
                             self.wouth_bf.r(f0=hc * c.D + dh * WD,
                                             f1=hc * c.D + (dh + 1) * WD)],
                      writes=[ps.r()], tag=f"hop{k}_{dh}_{hc}")
                A("vector", lambda e, ps=ps, o=o, dh=dh: e.tensor_sub(
                    o[:, dh * WD:(dh + 1) * WD], ps[:, 0:WD],
                    self.null_bc[:, dh * WD:(dh + 1) * WD]),
                  reads=[ps.r(), self.null_bc.all()],
                  writes=[o.r(f0=dh * WD, f1=(dh + 1) * WD)], tag=f"hfx{k}_{dh}")
            def emit_sc(e, o=o, k=k):
                return e.indirect_dma_start(
                    out=self.out_d[:, :],
                    out_offset=IndirectOffsetOnAxis(ap=idxq[k][:, 0:1], axis=0),
                    in_=o[:, :], in_offset=None,
                    compute_op=ALU.add)
            self.dma(self.q_g, emit_sc, reads=[o.all(), idxq[k].all(), self.out_d.r()],
                     writes=[self.out_d.r()], tag=f"hscat{k}")
    def finalize(self):
        self.p.resolve_and_emit()


import ml_dtypes

BF = ml_dtypes.bfloat16


def prep_consts(c):
    out = {}
    out["jcolU"] = (127 - np.arange(128, dtype=np.float32))[:, None]
    out["lramp"] = np.tile(np.arange(1, c.Q + 1, dtype=np.float32)[None, :], (128, 1))
    tid = (np.arange(c.T // 128)[None, :] * 128 + np.arange(128)[:, None])
    out["tokid"] = tid.astype(np.int32)
    out["tri"] = np.triu(np.ones((128, 128), np.float32), 1).astype(BF)
    out["onesc"] = np.ones((128, 2), BF)
    out["onesrow"] = np.ones((1, 128), BF)
    out["onesrowf"] = np.ones((1, 128), np.float32)
    out["iden"] = np.eye(128, dtype=np.float32)
    out["idenb"] = np.eye(128, dtype=np.float32).astype(BF)
    return out


def prep_core_inputs(c, x_b, w):
    d = {}
    d["x"] = np.ascontiguousarray(x_b)
    logA = w["L0_attn_log_A"]
    d["logA_nat"] = np.ascontiguousarray(
        logA.reshape(c.DC, 128, c.NS).transpose(1, 0, 2).reshape(128, c.DC * c.NS))
    d["logA_sc"] = np.ascontiguousarray(logA.T)
    d["B_sc"] = np.ascontiguousarray(w["L0_attn_B_param"].T)
    d["C_sc"] = np.ascontiguousarray(w["L0_attn_C"].T)
    d["Dvec"] = np.ascontiguousarray(w["L0_attn_D"][None, :])
    d["ln_g"] = np.ascontiguousarray(w["L0_ffn_ln_g"][None, :])
    d["ln_b"] = np.ascontiguousarray(w["L0_ffn_ln_b"][None, :])
    d["wqkv"] = np.ascontiguousarray(w["L0_ffn_w_qkv_l"])
    d["woutl"] = np.ascontiguousarray(w["L0_ffn_w_out_l"])
    d["nullq"] = np.ascontiguousarray(w["L0_ffn_null_q"][None, :])
    d["rt"] = np.ascontiguousarray(
        np.concatenate([w["L0_ffn_rt_q"], w["L0_ffn_rt_kv"]], axis=0))
    d["wqh"] = np.ascontiguousarray(w["L0_ffn_w_q_h"])
    wkv = w["L0_ffn_w_kv_h"]
    blocks = []
    for h in range(c.H):
        kb_ = wkv[:, h, :64]
        vb_ = wkv[:, h, 64:]
        if h % 2 == 0:
            blocks.append(np.concatenate([kb_, vb_], axis=1))
        else:
            blocks.append(np.concatenate([vb_, kb_], axis=1))
    d["wkvh"] = np.ascontiguousarray(np.concatenate(blocks, axis=1))
    d["nullkv"] = np.ascontiguousarray(w["L0_ffn_null_kv"].reshape(2, c.H * c.DH))
    d["wouth"] = np.ascontiguousarray(w["L0_ffn_w_out_h"])
    d["ln_g_col"] = np.ascontiguousarray(w["L0_ffn_ln_g"].reshape(c.DC, 128).T)
    d["ln_b_col"] = np.ascontiguousarray(w["L0_ffn_ln_b"].reshape(c.DC, 128).T.astype(BF))
    d["rms_g_col"] = np.ascontiguousarray(w["L0_ffn_rms_g"].reshape(c.DC, 128).T)
    d.update(prep_consts(c))
    return d


def _build_program():
    cfg = Cfg(mini=False)
    nc = bass.Bass("TRN2", target_bir_lowering=False, debug=False)
    kb = KB(cfg, nc, dbg=[])
    kb.declare()
    kb.bcast_rows()
    kb.phase_A()
    kb.phase_A2()
    kb.ssm_prep()
    kb.ssm_loop()
    kb.phase_C1()
    kb.phase_C2()
    kb.light_prep()
    kb.light_qkv()
    kb.light_vT()
    kb.light_attn()
    kb.routing()
    kb.light_out()
    kb.heavy()
    kb.finalize()
    return cfg, nc


def kernel(**inputs):
    from concourse.bass_utils import run_bass_kernel_spmd
    cfg, nc = _build_program()
    x = np.asarray(inputs["L0_attn_x"])
    w = {k: np.asarray(v) for k, v in inputs.items()}
    base = prep_core_inputs(cfg, x[0], w)
    in_maps = []
    for b in range(8):
        m = dict(base)
        m["x"] = np.ascontiguousarray(x[b])
        in_maps.append(m)
    res = run_bass_kernel_spmd(nc, in_maps, core_ids=list(range(8)))
    out = np.stack([res.results[b]["out"] for b in range(8)], axis=0)
    return out.astype(np.float32)
